# version 27
# baseline (speedup 1.0000x reference)
import os
import numpy as np
import ml_dtypes
import concourse.bass as bass
import concourse.mybir as mybir
from concourse.bass_utils import run_bass_kernel_spmd
from contextlib import ExitStack

F32 = mybir.dt.float32
BF16 = mybir.dt.bfloat16
AF = mybir.ActivationFunctionType
ALU = mybir.AluOpType
AX = mybir.AxisListType

P = 128
S = 2048
D = 1024
H = 16
HD = 64
NC = D // P
NT = S // P
I3 = 3072
NI = I3 // P
NQ = 4
QW = 512
MAGIC = 3 * 2.0**22
EPS = 1e-5
QMAX = 32767.0
ATT_SCALE = 1.0 / 32.0

ENGINES = ("sync", "gpsimd", "vector", "scalar", "tensor")


class Tok:
    __slots__ = ("sem", "val")

    def __init__(self, sem, val):
        self.sem = sem
        self.val = val


DMA_ROT = 16


class Sched:
    def __init__(self):
        self.items = {e: [] for e in ENGINES}
        self.tokn = {e: 0 for e in ENGINES}
        self.qn = {"sync": 0, "gpsimd": 0, "scalar": 0}
        self.bn = 0

    def op(self, eng, make, waits=(), token=False):
        tok = None
        if token:
            self.tokn[eng] += 1
            tok = Tok("t_" + eng, self.tokn[eng])
        self.items[eng].append((make, list(waits), tok))
        return tok

    def dma(self, eng, out_ap, in_ap, waits=()):
        i = self.qn[eng]
        self.qn[eng] += 1
        sem = f"q_{eng}{i % DMA_ROT}"
        occ = i // DMA_ROT
        w = list(waits)
        if occ > 0:
            w.append(Tok(sem, 16 * occ))
        tok = Tok(sem, 16 * (occ + 1))
        mk = (lambda o, i_: (lambda e: e.dma_start(out=o, in_=i_)))(out_ap, in_ap)
        self.items[eng].append((mk, w, tok))
        return tok

    def cc(self, make, waits=()):
        self.items["gpsimd"].append((make, list(waits), "C"))
        self.ccn = getattr(self, "ccn", 0) + 1
        return Tok("C", self.ccn)

    def barrier(self, engines=ENGINES):
        self.bn += len(engines)
        tgt = self.bn
        for e in engines:
            self.items[e].append((lambda en: en.drain(), [], Tok("B", tgt)))
        for e in engines:
            self.items[e].append((None, [Tok("B", tgt)], None))


def build_program():
    nc = bass.Bass("TRN2", target_bir_lowering=False,
                   detect_race_conditions=False)

    xT_d = nc.dram_tensor("xT", [D, S], F32, kind="ExternalInput").ap()
    wqT_d = nc.dram_tensor("wqT", [D, D], F32, kind="ExternalInput").ap()
    wkT_d = nc.dram_tensor("wkT", [D, D], F32, kind="ExternalInput").ap()
    wvT_d = nc.dram_tensor("wvT", [D, D], F32, kind="ExternalInput").ap()
    tt_d = nc.dram_tensor("tt", [H, P, 3968], F32, kind="ExternalInput").ap()
    gam_d = nc.dram_tensor("gam", [NC, P], F32, kind="ExternalInput").ap()
    bet_d = nc.dram_tensor("bet", [NC, P], F32, kind="ExternalInput").ap()
    wg_d = nc.dram_tensor("wg", [I3, D], F32, kind="ExternalInput").ap()
    wu_d = nc.dram_tensor("wu", [I3, D], F32, kind="ExternalInput").ap()
    wd_d = nc.dram_tensor("wd", [D, I3], F32, kind="ExternalInput").ap()
    id_d = nc.dram_tensor("ident", [P, P], BF16, kind="ExternalInput").ap()
    out_d = nc.dram_tensor("out", [D, S], F32, kind="ExternalOutput").ap()

    dbg_d = (nc.dram_tensor("dbg", [P, 16], F32, kind="ExternalOutput").ap()
             if os.environ.get("KDBG") else None)
    dbg2_d = (nc.dram_tensor("dbg2", [P, 16], F32, kind="ExternalOutput").ap()
              if os.environ.get("KDBG") else None)
    xq_d = (nc.dram_tensor("xq_dbg", [D, S], BF16, kind="ExternalOutput").ap()
            if os.environ.get("KDBG") else None)
    bscr_d = nc.dram_tensor("bscr", [8, 4096], F32)
    lscr_d = nc.dram_tensor("lscr", [12, 128], F32)
    rrow_d = nc.dram_tensor("rrow", [4, 512], F32)
    cci = [nc.dram_tensor(f"cci{i}", [1, 2], F32) for i in range(5)]
    cco = [nc.dram_tensor(f"cco{i}", [1, 2], F32) for i in range(5)]
    wdT_d = nc.dram_tensor("wdT_s", [48 * P, 512], BF16)
    wuT_d = nc.dram_tensor("wuT_s", [48 * P, 512], BF16)
    wgT_d = nc.dram_tensor("wgT_s", [48 * P, 512], BF16)
    spill = {}
    x2_d = nc.dram_tensor("x2_s", [D, S], BF16)
    up_d = nc.dram_tensor("up_s", [I3, S], BF16, kind="ExternalOutput" if os.environ.get("KDBG") else "Internal")
    gt_d = nc.dram_tensor("gt_s", [I3, S], BF16, kind="ExternalOutput" if os.environ.get("KDBG") else "Internal")
    or_d = nc.dram_tensor("or_s", [I3, S], BF16, kind="ExternalOutput" if os.environ.get("KDBG") else "Internal")

    sch = Sched()
    gstack = ExitStack()

    def gsb(name, shape, dt=F32):
        return gstack.enter_context(nc.sbuf_tensor(name, shape, dt))

    A = [gsb(f"A{c}", [P, S], BF16) for c in range(NC)]
    csum = gsb("csum", [P, 1], BF16)
    idn = gsb("idn", [P, P], BF16)
    gam = gsb("gam_s", [P, NC], F32)
    bet = gsb("bet_s", [P, NC], F32)
    fscr = gsb("fscr", [P, S], F32)
    rowbank = gsb("rowbank", [P, 4256], F32)
    lmax = gsb("lmax", [P, 6], F32)
    gsc = gsb("gsc", [P, 20], F32)

    sch.op("vector", lambda e: e.memset(gsc[:, 16:17], 2.0))
    t_csum0 = sch.op("vector", lambda e: e.memset(csum[:], 1.0 / 1024.0),
                     token=True)
    t_id = sch.dma("sync", idn[:], id_d)
    gb_toks = []
    for c in range(NC):
        gb_toks.append(sch.dma("sync", gam[:, c:c + 1],
                               gam_d[c:c+1].rearrange("a p -> p a")))
        gb_toks.append(sch.dma("sync", bet[:, c:c + 1],
                               bet_d[c:c+1].rearrange("a p -> p a")))
    t_gb = gb_toks[-1]

    ln_n = [0]

    def ln_stats(stack, get_chunk):
        ln_n[0] += 1
        nn = ln_n[0]
        pm = stack.enter_context(nc.psum_tensor(f"pm{nn}", [1, S], F32))
        pq = stack.enter_context(nc.psum_tensor(f"pq{nn}", [1, S], F32))
        sq0 = stack.enter_context(nc.sbuf_tensor(f"sq{nn}a", [P, S], BF16))
        sq1 = stack.enter_context(nc.sbuf_tensor(f"sq{nn}b", [P, S], BF16))
        sqs = [sq0, sq1]
        sq_cons = [None, None]
        cons = {}
        for c in range(NC):
            ap, toks = get_chunk(c, cons.get(c - 2))
            sq = sqs[c % 2]
            w = list(toks)
            if sq_cons[c % 2] is not None:
                w.append(sq_cons[c % 2])
            t_sq = sch.op("scalar", (lambda a_, q_: (lambda e: e.activation(
                q_[:], a_, AF.Square, bias=0.0, scale=1.0)))(ap, sq),
                waits=w, token=True)
            for n in range(NQ):
                st, sp = (c == 0), (c == NC - 1)
                sch.op("tensor", (lambda a_, n_, st_, sp_: (lambda e: e.matmul(
                    pm[0:1, n_ * QW:(n_ + 1) * QW], csum[:],
                    a_[:, n_ * QW:(n_ + 1) * QW], start=st_, stop=sp_)))(
                        ap, n, st, sp),
                    waits=list(toks) if n == 0 else ())
                cons[c] = sch.op("tensor", (lambda q_, n_, st_, sp_:
                                            (lambda e: e.matmul(
                    pq[0:1, n_ * QW:(n_ + 1) * QW], csum[:],
                    q_[:, n_ * QW:(n_ + 1) * QW], start=st_, stop=sp_)))(
                        sq, n, st, sp),
                    waits=[t_sq] if n == 0 else (), token=(n == NQ - 1))
            sq_cons[c % 2] = cons[c]
        t_mm = cons[NC - 1]
        t_sqm = sch.op("scalar", lambda e: e.activation(
            fscr[0:1, 0:S], pm[0:1, :], AF.Square, bias=0.0, scale=1.0),
            waits=[t_mm], token=True)
        t_v = sch.op("vector", lambda e: e.scalar_tensor_tensor(
            fscr[0:1, 0:S], pq[0:1, :], EPS, fscr[0:1, 0:S],
            ALU.add, ALU.subtract), waits=[t_sqm], token=True)
        t_ln = sch.op("scalar", lambda e: e.activation(
            rowbank[0:1, 0:S], fscr[0:1, 0:S], AF.Ln, bias=0.0, scale=1.0),
            waits=[t_v], token=True)
        t_h = sch.op("vector", lambda e: e.tensor_scalar(
            rowbank[0:1, S:2 * S], rowbank[0:1, 0:S], -0.5, None, ALU.mult),
            waits=[t_ln], token=True)
        t_a = sch.op("scalar", lambda e: e.activation(
            rowbank[0:1, 0:S], rowbank[0:1, S:2 * S], AF.Exp, bias=0.0,
            scale=1.0), waits=[t_h], token=True)
        t_b = sch.op("vector", lambda e: e.scalar_tensor_tensor(
            rowbank[0:1, S:2 * S], pm[0:1, :], -1.0, rowbank[0:1, 0:S],
            ALU.mult, ALU.mult), waits=[t_a], token=True)
        slot = nn % 8
        t_w = sch.dma("sync", bscr_d.ap()[slot:slot+1],
                      rowbank[0:1, 0:4096], waits=[t_b])
        abrep = stack.enter_context(
            nc.sbuf_tensor(f"abrep{nn}", [P, 4096], F32))
        t_bc = sch.dma("sync", abrep[:],
                       bscr_d.ap()[slot:slot+1]
                       .partition_broadcast(P), waits=[t_w])
        return abrep, t_bc, cons

    def fold_partitions_maxes(t_ready, round_idx, cols):
        toks = []
        nvals = len(cols)
        for j, c in enumerate(cols):
            t1 = sch.dma("sync",
                         lscr_d.ap()[2 * round_idx + j:2 * round_idx + j + 1, 0:P],
                         lmax[:, c:c + 1], waits=[t_ready] if t_ready else [])
            t2 = sch.dma("sync", rowbank[0:1, 4104:4104 + P],
                         lscr_d.ap()[2 * round_idx + j:2 * round_idx + j + 1, 0:P],
                         waits=[t1] + toks[-1:])
            tv = sch.op("vector", (lambda j_: (lambda e: e.tensor_reduce(
                rowbank[0:1, 4096 + j_:4097 + j_], rowbank[0:1, 4104:4104 + P],
                AX.X, ALU.max)))(j), waits=[t2], token=True)
            toks.append(tv)
        t3 = sch.dma("gpsimd", cci[round_idx].ap()[0:1, 0:nvals],
                     rowbank[0:1, 4096:4096 + nvals], waits=toks)
        t_cc = sch.cc((lambda ri, nv: (lambda e: e.collective_compute(
            "AllReduce", ALU.max, replica_groups=[list(range(8))],
            ins=[cci[ri].ap()[0:1, 0:nv].opt()],
            outs=[cco[ri].ap()[0:1, 0:nv].opt()])))(round_idx, nvals),
            waits=[t3])
        t4 = sch.dma("gpsimd", gsc[:, 10:10 + nvals],
                     cco[round_idx].ap()[0:1, 0:nvals]
                     .partition_broadcast(P), waits=[t_cc])
        return t4

    def make_scale(t_g, k, raw_col, pre=1.0):
        t_s = sch.op("vector", (lambda k_, rc, p_: (lambda e: e.tensor_scalar(
            gsc[:, 2 * k_:2 * k_ + 1], gsc[:, rc:rc + 1], p_ / QMAX, 1e-12,
            ALU.mult, ALU.max)))(k, raw_col, pre), waits=[t_g], token=True)
        sch.op("scalar", (lambda k_: (lambda e: e.activation(
            gsc[:, 12:13], gsc[:, 2 * k_:2 * k_ + 1], AF.Ln,
            bias=0.0, scale=1.0)))(k), waits=[t_s])
        t_r0 = sch.op("scalar", (lambda k_: (lambda e: e.activation(
            gsc[:, 13:14], gsc[:, 12:13], AF.Exp,
            bias=0.0, scale=-1.0)))(k), token=True)
        sch.op("vector", (lambda k_: (lambda e: e.tensor_tensor(
            gsc[:, 14:15], gsc[:, 2 * k_:2 * k_ + 1],
            gsc[:, 13:14], ALU.mult)))(k), waits=[t_r0])
        sch.op("vector", lambda e: e.drain())
        sch.op("vector", lambda e: e.tensor_tensor(
            gsc[:, 15:16], gsc[:, 16:17], gsc[:, 14:15], ALU.subtract))
        sch.op("vector", lambda e: e.drain())
        t_fin = sch.op("vector", (lambda k_: (lambda e: e.tensor_tensor(
            gsc[:, 2 * k_ + 1:2 * k_ + 2], gsc[:, 13:14], gsc[:, 15:16],
            ALU.mult)))(k), token=(pre == 1.0))
        if pre != 1.0:
            sch.op("vector", (lambda p_: (lambda e: e.memset(
                gsc[:, 17:18], p_)))(pre))
            sch.op("vector", lambda e: e.drain())
            t_fin = sch.op("vector", (lambda k_: (lambda e: e.tensor_tensor(
                gsc[:, 2 * k_ + 1:2 * k_ + 2],
                gsc[:, 2 * k_ + 1:2 * k_ + 2],
                gsc[:, 17:18], ALU.mult)))(k), token=True)
        return t_fin

    def qdq_pair(src_ap, dst_ap, k, waits, token=False):
        sch.op("vector", (lambda s_, k_: (lambda e: e.tensor_scalar(
            fscr[:, 0:s_ap_w(s_)], s_, gsc[:, 2 * k_ + 1:2 * k_ + 2], MAGIC,
            ALU.mult, ALU.add)))(src_ap, k), waits=waits)
        return sch.op("vector", (lambda s_, d_, k_: (lambda e: e.tensor_scalar(
            d_, fscr[:, 0:s_ap_w(s_)], MAGIC, gsc[:, 2 * k_:2 * k_ + 1],
            ALU.subtract, ALU.mult)))(src_ap, dst_ap, k), token=token)

    def s_ap_w(ap):
        return ap.shape[-1]

    t_csum = t_csum0
    with ExitStack() as ph:
        xs0 = ph.enter_context(nc.sbuf_tensor("xs0", [P, S], BF16))
        xs1 = ph.enter_context(nc.sbuf_tensor("xs1", [P, S], BF16))
        bufs = [xs0, xs1]

        def get_chunk(c, reuse):
            b = bufs[c % 2]
            w = [reuse] if reuse is not None else []
            t = sch.dma("gpsimd", b[:], xT_d[c * P:(c + 1) * P, :], waits=w)
            toks = [t] + ([t_csum] if c == 0 else [])
            return b[:], toks

        abrep, t_ab, cons = ln_stats(ph, get_chunk)

        prev = [cons[NC - 2], cons[NC - 1]]
        for c in range(NC):
            b = bufs[c % 2]
            w = [prev[c % 2]] if prev[c % 2] is not None else []
            t_l = sch.dma("gpsimd", b[:], xT_d[c * P:(c + 1) * P, :],
                          waits=w)
            sch.op("vector", (lambda b_, ab_: (lambda e: e.tensor_tensor(
                fscr[:], b_[:], ab_[:, 0:S], ALU.mult)))(b, abrep),
                waits=[t_l, t_ab])
            prev[c % 2] = sch.op("vector", (lambda c_, ab_: (lambda e: e.tensor_tensor(
                A[c_][:], fscr[:], ab_[:, S:2 * S], ALU.add)))(c, abrep),
                token=True)
        sch.barrier()

    big = ExitStack()
    qsb = [big.enter_context(nc.sbuf_tensor(f"q{c}", [P, S], BF16))
           for c in range(NC)]
    ksb = [big.enter_context(nc.sbuf_tensor(f"k{c}", [P, S], BF16))
           for c in range(NC)]
    vsb = [big.enter_context(nc.sbuf_tensor(f"v{t}", [P, H * 65], BF16))
           for t in range(NT)]
    with ExitStack() as ph:
        wfull = [ph.enter_context(nc.sbuf_tensor(f"wf{c}", [P, D], BF16))
                 for c in range(NC)]
        pj = [ph.enter_context(nc.psum_tensor(f"pj{i}", [P, QW], F32))
              for i in range(4)]
        t_vones = None
        for t in range(NT):
            t_vones = sch.op("vector", (lambda t_: (lambda e: e.memset(
                vsb[t_].ap().rearrange("p (h x) -> p h x", x=65)[:, :, 64:65],
                1.0)))(t), token=(t == NT - 1))
        bank_free = [None] * 4

        wslab = [ph.enter_context(nc.sbuf_tensor(f"wsl{i}", [P, D], F32))
                 for i in range(2)]
        wsl2 = ph.enter_context(nc.sbuf_tensor("wsl2", [P, D], F32))
        wqb = [ph.enter_context(nc.sbuf_tensor(f"wqb{i}", [P, D], BF16))
               for i in range(2)]
        Ssc = ph.enter_context(nc.sbuf_tensor("Ssc", [P, 32], F32))
        Sd = ph.enter_context(nc.sbuf_tensor("Sd", [P, 32], F32))
        Rsc = ph.enter_context(nc.sbuf_tensor("Rsc", [P, 32], F32))
        Rs2 = ph.enter_context(nc.sbuf_tensor("Rs2", [P, 32], F32))
        RsB = ph.enter_context(nc.sbuf_tensor("RsB", [P, 32], F32))
        tstg = [ph.enter_context(nc.sbuf_tensor(f"tstg{i}", [P, 512], BF16))
                for i in range(2)]
        psT = [ph.enter_context(nc.psum_tensor(f"psT{i}", [P, QW], BF16))
               for i in range(2)]
        deq_spills = []

        def b3(t):
            return t.ap()[:, 0:D].rearrange("p (a b) -> p a b", b=32)

        def deq_gen():
            t_q_hist = [None, None]
            t_psT_free = [None, None]
            t_stg_free = [None, None]
            t_tr_bypar = [None, None]
            units = []
            for src_d, dst_hbm, key, nsl, ncof in (
                    (wu_d, wuT_d, "wuT", NI, 1),
                    (wg_d, wgT_d, "wgT", NI, 1),
                    (wd_d, wdT_d, "wdT", NC, 3)):
                for widx in range(nsl):
                    for coff in range(ncof):
                        units.append((src_d, dst_hbm, key, widx, coff,
                                      ncof))
            pend = []
            for ui, unit in enumerate(units + [None] * 3):
                if len(pend) >= 2 or (unit is None and pend):
                    pui, t_q, dst_hbm, key, bib = pend.pop(0)
                    for jg in range(2):
                        pT = psT[jg]
                        stg = tstg[jg]
                        for jj in range(4):
                            j = 4 * jg + jj
                            w = [t_q] if jj == 0 else []
                            if jj == 0 and t_psT_free[jg] is not None:
                                w.append(t_psT_free[jg])
                            sch.op("tensor", (lambda j_, jj_, pT_, qb_:
                                              (lambda e: e.transpose(
                                pT_[:, jj_ * P:(jj_ + 1) * P],
                                qb_[:, j_ * P:(j_ + 1) * P], idn[:])))(
                                    j, jj, pT, wqb[pui % 2]),
                                waits=w + ([t_id] if jg == 0 and pui == 0
                                           else []))
                        t_tr = sch.op("tensor", lambda e: e.drain(),
                                      token=True)
                        t_tr_bypar[pui % 2] = t_tr
                        wev = [t_tr] + ([t_stg_free[jg]]
                                        if t_stg_free[jg] is not None
                                        else [])
                        t_ev = sch.op("scalar", (lambda pT_, stg_:
                                      (lambda e: e.copy(
                            stg_[:], pT_[:])))(pT, stg),
                            waits=wev, token=True)
                        t_psT_free[jg] = t_ev
                        bi = bib + jg
                        t_stg_free[jg] = sch.dma(
                            "sync", dst_hbm[bi * P:(bi + 1) * P, :],
                            stg[:], waits=[t_ev])
                        spill.setdefault(key, []).append(t_stg_free[jg])
                        deq_spills.append(t_stg_free[jg])
                if unit is not None:
                    src_d, dst_hbm, key, widx, coff, ncof = unit
                    sl = wslab[ui % 2]
                    w = ([t_q_hist[ui % 2]]
                         if t_q_hist[ui % 2] is not None else [])
                    t_l = sch.dma("sync", sl[:],
                                  src_d[widx * P:(widx + 1) * P,
                                        coff * D:(coff + 1) * D], waits=w)
                    sch.op("vector", (lambda s_: (lambda e: e.tensor_reduce(
                        Ssc[:], b3(s_), AX.X, ALU.max,
                        apply_absolute_value=True)))(sl), waits=[t_l])
                    sch.op("vector", lambda e: e.drain())
                    t_ss = sch.op("vector", lambda e: e.tensor_scalar(
                        Sd[:], Ssc[:], 1.0 / 7.0, 1e-12,
                        ALU.mult, ALU.max), token=True)
                    sch.op("scalar", lambda e: e.activation(
                        Rs2[:], Sd[:], AF.Ln, bias=0.0, scale=1.0),
                        waits=[t_ss])
                    t_r0 = sch.op("scalar", lambda e: e.activation(
                        Rsc[:], Rs2[:], AF.Exp, bias=0.0, scale=-1.0),
                        token=True)
                    sch.op("vector", lambda e: e.tensor_tensor(
                        Rs2[:], Sd[:], Rsc[:], ALU.mult), waits=[t_r0])
                    sch.op("vector", lambda e: e.drain())
                    sch.op("vector", lambda e: e.tensor_scalar(
                        RsB[:], Rs2[:], -1.0, 2.0, ALU.mult, ALU.add))
                    sch.op("vector", lambda e: e.drain())
                    sch.op("vector", lambda e: e.tensor_tensor(
                        Rsc[:], Rsc[:], RsB[:], ALU.mult))
                    sch.op("vector", lambda e: e.drain())
                    sch.op("vector", (lambda s_: (lambda e: e.tensor_tensor(
                        b3(wsl2), b3(s_),
                        Rsc[:, :, None].broadcast_to([P, 32, 32]),
                        ALU.mult)))(sl))
                    sch.op("vector", (lambda s_: (lambda e: e.tensor_scalar(
                        s_[:], wsl2[:], MAGIC, MAGIC,
                        ALU.add, ALU.subtract)))(sl))
                    t_q = sch.op("vector", (lambda s_, qb_: (lambda e:
                                 e.tensor_tensor(
                        b3(qb_), b3(s_),
                        Sd[:, :, None].broadcast_to([P, 32, 32]),
                        ALU.mult)))(sl, wqb[ui % 2]),
                        waits=[t_tr_bypar[ui % 2]]
                        if t_tr_bypar[ui % 2] is not None else (),
                        token=True)
                    t_q_hist[ui % 2] = t_q
                    pend.append((ui, t_q, dst_hbm, key,
                                 (widx * ncof + coff) * 2))
                yield

        deq = deq_gen()
        deq_n = [0, 0]

        def pace():
            deq_n[0] += 1
            tgt = min(72, deq_n[0] * 72 // 96 + 1)
            while deq_n[1] < tgt:
                if next(deq, "END") == "END":
                    break
                deq_n[1] += 1

        for mi, (wd, dst) in enumerate(((wqT_d, qsb), (wkT_d, ksb))):
            wl = ([t for t in bank_free if t is not None] if mi else [])
            wtoks = [sch.dma("gpsimd", wfull[c][:], wd[c * P:(c + 1) * P, :],
                             waits=wl if c == 0 else ())
                     for c in range(NC)]
            ti = 0
            for oc in range(NC):
                for n in range(NQ):
                    bk = ti % 4
                    ti += 1
                    pt = pj[bk]
                    for kc in range(NC):
                        w = []
                        if kc == 0:
                            w = list(wtoks)
                            if bank_free[bk] is not None:
                                w.append(bank_free[bk])
                        sch.op("tensor", (lambda kc_, oc_, n_, pt_, st_, sp_:
                                          (lambda e: e.matmul(
                            pt_[:], wfull[kc_][:, oc_ * P:(oc_ + 1) * P],
                            A[kc_][:, n_ * QW:(n_ + 1) * QW],
                            start=st_, stop=sp_)))(
                                kc, oc, n, pt, kc == 0, kc == NC - 1),
                            waits=w)
                    t_mm = sch.op("tensor", lambda e: e.drain(),
                                  token=True)
                    if mi == 0:
                        bank_free[bk] = sch.op(
                            "vector", (lambda d_, oc_, n_, pt_: (lambda e:
                                       e.tensor_scalar(
                                d_[oc_][:, n_ * QW:(n_ + 1) * QW], pt_[:],
                                ATT_SCALE, None, ALU.mult)))(dst, oc, n, pt),
                            waits=[t_mm], token=True)
                    else:
                        bank_free[bk] = sch.op(
                            "scalar", (lambda d_, oc_, n_, pt_: (lambda e:
                                       e.copy(
                                d_[oc_][:, n_ * QW:(n_ + 1) * QW], pt_[:])))(
                                    dst, oc, n, pt),
                            waits=[t_mm], token=True)
                    pace()
        wtoks = [sch.dma("gpsimd", wfull[c][:], wvT_d[c * P:(c + 1) * P, :],
                         waits=[bank_free[i] for i in range(4)] if c == 0
                         else ())
                 for c in range(NC)]
        ti = 0
        for t in range(NT):
            for half in range(2):
                bk = ti % 4
                ti += 1
                pt = pj[bk]
                for kc in range(NC):
                    w = []
                    if kc == 0:
                        w = list(wtoks)
                        if bank_free[bk] is not None:
                            w.append(bank_free[bk])
                    sch.op("tensor", (lambda kc_, t_, h_, pt_, st_, sp_:
                                      (lambda e: e.matmul(
                        pt_[:], A[kc_][:, t_ * P:(t_ + 1) * P],
                        wfull[kc_][:, h_ * QW:(h_ + 1) * QW],
                        start=st_, stop=sp_)))(
                            kc, t, half, pt, kc == 0, kc == NC - 1),
                        waits=w)
                t_mm = sch.op("tensor", lambda e: e.drain(), token=True)
                bank_free[bk] = sch.op(
                    "scalar", (lambda t_, h_, pt_: (lambda e: e.copy(
                        vsb[t_].ap().rearrange("p (hh x) -> p hh x", x=65)[
                            :, h_ * 8:(h_ + 1) * 8, 0:64],
                        pt_.ap().rearrange("p (hh x) -> p hh x", x=64))))(
                            t, half, pt),
                    waits=[t_mm, t_vones], token=True)
                pace()
        while next(deq, "END") != "END":
            pass
        mxs = {}
        for tk in deq_spills:
            mxs[tk.sem] = max(mxs.get(tk.sem, 0), tk.val)
        sch.op("sync", None, waits=[Tok(s_, v_) for s_, v_ in mxs.items()])
        sch.barrier()

    with ExitStack() as ph:
        ttsb = [ph.enter_context(nc.sbuf_tensor(f"tts{i}", [P, 3968], BF16))
                for i in range(4)]
        esb = [ph.enter_context(nc.sbuf_tensor(f"esb{i}", [P, 2 * QW], BF16))
               for i in range(3)]
        rrep = [ph.enter_context(nc.sbuf_tensor(f"rrep{i}", [HD, QW], F32))
                for i in range(2)]
        oscr = [ph.enter_context(nc.sbuf_tensor(f"oscr{i}", [HD, QW], F32))
                for i in range(2)]
        obias = [ph.enter_context(nc.sbuf_tensor(f"obias{i}", [HD, QW], F32))
                 for i in range(2)]
        ostg = ph.enter_context(nc.sbuf_tensor("ostg", [HD, QW], BF16))
        psS = [ph.enter_context(nc.psum_tensor(f"psS{i}", [P, 2 * QW],
                                               F32)) for i in range(2)]
        psUE = [ph.enter_context(nc.psum_tensor(f"psUE{h}", [P, QW], F32))
                for h in range(2)]
        psUB = [ph.enter_context(nc.psum_tensor(f"psUB{h}", [P, QW], F32))
                for h in range(2)]

        qzb = [[ph.enter_context(nc.sbuf_tensor(f"qz{i}{j}", [P, QW], BF16))
                for j in range(2)] for i in range(2)]
        t_qzz = sch.op("vector", lambda e: (
            e.memset(qzb[0][0][HD:P, :], 0.0)), token=False)
        sch.op("vector", lambda e: e.memset(qzb[0][1][0:HD, :], 0.0))
        sch.op("vector", lambda e: e.memset(qzb[1][0][HD:P, :], 0.0))
        t_qzz = sch.op("vector", lambda e: e.memset(qzb[1][1][0:HD, :], 0.0),
                       token=True)
        qz_free = [None, None]
        strip_tok = [None, None, None, None]
        strip_tok[0] = sch.dma("gpsimd", ttsb[0][:], tt_d[0])
        strip_tok[1] = sch.dma("gpsimd", ttsb[1][:], tt_d[1])
        pair_done = [None, None]
        t_exp = [None, None]
        t_av = [None] * 4
        ue_free = [None, None]
        ub_free = [None, None]
        oscr_free = [None, None]
        obias_free = [None, None]
        rrep_free = [None, None]
        t_ostg_dma = None
        rrow_free = [None] * 4
        fscr_free = [None] * 4

        for pr in range(8):
            par = pr % 2
            cur = [ttsb[par * 2], ttsb[par * 2 + 1]]
            cur_toks = [strip_tok[par * 2], strip_tok[par * 2 + 1]]
            if pr < 7:
                npar = (pr + 1) % 2
                w = [pair_done[npar]] if pair_done[npar] is not None else []
                strip_tok[npar * 2] = sch.dma(
                    "gpsimd", ttsb[npar * 2][:], tt_d[2 * pr + 2], waits=w)
                strip_tok[npar * 2 + 1] = sch.dma(
                    "gpsimd", ttsb[npar * 2 + 1][:], tt_d[2 * pr + 3])
            def emit_qz(pq, qt):
                qp = qt % 2
                w = [qz_free[qp]] if qz_free[qp] is not None else [t_qzz]
                return [
                    sch.op("scalar", (lambda pr_, qt_, qp_: (lambda e:
                           e.copy(
                        qzb[qp_][0][0:HD, :],
                        qsb[pr_][0:HD, qt_ * QW:(qt_ + 1) * QW])))(
                            pq, qt, qp),
                        waits=w, token=True),
                    sch.op("scalar", (lambda pr_, qt_, qp_: (lambda e:
                           e.copy(
                        qzb[qp_][1][HD:P, :],
                        qsb[pr_][HD:P, qt_ * QW:(qt_ + 1) * QW])))(
                            pq, qt, qp),
                        token=True)]

            if pr == 0:
                t_qz_next = emit_qz(0, 0)
            for qt in range(NQ):
                def emit_av(kc, t_e):
                    eb = esb[kc % 3]
                    for hh in range(2):
                        w = [t_e] if hh == 0 else []
                        if kc == 0:
                            if ue_free[hh] is not None:
                                w.append(ue_free[hh])
                            if ub_free[hh] is not None:
                                w.append(ub_free[hh])
                            w.extend(cur_toks)
                        col = (2 * pr + hh) * 65
                        ms = qt * QW - kc * P + 1920
                        sch.op("tensor", (lambda kc_, hh_, st_, sp_, b_,
                                          col_: (lambda e: e.matmul(
                            psUE[hh_][0:65, :],
                            vsb[kc_][:, col_:col_ + 65],
                            b_[:, hh_ * QW:(hh_ + 1) * QW],
                            start=st_, stop=sp_)))(
                                kc, hh, kc == 0, kc == 15, eb, col),
                            waits=w)
                        sch.op("tensor", (lambda kc_, hh_, st_, sp_, ms_, c_,
                                          col_: (lambda e: e.matmul(
                            psUB[hh_][0:65, :],
                            vsb[kc_][:, col_:col_ + 65],
                            c_[:, ms_:ms_ + QW],
                            start=st_, stop=sp_)))(
                                kc, hh, kc == 0, kc == 15, ms, cur[hh], col),
                            waits=[])
                    return sch.op("tensor", lambda e: e.drain(), token=True)

                qp = qt % 2
                t_qz = t_qz_next
                pend = []
                for kc in range(16):
                    p2 = kc % 2
                    ps = psS[p2]
                    for hh in range(2):
                        w = []
                        if hh == 0 and t_exp[p2] is not None:
                            w.append(t_exp[p2])
                        if kc == 0 and hh == 0:
                            w.extend(t_qz)
                        sch.op("tensor", (lambda pr_, kc_, hh_, qz_,
                                          ps_: (lambda e: e.matmul(
                            ps_[:, hh_ * QW:(hh_ + 1) * QW],
                            ksb[pr_][:, kc_ * P:(kc_ + 1) * P],
                            qz_[:, 0:QW],
                            start=True, stop=True)))(pr, kc, hh,
                                                     qzb[qp][hh], ps),
                            waits=w)
                    t_qk = sch.op("tensor", lambda e: e.drain(),
                                  token=True)
                    qz_free[qp] = t_qk
                    if pend:
                        kcp, tep = pend.pop(0)
                        t_av[kcp % 3] = emit_av(kcp, tep)
                    w = [t_qk]
                    if t_av[kc % 3] is not None:
                        w.append(t_av[kc % 3])
                    t_exp[p2] = sch.op(
                        "scalar", (lambda b_, ps_: (lambda e: e.activation(
                            b_[:], ps_[:], AF.Exp, bias=0.0,
                            scale=1.0)))(esb[kc % 3], ps),
                        waits=w, token=True)
                    pend.append((kc, t_exp[p2]))
                if qt < NQ - 1:
                    t_qz_next = emit_qz(pr, qt + 1)
                elif pr < 7:
                    t_qz_next = emit_qz(pr + 1, 0)
                t_pv = emit_av(*pend.pop(0))

                for hh in range(2):
                    slot = (qt * 2 + hh) % 4
                    wln = [t_pv]
                    if rrow_free[slot] is not None:
                        wln.append(rrow_free[slot])
                    sch.op("scalar", (lambda hh_, sl: (lambda e: e.activation(
                        rowbank[64:65, sl * QW:(sl + 1) * QW], psUE[hh_][64:65, :],
                        AF.Ln, bias=0.0, scale=1.0)))(hh, slot),
                        waits=wln)
                    w = [oscr_free[hh]] if oscr_free[hh] is not None else []
                    ue_free[hh] = sch.op(
                        "scalar", (lambda hh_: (lambda e: e.copy(
                            oscr[hh_][:], psUE[hh_][0:64, :])))(hh),
                        waits=w, token=True)
                    w = [obias_free[hh]] if obias_free[hh] is not None else []
                    ub_free[hh] = sch.op(
                        "scalar", (lambda hh_: (lambda e: e.copy(
                            obias[hh_][:], psUB[hh_][0:64, :])))(hh),
                        waits=w, token=True)
                    t_r = sch.op("scalar", (lambda sl: (lambda e: e.activation(
                        rowbank[64:65, sl * QW:(sl + 1) * QW],
                        rowbank[64:65, sl * QW:(sl + 1) * QW],
                        AF.Exp, bias=0.0, scale=-1.0)))(slot), token=True)
                    t_w = sch.dma("sync", rrow_d.ap()[slot:slot+1], rowbank[64:65, slot * QW:(slot + 1) * QW],
                        waits=[t_r])
                    rrow_free[slot] = t_w
                    wb = [t_w]
                    if rrep_free[hh] is not None:
                        wb.append(rrep_free[hh])
                    t_rr = sch.dma("sync", rrep[hh][:],
                                   rrow_d.ap()[slot:slot+1]
                                   .partition_broadcast(HD), waits=wb)
                    fslot = fscr[0:64, slot * QW:(slot + 1) * QW]
                    w = [t_rr, ue_free[hh]]
                    if fscr_free[slot] is not None:
                        w.append(fscr_free[slot])
                    t_mul = sch.op(
                        "vector", (lambda hh_, fs_: (lambda e: e.tensor_tensor(
                            fs_, oscr[hh_][:], rrep[hh_][:],
                            ALU.mult)))(hh, fslot),
                        waits=w, token=True)
                    rrep_free[hh] = t_mul
                    oscr_free[hh] = t_mul
                    if hh == 0:
                        t_o = sch.op(
                            "vector", (lambda pr_, qt_, fs_: (lambda e:
                                       e.tensor_tensor(
                                A[pr_][0:64, qt_ * QW:(qt_ + 1) * QW],
                                fs_, obias[0][:], ALU.add)))(
                                    pr, qt, fslot),
                            waits=[ub_free[0]], token=True)
                    else:
                        w2 = [ub_free[1], t_ostg_dma] if t_ostg_dma is not None \
                            else [ub_free[1]]
                        t_o = sch.op(
                            "vector", (lambda fs_: (lambda e: e.tensor_tensor(
                                ostg[:], fs_, obias[1][:],
                                ALU.add)))(fslot),
                            waits=w2, token=True)
                        t_ostg_dma = sch.dma(
                            "sync", A[pr][64:128, qt * QW:(qt + 1) * QW],
                            ostg[:], waits=[t_o])
                    obias_free[hh] = t_o
                    fscr_free[slot] = t_o
            pair_done[par] = sch.op("tensor", lambda e: e.drain(),
                                    token=True)
        t_attn_out = t_ostg_dma
        sch.barrier()
    big.close()
    late = ExitStack()
    X2 = [late.enter_context(nc.sbuf_tensor(f"X2{c}", [P, S], BF16))
          for c in range(NC)]
    x2_spill = []

    with ExitStack() as ph:
        abrep, t_ab, _ = ln_stats(
            ph, lambda c, r: (A[c][:], [t_attn_out]))
        t_prev = None
        for c in range(NC):
            sch.op("vector", (lambda c_, ab_: (lambda e: e.tensor_tensor(
                fscr[:], A[c_][:], ab_[:, 0:S], ALU.mult)))(c, abrep),
                waits=[t_ab] if c == 0 else ())
            sch.op("vector", (lambda ab_: (lambda e: e.tensor_tensor(
                fscr[:], fscr[:], ab_[:, S:2 * S], ALU.add)))(abrep))
            t_prev = sch.op("vector", (lambda c_: (lambda e: e.tensor_scalar(
                X2[c_][:], fscr[:], gam[:, c_:c_ + 1], bet[:, c_:c_ + 1],
                ALU.mult, ALU.add)))(c),
                waits=gb_toks if c == 0 else (), token=True)
        sch.barrier()
    with ExitStack() as ph:
        abrep, t_ab, _ = ln_stats(ph, lambda c, r: (X2[c][:], []))
        xb0 = ph.enter_context(nc.sbuf_tensor("xb0", [P, S], BF16))
        xb1 = ph.enter_context(nc.sbuf_tensor("xb1", [P, S], BF16))
        prev = [None, None]
        for c in range(NC):
            b = [xb0, xb1][c % 2]
            w = [prev[c % 2]] if prev[c % 2] is not None else []
            t_l = sch.dma("gpsimd", b[:], xT_d[c * P:(c + 1) * P, :], waits=w)
            sch.op("vector", (lambda c_, ab_: (lambda e: e.tensor_tensor(
                fscr[:], X2[c_][:], ab_[:, 0:S], ALU.mult)))(c, abrep),
                waits=[t_ab] if c == 0 else ())
            sch.op("vector", (lambda ab_: (lambda e: e.tensor_tensor(
                fscr[:], fscr[:], ab_[:, S:2 * S], ALU.add)))(abrep))
            prev[c % 2] = sch.op("vector", (lambda c_, b_: (lambda e:
                                 e.tensor_tensor(
                X2[c_][:], fscr[:], b_[:], ALU.add)))(c, b),
                waits=[t_l], token=True)
        sch.barrier()

    with ExitStack() as ph:
        abrep, t_ab, _ = ln_stats(ph, lambda c, r: (X2[c][:], []))
        t_mz = sch.op("vector", lambda e: e.memset(lmax[:, 0:2], 0.0),
                      token=True)
        t_prev = t_mz
        for c in range(NC):
            sch.op("vector", (lambda c_, ab_: (lambda e: e.tensor_tensor(
                fscr[:], X2[c_][:], ab_[:, 0:S], ALU.mult)))(c, abrep),
                waits=[t_ab, t_mz] if c == 0 else ())
            sch.op("vector", (lambda c_, ab_: (lambda e: e.tensor_tensor(
                A[c_][:], fscr[:], ab_[:, S:2 * S], ALU.add)))(c, abrep))
            sch.op("vector", (lambda c_: (lambda e: e.tensor_reduce(
                lmax[:, 2:3], A[c_][:], AX.X, ALU.max,
                apply_absolute_value=True)))(c))
            sch.op("vector", lambda e: e.drain())
            t_prev = sch.op("vector", lambda e: e.tensor_tensor(
                lmax[:, 0:1], lmax[:, 0:1], lmax[:, 2:3], ALU.max),
                token=True)
            x2_spill.append(sch.dma(
                "sync", x2_d.ap()[c * P:(c + 1) * P, :], X2[c][:],
                waits=[t_prev]))
        t_g = fold_partitions_maxes(t_prev, 0, [0])
        t_s1 = make_scale(t_g, 0, 10)
        if dbg2_d is not None:
            sch.dma("sync", dbg2_d[:, 0:16], gsc[:, 0:16], waits=[t_s1])
        for c in range(NC):
            tq_ = qdq_pair(A[c][:], A[c][:], 0, [t_s1] if c == 0 else [],
                           token=True)
            if xq_d is not None:
                sch.dma("sync", xq_d[c * P:(c + 1) * P, :], A[c][:],
                        waits=[tq_])
        sch.op("sync", None, waits=x2_spill)
        sch.barrier()
    late.close()

    big2 = ExitStack()
    wuT = big2.enter_context(nc.sbuf_tensor("wuT", [P, 48 * 512], BF16))
    wgT = big2.enter_context(nc.sbuf_tensor("wgT", [P, 48 * 512], BF16))

    def ug_stat(wT, kc, oc):
        off = oc * 1024 + (kc // 4) * 512 + (kc % 4) * P
        return wT[:, off:off + P]
    with ExitStack() as ph:
        evb = [ph.enter_context(nc.sbuf_tensor(f"ev{i}", [P, S], BF16))
               for i in range(2)]
        pj2 = [ph.enter_context(nc.psum_tensor(f"pj2{i}", [P, QW], F32))
               for i in range(4)]
        b0s = [ph.enter_context(nc.sbuf_tensor(f"s1b{i}", [P, S], BF16))
               for i in range(2)]
        fscr2 = ph.enter_context(nc.sbuf_tensor("fscr2", [P, S], F32))
        fscrC = ph.enter_context(nc.sbuf_tensor("fscrC", [P, S], F32))
        tg2b = ph.enter_context(nc.sbuf_tensor("tg2b", [P, S], BF16))
        t_mz = sch.op("vector", lambda e: e.memset(lmax[:], 0.0),
                      token=True)
        wg_toks = [sch.dma("sync", wgT[:, b * 512:(b + 1) * 512],
                           wgT_d.ap()[b * P:(b + 1) * P, :],
                           waits=spill["wgT"] if b == 0 else ())
                   for b in range(48)]
        wu_toks = [sch.dma("sync", wuT[:, b * 512:(b + 1) * 512],
                           wuT_d.ap()[b * P:(b + 1) * P, :],
                           waits=spill["wuT"] if b == 0 else ())
                   for b in range(48)]
        bank_free = [None] * 4
        ev_free = [None, None]

        def mm_group(wT, oc, first_toks, sp_d, key, acol, scol):
            eb = evb[oc % 2]
            for n in range(NQ):
                pt = pj2[n]
                for kc in range(NC):
                    w = []
                    if kc == 0 and bank_free[n] is not None:
                        w.append(bank_free[n])
                    if first_toks is not None and n == 0 and kc == 0:
                        w.extend(first_toks)
                    sch.op("tensor", (lambda wT_, kc_, oc_, n_, pt_, st_,
                                      sp_: (lambda e: e.matmul(
                        pt_[:], ug_stat(wT_, kc_, oc_),
                        A[kc_][:, n_ * QW:(n_ + 1) * QW],
                        start=st_, stop=sp_)))(
                            wT, kc, oc, n, pt, kc == 0, kc == NC - 1),
                        waits=w)
            t_mm = sch.op("tensor", lambda e: e.drain(), token=True)
            t_ev = None
            for n in range(NQ):
                w = [t_mm] if n == 0 else []
                if n == 0 and ev_free[oc % 2] is not None:
                    w.append(ev_free[oc % 2])
                t_ev = sch.op("scalar", (lambda b_, n_, pt_: (lambda e:
                                         e.copy(
                    b_[:, n_ * QW:(n_ + 1) * QW], pt_[:])))(
                        eb, n, pj2[n]),
                    waits=w, token=True)
                bank_free[n] = t_ev
            sch.op("vector", (lambda b_, sc: (lambda e: e.tensor_reduce(
                lmax[:, sc:sc + 1], b_[:], AX.X, ALU.max,
                apply_absolute_value=True)))(eb, scol),
                waits=[t_ev, t_mz])
            sch.op("vector", lambda e: e.drain())
            t_mx = sch.op("vector", (lambda ac, sc: (lambda e:
                          e.tensor_tensor(
                lmax[:, ac:ac + 1], lmax[:, ac:ac + 1], lmax[:, sc:sc + 1],
                ALU.max)))(acol, scol), token=True)
            ev_free[oc % 2] = sch.dma(
                "sync", sp_d.ap()[oc * P:(oc + 1) * P, :], eb[:],
                waits=[t_ev, t_mx])
            spill[(key, oc)] = ev_free[oc % 2]
            return t_mx, t_mm

        for oc in range(NI):
            t_mx_g, _ = mm_group(wgT, oc, wg_toks if oc == 0 else None,
                                 gt_d, "gt", 0, 2)
        t_g3 = fold_partitions_maxes(t_mx_g, 1, [0])
        t_s3 = make_scale(t_g3, 2, 10)

        prev = [None, None]
        t_tg_free = None
        for oc in range(NI):
            t_mx_u, _ = mm_group(wuT, oc, wu_toks if oc == 0 else None,
                                 up_d, "up", 1, 2)
            b0 = b0s[oc % 2]
            w = [prev[oc % 2]] if prev[oc % 2] is not None else []
            w.append(spill[("gt", oc)])
            t_l = sch.dma("sync", b0[:], gt_d.ap()[oc * P:(oc + 1) * P, :],
                          waits=w)
            sch.op("vector", (lambda b_: (lambda e: e.tensor_scalar(
                fscrC[:], b_[:], gsc[:, 5:6], MAGIC, ALU.mult, ALU.add)))(b0),
                waits=[t_l, t_s3] if oc == 0 else [t_l])
            prev[oc % 2] = sch.op("vector", lambda e: e.tensor_scalar(
                fscr[:], fscrC[:], MAGIC, gsc[:, 4:5], ALU.subtract, ALU.mult),
                token=True)
            t_gq = prev[oc % 2]
            t_sig = sch.op("scalar", lambda e: e.activation(
                fscr2[:], fscr[:], AF.Sigmoid, bias=0.0, scale=1.0),
                waits=[t_gq], token=True)
            sch.op("vector", lambda e: e.tensor_scalar(
                fscrC[:], fscr2[:], 65536.0, MAGIC, ALU.mult, ALU.add),
                waits=[t_sig])
            sch.op("vector", lambda e: e.tensor_scalar(
                fscr2[:], fscrC[:], MAGIC, 65535.0, ALU.subtract, ALU.min))
            w = [t_tg_free] if t_tg_free is not None else []
            t_t2 = sch.op("vector", lambda e: e.tensor_tensor(
                tg2b[:], fscr[:], fscr2[:], ALU.mult), waits=w, token=True)
            sch.op("vector", lambda e: e.tensor_reduce(
                lmax[:, 3:4], tg2b[:], AX.X, ALU.max,
                apply_absolute_value=True))
            sch.op("vector", lambda e: e.drain())
            t_mx_s1 = sch.op("vector", lambda e: e.tensor_tensor(
                lmax[:, 0:1], lmax[:, 0:1], lmax[:, 3:4], ALU.max),
                token=True)
            t_tg_free = sch.dma("sync", or_d.ap()[oc * P:(oc + 1) * P, :],
                                tg2b[:], waits=[t_t2, t_mx_s1])
            spill[("or", oc)] = t_tg_free
        t_g24 = fold_partitions_maxes(t_mx_s1, 2, [1, 0])
        t_s2 = make_scale(t_g24, 1, 10)
        t_s4 = make_scale(t_g24, 3, 11, pre=1.0 / 65536.0)
        sch.barrier()
    big2.close()

    with ExitStack() as ph:
        wdT = ph.enter_context(nc.sbuf_tensor("wdT", [P, 48 * 512], BF16))

        def wd_stat(icx, oc):
            off = ((oc * 3 + icx // 8) * 2 + (icx % 8) // 4) * 512 \
                + (icx % 4) * P
            return wdT[:, off:off + P]
        b0s = [ph.enter_context(nc.sbuf_tensor(f"s2b{i}", [P, S], BF16))
               for i in range(2)]
        upb = [ph.enter_context(nc.sbuf_tensor(f"upb{i}", [P, S], BF16))
               for i in range(2)]
        fscr2 = ph.enter_context(nc.sbuf_tensor("fscr2b", [P, S], F32))
        fscrC = ph.enter_context(nc.sbuf_tensor("fscrCb", [P, S], F32))
        ph9_out = ph.enter_context(nc.sbuf_tensor("ph9o", [P, S], BF16))
        wd_toks = [sch.dma("sync", wdT[:, b * 512:(b + 1) * 512],
                           wdT_d.ap()[b * P:(b + 1) * P, :],
                           waits=spill["wdT"] if b == 0 else ())
                   for b in range(48)]
        prev = [None, None]
        t_up_free = [None, None]
        t_out_free = None
        for oc in range(NI):
            b0 = b0s[oc % 2]
            w = [prev[oc % 2]] if prev[oc % 2] is not None else []
            w.append(spill[("or", oc)])
            t_l = sch.dma("sync", b0[:], or_d.ap()[oc * P:(oc + 1) * P, :],
                          waits=w)
            w = [t_up_free[oc % 2]] if t_up_free[oc % 2] is not None else []
            w.append(spill[("up", oc)])
            t_lu = sch.dma("sync", upb[oc % 2][:],
                           up_d.ap()[oc * P:(oc + 1) * P, :], waits=w)
            sch.op("vector", (lambda b_: (lambda e: e.tensor_scalar(
                fscrC[:], b_[:], gsc[:, 7:8], MAGIC, ALU.mult, ALU.add)))(b0),
                waits=([t_l, t_s4, t_s2] if oc == 0 else [t_l]))
            prev[oc % 2] = sch.op("vector", lambda e: e.tensor_scalar(
                fscr[:], fscrC[:], MAGIC, gsc[:, 6:7], ALU.subtract, ALU.mult),
                token=True)
            sch.op("vector", (lambda u_: (lambda e: e.tensor_scalar(
                fscrC[:], u_[:], gsc[:, 3:4], MAGIC, ALU.mult,
                ALU.add)))(upb[oc % 2]), waits=[t_lu])
            t_up_free[oc % 2] = sch.op("vector", lambda e: e.tensor_scalar(
                fscr2[:], fscrC[:], MAGIC, gsc[:, 2:3],
                ALU.subtract, ALU.mult), token=True)
            w = [t_out_free] if t_out_free is not None else []
            t_o = sch.op("vector", lambda e: e.tensor_tensor(
                ph9_out[:], fscr[:], fscr2[:], ALU.mult), waits=w, token=True)
            sch.op("vector", lambda e: e.tensor_reduce(
                lmax[:, 2:3], ph9_out[:], AX.X, ALU.max,
                apply_absolute_value=True))
            sch.op("vector", lambda e: e.drain())
            t_mx = sch.op("vector", lambda e: e.tensor_tensor(
                lmax[:, 5:6], lmax[:, 5:6], lmax[:, 2:3], ALU.max),
                token=True)
            t_out_free = sch.dma("sync", gt_d.ap()[oc * P:(oc + 1) * P, :],
                                 ph9_out[:], waits=[t_o, t_mx])
            spill[("og", oc)] = t_out_free
        t_g = fold_partitions_maxes(t_mx, 3, [5])
        t_s5 = make_scale(t_g, 4, 10)
        ob = [ph.enter_context(nc.sbuf_tensor(f"ob{i}", [P, QW], BF16))
              for i in range(NI)]
        orb = [ph.enter_context(nc.sbuf_tensor(f"orb{i}", [P, QW], BF16))
               for i in range(2)]
        pj3 = [ph.enter_context(nc.psum_tensor(f"pj3{i}", [P, QW], F32))
               for i in range(NC)]
        bank_free = [None] * NC
        ob_free = [None] * NI

        prev = [None, None]
        for n in range(NQ):
            for icx in range(NI):
                b0 = orb[icx % 2]
                w = [prev[icx % 2]] if prev[icx % 2] is not None else []
                w.append(spill[("og", icx)])
                t_l = sch.dma("sync", b0[:, 0:QW],
                              gt_d.ap()[icx * P:(icx + 1) * P,
                                        n * QW:(n + 1) * QW], waits=w)
                w = [t_l] + ([t_s5] if (n == 0 and icx == 0) else [])
                if ob_free[icx] is not None:
                    w.append(ob_free[icx])
                sch.op("vector", (lambda b_: (lambda e: e.tensor_scalar(
                    fscr[:, 0:QW], b_[:, 0:QW], gsc[:, 9:10], MAGIC,
                    ALU.mult, ALU.add)))(b0), waits=w)
                t_ov = sch.op(
                    "vector", (lambda icx_: (lambda e: e.tensor_scalar(
                        ob[icx_][:], fscr[:, 0:QW], MAGIC, gsc[:, 8:9],
                        ALU.subtract, ALU.mult)))(icx), token=True)
                prev[icx % 2] = t_ov
                for oc in range(NC):
                    w = []
                    if oc == 0:
                        w = [t_ov] + list(
                            [wd_toks[-1]] if (n == 0 and icx == 0) else [])
                    if icx == 0:
                        if bank_free[oc] is not None:
                            w.append(bank_free[oc])
                    sch.op("tensor", (lambda icx_, oc_, st_, sp_:
                                      (lambda e: e.matmul(
                        pj3[oc_][:], wd_stat(icx_, oc_),
                        ob[icx_][:], start=st_, stop=sp_)))(
                            icx, oc, icx == 0, icx == NI - 1),
                        waits=w)
                ob_free[icx] = sch.op("tensor", lambda e: e.drain(),
                                      token=True)
            t_mm = ob_free[NI - 1]
            for oc in range(NC):
                t_ev = sch.op("scalar", (lambda oc_, n_: (lambda e:
                              e.copy(
                    A[oc_][:, n_ * QW:(n_ + 1) * QW], pj3[oc_][:])))(oc, n),
                    waits=[t_mm] if oc == 0 else (), token=True)
                bank_free[oc] = t_ev
                spill.setdefault(("y", oc), []).append(t_ev)
        sch.barrier()

    with ExitStack() as ph:
        obuf = [ph.enter_context(nc.sbuf_tensor(f"obf{i}", [P, S], F32))
                for i in range(2)]
        xrb = [ph.enter_context(nc.sbuf_tensor(f"xr{i}", [P, S], BF16))
               for i in range(2)]

        def get_y(c, reuse):
            return A[c][:], list(spill[("y", c)])

        abrep, t_ab, cons = ln_stats(ph, get_y)
        prev = [None, None]
        oprev = [None, None]
        for c in range(NC):
            okb = obuf[c % 2]
            w = [prev[c % 2]] if prev[c % 2] is not None else []
            t_x2 = sch.dma("sync", xrb[c % 2][:],
                           x2_d.ap()[c * P:(c + 1) * P, :], waits=w)
            sch.op("vector", (lambda c_, ab_: (lambda e: e.tensor_tensor(
                fscr[:], A[c_][:], ab_[:, 0:S], ALU.mult)))(c, abrep),
                waits=[t_ab])
            sch.op("vector", (lambda ab_: (lambda e: e.tensor_tensor(
                fscr[:], fscr[:], ab_[:, S:2 * S], ALU.add)))(abrep))
            w = [oprev[c % 2], t_x2] if oprev[c % 2] is not None else [t_x2]
            prev[c % 2] = sch.op(
                "vector", (lambda c_, o_: (lambda e: e.tensor_tensor(
                    o_[:], fscr[:], xrb[c_ % 2][:], ALU.add)))(c, okb),
                waits=w, token=True)
            oprev[c % 2] = sch.dma(
                "sync", out_d[c * P:(c + 1) * P, :], obuf[c % 2][:],
                waits=[prev[c % 2]])
        if dbg_d is not None:
            sch.dma("sync", dbg_d[:, 0:12], gsc[:, 0:12])
            sch.dma("sync", dbg_d[:, 12:16], lmax[:])
        total = sch.qn["sync"]
        fin = []
        for r in range(DMA_ROT):
            n_r = (total - r + DMA_ROT - 1) // DMA_ROT if total > r else 0
            if n_r:
                fin.append(Tok(f"q_sync{r}", 16 * n_r))
        sch.op("sync", None, waits=fin)

    emit(nc, sch)
    gstack.close()
    return nc


def emit(nc, sch):
    with ExitStack() as es:
        names = ["t_" + e for e in ENGINES] + ["B", "C"]
        for q in ("sync", "gpsimd", "scalar"):
            for r in range(DMA_ROT):
                names.append(f"q_{q}{r}")
        sems = {n: es.enter_context(nc.semaphore(n)) for n in names}
        with nc.Block() as block:
            def runner(eng_name):
                def f(eng):
                    for make, waits, tok in sch.items[eng_name]:
                        for t in waits:
                            eng.wait_ge(sems[t.sem], t.val)
                        if make is None:
                            continue
                        inst = make(eng)
                        if tok is not None:
                            if tok == "C":
                                inst.then_inc(sems["C"], 1)
                            elif tok.sem.startswith("q_"):
                                inst.then_inc(sems[tok.sem], 16)
                            elif eng_name in ("vector", "scalar") and \
                                    tok.sem != "B":
                                eng.drain().then_inc(sems[tok.sem], 1)
                            else:
                                inst.then_inc(sems[tok.sem], 1)

                return f
            block.sync(runner("sync"))
            block.gpsimd(runner("gpsimd"))
            block.vector(runner("vector"))
            block.scalar(runner("scalar"))
            block.tensor(runner("tensor"))


def kernel(L0_attn_x, L0_attn_Wq, L0_attn_Wk, L0_attn_Wv,
           L0_attn_rel_bias_table, L0_attn_ln_gamma, L0_attn_ln_beta,
           L0_ffn_w_gate, L0_ffn_w_up, L0_ffn_w_down):
    x = np.asarray(L0_attn_x)
    table = np.asarray(L0_attn_rel_bias_table)
    idx = 127 + np.arange(3968)[None, :] - np.arange(128)[:, None]
    ttarr = np.ascontiguousarray(np.transpose(table[idx, :], (2, 0, 1)))
    base = {
        "wqT": np.ascontiguousarray(np.asarray(L0_attn_Wq).T),
        "wkT": np.ascontiguousarray(np.asarray(L0_attn_Wk).T),
        "wvT": np.ascontiguousarray(np.asarray(L0_attn_Wv).T),
        "tt": ttarr,
        "gam": np.ascontiguousarray(
            np.asarray(L0_attn_ln_gamma).reshape(NC, P)),
        "bet": np.ascontiguousarray(
            np.asarray(L0_attn_ln_beta).reshape(NC, P)),
        "wg": np.ascontiguousarray(np.asarray(L0_ffn_w_gate)),
        "wu": np.ascontiguousarray(np.asarray(L0_ffn_w_up)),
        "wd": np.ascontiguousarray(np.asarray(L0_ffn_w_down)),
        "ident": np.eye(P, dtype=ml_dtypes.bfloat16),
    }
    in_maps = []
    for b in range(8):
        m = dict(base)
        m["xT"] = np.ascontiguousarray(x[b].T)
        in_maps.append(m)
    nc = build_program()
    res = run_bass_kernel_spmd(nc, in_maps, core_ids=list(range(8)))
    outs = [np.asarray(r["out"]).T for r in res.results]
    return np.ascontiguousarray(np.stack(outs, axis=0))


if __name__ == "__main__":
    print("building...")
    nc = build_program()
    print("built ok")


# version 28
# speedup vs baseline: 1.0426x; 1.0426x over previous
import os
import numpy as np
import ml_dtypes
import concourse.bass as bass
import concourse.mybir as mybir
from concourse.bass_utils import run_bass_kernel_spmd
from contextlib import ExitStack

F32 = mybir.dt.float32
BF16 = mybir.dt.bfloat16
AF = mybir.ActivationFunctionType
ALU = mybir.AluOpType
AX = mybir.AxisListType

P = 128
S = 2048
D = 1024
H = 16
HD = 64
NC = D // P
NT = S // P
I3 = 3072
NI = I3 // P
NQ = 4
QW = 512
MAGIC = 3 * 2.0**22
EPS = 1e-5
QMAX = 32767.0
ATT_SCALE = 1.0 / 32.0

ENGINES = ("sync", "gpsimd", "vector", "scalar", "tensor")


class Tok:
    __slots__ = ("sem", "val")

    def __init__(self, sem, val):
        self.sem = sem
        self.val = val


DMA_ROT = 16


class Sched:
    def __init__(self):
        self.items = {e: [] for e in ENGINES}
        self.tokn = {e: 0 for e in ENGINES}
        self.qn = {"sync": 0, "gpsimd": 0, "scalar": 0}
        self.bn = 0

    def op(self, eng, make, waits=(), token=False):
        tok = None
        if token:
            self.tokn[eng] += 1
            tok = Tok("t_" + eng, self.tokn[eng])
        self.items[eng].append((make, list(waits), tok))
        return tok

    def dma(self, eng, out_ap, in_ap, waits=()):
        i = self.qn[eng]
        self.qn[eng] += 1
        sem = f"q_{eng}{i % DMA_ROT}"
        occ = i // DMA_ROT
        w = list(waits)
        if occ > 0:
            w.append(Tok(sem, 16 * occ))
        tok = Tok(sem, 16 * (occ + 1))
        mk = (lambda o, i_: (lambda e: e.dma_start(out=o, in_=i_)))(out_ap, in_ap)
        self.items[eng].append((mk, w, tok))
        return tok

    def cc(self, make, waits=()):
        self.items["gpsimd"].append((make, list(waits), "C"))
        self.ccn = getattr(self, "ccn", 0) + 1
        return Tok("C", self.ccn)

    def barrier(self, engines=ENGINES):
        self.bn += len(engines)
        tgt = self.bn
        for e in engines:
            self.items[e].append((lambda en: en.drain(), [], Tok("B", tgt)))
        for e in engines:
            self.items[e].append((None, [Tok("B", tgt)], None))


def build_program():
    nc = bass.Bass("TRN2", target_bir_lowering=False,
                   detect_race_conditions=False)

    xT_d = nc.dram_tensor("xT", [D, S], F32, kind="ExternalInput").ap()
    wqT_d = nc.dram_tensor("wqT", [D, D], F32, kind="ExternalInput").ap()
    wkT_d = nc.dram_tensor("wkT", [D, D], F32, kind="ExternalInput").ap()
    wvT_d = nc.dram_tensor("wvT", [D, D], F32, kind="ExternalInput").ap()
    tt_d = nc.dram_tensor("tt", [H, P, 3968], F32, kind="ExternalInput").ap()
    gam_d = nc.dram_tensor("gam", [NC, P], F32, kind="ExternalInput").ap()
    bet_d = nc.dram_tensor("bet", [NC, P], F32, kind="ExternalInput").ap()
    wg_d = nc.dram_tensor("wg", [I3, D], F32, kind="ExternalInput").ap()
    wu_d = nc.dram_tensor("wu", [I3, D], F32, kind="ExternalInput").ap()
    wd_d = nc.dram_tensor("wd", [D, I3], F32, kind="ExternalInput").ap()
    id_d = nc.dram_tensor("ident", [P, P], BF16, kind="ExternalInput").ap()
    out_d = nc.dram_tensor("out", [D, S], F32, kind="ExternalOutput").ap()

    dbg_d = (nc.dram_tensor("dbg", [P, 16], F32, kind="ExternalOutput").ap()
             if os.environ.get("KDBG") else None)
    dbg2_d = (nc.dram_tensor("dbg2", [P, 16], F32, kind="ExternalOutput").ap()
              if os.environ.get("KDBG") else None)
    xq_d = (nc.dram_tensor("xq_dbg", [D, S], BF16, kind="ExternalOutput").ap()
            if os.environ.get("KDBG") else None)
    bscr_d = nc.dram_tensor("bscr", [8, 4096], F32)
    lscr_d = nc.dram_tensor("lscr", [12, 128], F32)
    rrow_d = nc.dram_tensor("rrow", [4, 512], F32)
    cci = [nc.dram_tensor(f"cci{i}", [1, 2], F32) for i in range(5)]
    cco = [nc.dram_tensor(f"cco{i}", [1, 2], F32) for i in range(5)]
    wdT_d = nc.dram_tensor("wdT_s", [48 * P, 512], BF16)
    wuT_d = nc.dram_tensor("wuT_s", [48 * P, 512], BF16)
    wgT_d = nc.dram_tensor("wgT_s", [48 * P, 512], BF16)
    spill = {}
    x2_d = nc.dram_tensor("x2_s", [D, S], BF16)
    up_d = nc.dram_tensor("up_s", [I3, S], BF16, kind="ExternalOutput" if os.environ.get("KDBG") else "Internal")
    gt_d = nc.dram_tensor("gt_s", [I3, S], BF16, kind="ExternalOutput" if os.environ.get("KDBG") else "Internal")
    or_d = nc.dram_tensor("or_s", [I3, S], BF16, kind="ExternalOutput" if os.environ.get("KDBG") else "Internal")

    sch = Sched()
    gstack = ExitStack()

    def gsb(name, shape, dt=F32):
        return gstack.enter_context(nc.sbuf_tensor(name, shape, dt))

    A = [gsb(f"A{c}", [P, S], BF16) for c in range(NC)]
    csum = gsb("csum", [P, 1], BF16)
    idn = gsb("idn", [P, P], BF16)
    gam = gsb("gam_s", [P, NC], F32)
    bet = gsb("bet_s", [P, NC], F32)
    fscr = gsb("fscr", [P, S], F32)
    rowbank = gsb("rowbank", [P, 4256], F32)
    lmax = gsb("lmax", [P, 6], F32)
    gsc = gsb("gsc", [P, 20], F32)

    sch.op("vector", lambda e: e.memset(gsc[:, 16:17], 2.0))
    t_csum0 = sch.op("vector", lambda e: e.memset(csum[:], 1.0 / 1024.0),
                     token=True)
    t_id = sch.dma("sync", idn[:], id_d)
    gb_toks = []
    for c in range(NC):
        gb_toks.append(sch.dma("sync", gam[:, c:c + 1],
                               gam_d[c:c+1].rearrange("a p -> p a")))
        gb_toks.append(sch.dma("sync", bet[:, c:c + 1],
                               bet_d[c:c+1].rearrange("a p -> p a")))
    t_gb = gb_toks[-1]

    ln_n = [0]

    def ln_stats(stack, get_chunk):
        ln_n[0] += 1
        nn = ln_n[0]
        pm = stack.enter_context(nc.psum_tensor(f"pm{nn}", [1, S], F32))
        pq = stack.enter_context(nc.psum_tensor(f"pq{nn}", [1, S], F32))
        sq0 = stack.enter_context(nc.sbuf_tensor(f"sq{nn}a", [P, S], BF16))
        sq1 = stack.enter_context(nc.sbuf_tensor(f"sq{nn}b", [P, S], BF16))
        sqs = [sq0, sq1]
        sq_cons = [None, None]
        cons = {}
        for c in range(NC):
            ap, toks = get_chunk(c, cons.get(c - 2))
            sq = sqs[c % 2]
            w = list(toks)
            if sq_cons[c % 2] is not None:
                w.append(sq_cons[c % 2])
            t_sq = sch.op("scalar", (lambda a_, q_: (lambda e: e.activation(
                q_[:], a_, AF.Square, bias=0.0, scale=1.0)))(ap, sq),
                waits=w, token=True)
            for n in range(NQ):
                st, sp = (c == 0), (c == NC - 1)
                sch.op("tensor", (lambda a_, n_, st_, sp_: (lambda e: e.matmul(
                    pm[0:1, n_ * QW:(n_ + 1) * QW], csum[:],
                    a_[:, n_ * QW:(n_ + 1) * QW], start=st_, stop=sp_)))(
                        ap, n, st, sp),
                    waits=list(toks) if n == 0 else ())
                cons[c] = sch.op("tensor", (lambda q_, n_, st_, sp_:
                                            (lambda e: e.matmul(
                    pq[0:1, n_ * QW:(n_ + 1) * QW], csum[:],
                    q_[:, n_ * QW:(n_ + 1) * QW], start=st_, stop=sp_)))(
                        sq, n, st, sp),
                    waits=[t_sq] if n == 0 else (), token=(n == NQ - 1))
            sq_cons[c % 2] = cons[c]
        t_mm = cons[NC - 1]
        t_sqm = sch.op("scalar", lambda e: e.activation(
            fscr[0:1, 0:S], pm[0:1, :], AF.Square, bias=0.0, scale=1.0),
            waits=[t_mm], token=True)
        t_v = sch.op("vector", lambda e: e.scalar_tensor_tensor(
            fscr[0:1, 0:S], pq[0:1, :], EPS, fscr[0:1, 0:S],
            ALU.add, ALU.subtract), waits=[t_sqm], token=True)
        t_ln = sch.op("scalar", lambda e: e.activation(
            rowbank[0:1, 0:S], fscr[0:1, 0:S], AF.Ln, bias=0.0, scale=1.0),
            waits=[t_v], token=True)
        t_h = sch.op("vector", lambda e: e.tensor_scalar(
            rowbank[0:1, S:2 * S], rowbank[0:1, 0:S], -0.5, None, ALU.mult),
            waits=[t_ln], token=True)
        t_a = sch.op("scalar", lambda e: e.activation(
            rowbank[0:1, 0:S], rowbank[0:1, S:2 * S], AF.Exp, bias=0.0,
            scale=1.0), waits=[t_h], token=True)
        t_b = sch.op("vector", lambda e: e.scalar_tensor_tensor(
            rowbank[0:1, S:2 * S], pm[0:1, :], -1.0, rowbank[0:1, 0:S],
            ALU.mult, ALU.mult), waits=[t_a], token=True)
        slot = nn % 8
        t_w = sch.dma("sync", bscr_d.ap()[slot:slot+1],
                      rowbank[0:1, 0:4096], waits=[t_b])
        abrep = stack.enter_context(
            nc.sbuf_tensor(f"abrep{nn}", [P, 4096], F32))
        t_bc = sch.dma("sync", abrep[:],
                       bscr_d.ap()[slot:slot+1]
                       .partition_broadcast(P), waits=[t_w])
        return abrep, t_bc, cons

    def fold_partitions_maxes(t_ready, round_idx, cols):
        toks = []
        nvals = len(cols)
        for j, c in enumerate(cols):
            t1 = sch.dma("sync",
                         lscr_d.ap()[2 * round_idx + j:2 * round_idx + j + 1, 0:P],
                         lmax[:, c:c + 1], waits=[t_ready] if t_ready else [])
            t2 = sch.dma("sync", rowbank[0:1, 4104:4104 + P],
                         lscr_d.ap()[2 * round_idx + j:2 * round_idx + j + 1, 0:P],
                         waits=[t1] + toks[-1:])
            tv = sch.op("vector", (lambda j_: (lambda e: e.tensor_reduce(
                rowbank[0:1, 4096 + j_:4097 + j_], rowbank[0:1, 4104:4104 + P],
                AX.X, ALU.max)))(j), waits=[t2], token=True)
            toks.append(tv)
        t3 = sch.dma("gpsimd", cci[round_idx].ap()[0:1, 0:nvals],
                     rowbank[0:1, 4096:4096 + nvals], waits=toks)
        t_cc = sch.cc((lambda ri, nv: (lambda e: e.collective_compute(
            "AllReduce", ALU.max, replica_groups=[list(range(8))],
            ins=[cci[ri].ap()[0:1, 0:nv].opt()],
            outs=[cco[ri].ap()[0:1, 0:nv].opt()])))(round_idx, nvals),
            waits=[t3])
        t4 = sch.dma("gpsimd", gsc[:, 10:10 + nvals],
                     cco[round_idx].ap()[0:1, 0:nvals]
                     .partition_broadcast(P), waits=[t_cc])
        return t4

    def make_scale(t_g, k, raw_col, pre=1.0):
        t_s = sch.op("vector", (lambda k_, rc, p_: (lambda e: e.tensor_scalar(
            gsc[:, 2 * k_:2 * k_ + 1], gsc[:, rc:rc + 1], p_ / QMAX, 1e-12,
            ALU.mult, ALU.max)))(k, raw_col, pre), waits=[t_g], token=True)
        sch.op("scalar", (lambda k_: (lambda e: e.activation(
            gsc[:, 12:13], gsc[:, 2 * k_:2 * k_ + 1], AF.Ln,
            bias=0.0, scale=1.0)))(k), waits=[t_s])
        t_r0 = sch.op("scalar", (lambda k_: (lambda e: e.activation(
            gsc[:, 13:14], gsc[:, 12:13], AF.Exp,
            bias=0.0, scale=-1.0)))(k), token=True)
        sch.op("vector", (lambda k_: (lambda e: e.tensor_tensor(
            gsc[:, 14:15], gsc[:, 2 * k_:2 * k_ + 1],
            gsc[:, 13:14], ALU.mult)))(k), waits=[t_r0])
        sch.op("vector", lambda e: e.drain())
        sch.op("vector", lambda e: e.tensor_tensor(
            gsc[:, 15:16], gsc[:, 16:17], gsc[:, 14:15], ALU.subtract))
        sch.op("vector", lambda e: e.drain())
        t_fin = sch.op("vector", (lambda k_: (lambda e: e.tensor_tensor(
            gsc[:, 2 * k_ + 1:2 * k_ + 2], gsc[:, 13:14], gsc[:, 15:16],
            ALU.mult)))(k), token=(pre == 1.0))
        if pre != 1.0:
            sch.op("vector", (lambda p_: (lambda e: e.memset(
                gsc[:, 17:18], p_)))(pre))
            sch.op("vector", lambda e: e.drain())
            t_fin = sch.op("vector", (lambda k_: (lambda e: e.tensor_tensor(
                gsc[:, 2 * k_ + 1:2 * k_ + 2],
                gsc[:, 2 * k_ + 1:2 * k_ + 2],
                gsc[:, 17:18], ALU.mult)))(k), token=True)
        return t_fin

    def qdq_pair(src_ap, dst_ap, k, waits, token=False):
        sch.op("vector", (lambda s_, k_: (lambda e: e.tensor_scalar(
            fscr[:, 0:s_ap_w(s_)], s_, gsc[:, 2 * k_ + 1:2 * k_ + 2], MAGIC,
            ALU.mult, ALU.add)))(src_ap, k), waits=waits)
        return sch.op("vector", (lambda s_, d_, k_: (lambda e: e.tensor_scalar(
            d_, fscr[:, 0:s_ap_w(s_)], MAGIC, gsc[:, 2 * k_:2 * k_ + 1],
            ALU.subtract, ALU.mult)))(src_ap, dst_ap, k), token=token)

    def s_ap_w(ap):
        return ap.shape[-1]

    t_csum = t_csum0
    with ExitStack() as ph:
        xs0 = ph.enter_context(nc.sbuf_tensor("xs0", [P, S], BF16))
        xs1 = ph.enter_context(nc.sbuf_tensor("xs1", [P, S], BF16))
        bufs = [xs0, xs1]

        def get_chunk(c, reuse):
            b = bufs[c % 2]
            w = [reuse] if reuse is not None else []
            t = sch.dma("gpsimd", b[:], xT_d[c * P:(c + 1) * P, :], waits=w)
            toks = [t] + ([t_csum] if c == 0 else [])
            return b[:], toks

        abrep, t_ab, cons = ln_stats(ph, get_chunk)

        prev = [cons[NC - 2], cons[NC - 1]]
        for c in range(NC):
            b = bufs[c % 2]
            w = [prev[c % 2]] if prev[c % 2] is not None else []
            t_l = sch.dma("gpsimd", b[:], xT_d[c * P:(c + 1) * P, :],
                          waits=w)
            sch.op("vector", (lambda b_, ab_: (lambda e: e.tensor_tensor(
                fscr[:], b_[:], ab_[:, 0:S], ALU.mult)))(b, abrep),
                waits=[t_l, t_ab])
            prev[c % 2] = sch.op("vector", (lambda c_, ab_: (lambda e: e.tensor_tensor(
                A[c_][:], fscr[:], ab_[:, S:2 * S], ALU.add)))(c, abrep),
                token=True)
        sch.barrier()

    big = ExitStack()
    qsb = [big.enter_context(nc.sbuf_tensor(f"q{c}", [P, S], BF16))
           for c in range(NC)]
    ksb = [big.enter_context(nc.sbuf_tensor(f"k{c}", [P, S], BF16))
           for c in range(NC)]
    vsb = [big.enter_context(nc.sbuf_tensor(f"v{t}", [P, H * 65], BF16))
           for t in range(NT)]
    with ExitStack() as ph:
        wfull = [ph.enter_context(nc.sbuf_tensor(f"wf{c}", [P, D], BF16))
                 for c in range(NC)]
        pj = [ph.enter_context(nc.psum_tensor(f"pj{i}", [P, QW], F32))
              for i in range(4)]
        t_vones = None
        for t in range(NT):
            t_vones = sch.op("vector", (lambda t_: (lambda e: e.memset(
                vsb[t_].ap().rearrange("p (h x) -> p h x", x=65)[:, :, 64:65],
                1.0)))(t), token=(t == NT - 1))
        bank_free = [None] * 4

        wslab = [ph.enter_context(nc.sbuf_tensor(f"wsl{i}", [P, D], F32))
                 for i in range(2)]
        wsl2 = ph.enter_context(nc.sbuf_tensor("wsl2", [P, D], F32))
        wqb = [ph.enter_context(nc.sbuf_tensor(f"wqb{i}", [P, D], BF16))
               for i in range(2)]
        Ssc = ph.enter_context(nc.sbuf_tensor("Ssc", [P, 32], F32))
        Sd = ph.enter_context(nc.sbuf_tensor("Sd", [P, 32], F32))
        Rsc = ph.enter_context(nc.sbuf_tensor("Rsc", [P, 32], F32))
        Rs2 = ph.enter_context(nc.sbuf_tensor("Rs2", [P, 32], F32))
        RsB = ph.enter_context(nc.sbuf_tensor("RsB", [P, 32], F32))
        tstg = [ph.enter_context(nc.sbuf_tensor(f"tstg{i}", [P, 512], BF16))
                for i in range(2)]
        psT = [ph.enter_context(nc.psum_tensor(f"psT{i}", [P, QW], BF16))
               for i in range(2)]
        deq_spills = []

        def b3(t):
            return t.ap()[:, 0:D].rearrange("p (a b) -> p a b", b=32)

        def deq_gen():
            t_q_hist = [None, None]
            t_psT_free = [None, None]
            t_stg_free = [None, None]
            t_tr_bypar = [None, None]
            units = []
            for src_d, dst_hbm, key, nsl, ncof in (
                    (wu_d, wuT_d, "wuT", NI, 1),
                    (wg_d, wgT_d, "wgT", NI, 1),
                    (wd_d, wdT_d, "wdT", NC, 3)):
                for widx in range(nsl):
                    for coff in range(ncof):
                        units.append((src_d, dst_hbm, key, widx, coff,
                                      ncof))
            prev = None
            for ui, unit in enumerate(units + [None, None]):
                if unit is not None:
                    src_d, dst_hbm, key, widx, coff, ncof = unit
                    sl = wslab[ui % 2]
                    w = ([t_q_hist[ui % 2]]
                         if t_q_hist[ui % 2] is not None else [])
                    t_l = sch.dma("sync", sl[:],
                                  src_d[widx * P:(widx + 1) * P,
                                        coff * D:(coff + 1) * D], waits=w)
                    sch.op("vector", (lambda s_: (lambda e: e.tensor_reduce(
                        Ssc[:], b3(s_), AX.X, ALU.max,
                        apply_absolute_value=True)))(sl), waits=[t_l])
                    sch.op("vector", lambda e: e.drain())
                    t_ss = sch.op("vector", lambda e: e.tensor_scalar(
                        Sd[:], Ssc[:], 1.0 / 7.0, 1e-12,
                        ALU.mult, ALU.max), token=True)
                    sch.op("scalar", lambda e: e.activation(
                        Rs2[:], Sd[:], AF.Ln, bias=0.0, scale=1.0),
                        waits=[t_ss])
                    t_r0 = sch.op("scalar", lambda e: e.activation(
                        Rsc[:], Rs2[:], AF.Exp, bias=0.0, scale=-1.0),
                        token=True)
                    sch.op("vector", lambda e: e.tensor_tensor(
                        Rs2[:], Sd[:], Rsc[:], ALU.mult), waits=[t_r0])
                    sch.op("vector", lambda e: e.drain())
                    sch.op("vector", lambda e: e.tensor_scalar(
                        RsB[:], Rs2[:], -1.0, 2.0, ALU.mult, ALU.add))
                    sch.op("vector", lambda e: e.drain())
                    sch.op("vector", lambda e: e.tensor_tensor(
                        Rsc[:], Rsc[:], RsB[:], ALU.mult))
                    sch.op("vector", lambda e: e.drain())
                    sch.op("vector", (lambda s_: (lambda e: e.tensor_tensor(
                        b3(wsl2), b3(s_),
                        Rsc[:, :, None].broadcast_to([P, 32, 32]),
                        ALU.mult)))(sl))
                    sch.op("vector", (lambda s_: (lambda e: e.tensor_scalar(
                        s_[:], wsl2[:], MAGIC, MAGIC,
                        ALU.add, ALU.subtract)))(sl))
                    t_q = sch.op("vector", (lambda s_, qb_: (lambda e:
                                 e.tensor_tensor(
                        b3(qb_), b3(s_),
                        Sd[:, :, None].broadcast_to([P, 32, 32]),
                        ALU.mult)))(sl, wqb[ui % 2]),
                        waits=[t_tr_bypar[ui % 2]]
                        if t_tr_bypar[ui % 2] is not None else (),
                        token=True)
                    t_q_hist[ui % 2] = t_q
                    nxt = (ui, t_q, dst_hbm, key,
                           (widx * ncof + coff) * 2)
                else:
                    nxt = None
                if prev is not None:
                    pui, t_q, dst_hbm, key, bib = prev
                    for jg in range(2):
                        pT = psT[jg]
                        stg = tstg[jg]
                        for jj in range(4):
                            j = 4 * jg + jj
                            w = [t_q] if jj == 0 else []
                            if jj == 0 and t_psT_free[jg] is not None:
                                w.append(t_psT_free[jg])
                            sch.op("tensor", (lambda j_, jj_, pT_, qb_:
                                              (lambda e: e.transpose(
                                pT_[:, jj_ * P:(jj_ + 1) * P],
                                qb_[:, j_ * P:(j_ + 1) * P], idn[:])))(
                                    j, jj, pT, wqb[pui % 2]),
                                waits=w + ([t_id] if jg == 0 and pui == 0
                                           else []))
                        t_tr = sch.op("tensor", lambda e: e.drain(),
                                      token=True)
                        t_tr_bypar[pui % 2] = t_tr
                        wev = [t_tr] + ([t_stg_free[jg]]
                                        if t_stg_free[jg] is not None
                                        else [])
                        t_ev = sch.op("scalar", (lambda pT_, stg_:
                                      (lambda e: e.copy(
                            stg_[:], pT_[:])))(pT, stg),
                            waits=wev, token=True)
                        t_psT_free[jg] = t_ev
                        bi = bib + jg
                        t_stg_free[jg] = sch.dma(
                            "sync", dst_hbm[bi * P:(bi + 1) * P, :],
                            stg[:], waits=[t_ev])
                        spill.setdefault(key, []).append(t_stg_free[jg])
                        deq_spills.append(t_stg_free[jg])
                prev = nxt
                yield

        deq = deq_gen()
        deq_n = [0, 0]

        def pace():
            deq_n[0] += 1
            tgt = min(72, deq_n[0] * 72 // 96 + 1)
            while deq_n[1] < tgt:
                if next(deq, "END") == "END":
                    break
                deq_n[1] += 1

        for mi, (wd, dst) in enumerate(((wqT_d, qsb), (wkT_d, ksb))):
            wl = ([t for t in bank_free if t is not None] if mi else [])
            wtoks = [sch.dma("gpsimd", wfull[c][:], wd[c * P:(c + 1) * P, :],
                             waits=wl if c == 0 else ())
                     for c in range(NC)]
            ti = 0
            for oc in range(NC):
                for n in range(NQ):
                    bk = ti % 4
                    ti += 1
                    pt = pj[bk]
                    for kc in range(NC):
                        w = []
                        if kc == 0:
                            w = list(wtoks)
                            if bank_free[bk] is not None:
                                w.append(bank_free[bk])
                        sch.op("tensor", (lambda kc_, oc_, n_, pt_, st_, sp_:
                                          (lambda e: e.matmul(
                            pt_[:], wfull[kc_][:, oc_ * P:(oc_ + 1) * P],
                            A[kc_][:, n_ * QW:(n_ + 1) * QW],
                            start=st_, stop=sp_)))(
                                kc, oc, n, pt, kc == 0, kc == NC - 1),
                            waits=w)
                    t_mm = sch.op("tensor", lambda e: e.drain(),
                                  token=True)
                    if mi == 0:
                        bank_free[bk] = sch.op(
                            "vector", (lambda d_, oc_, n_, pt_: (lambda e:
                                       e.tensor_scalar(
                                d_[oc_][:, n_ * QW:(n_ + 1) * QW], pt_[:],
                                ATT_SCALE, None, ALU.mult)))(dst, oc, n, pt),
                            waits=[t_mm], token=True)
                    else:
                        bank_free[bk] = sch.op(
                            "scalar", (lambda d_, oc_, n_, pt_: (lambda e:
                                       e.copy(
                                d_[oc_][:, n_ * QW:(n_ + 1) * QW], pt_[:])))(
                                    dst, oc, n, pt),
                            waits=[t_mm], token=True)
                    pace()
        wtoks = [sch.dma("gpsimd", wfull[c][:], wvT_d[c * P:(c + 1) * P, :],
                         waits=[bank_free[i] for i in range(4)] if c == 0
                         else ())
                 for c in range(NC)]
        ti = 0
        for t in range(NT):
            for half in range(2):
                bk = ti % 4
                ti += 1
                pt = pj[bk]
                for kc in range(NC):
                    w = []
                    if kc == 0:
                        w = list(wtoks)
                        if bank_free[bk] is not None:
                            w.append(bank_free[bk])
                    sch.op("tensor", (lambda kc_, t_, h_, pt_, st_, sp_:
                                      (lambda e: e.matmul(
                        pt_[:], A[kc_][:, t_ * P:(t_ + 1) * P],
                        wfull[kc_][:, h_ * QW:(h_ + 1) * QW],
                        start=st_, stop=sp_)))(
                            kc, t, half, pt, kc == 0, kc == NC - 1),
                        waits=w)
                t_mm = sch.op("tensor", lambda e: e.drain(), token=True)
                bank_free[bk] = sch.op(
                    "scalar", (lambda t_, h_, pt_: (lambda e: e.copy(
                        vsb[t_].ap().rearrange("p (hh x) -> p hh x", x=65)[
                            :, h_ * 8:(h_ + 1) * 8, 0:64],
                        pt_.ap().rearrange("p (hh x) -> p hh x", x=64))))(
                            t, half, pt),
                    waits=[t_mm, t_vones], token=True)
                pace()
        while next(deq, "END") != "END":
            pass
        mxs = {}
        for tk in deq_spills:
            mxs[tk.sem] = max(mxs.get(tk.sem, 0), tk.val)
        sch.op("sync", None, waits=[Tok(s_, v_) for s_, v_ in mxs.items()])
        sch.barrier()

    with ExitStack() as ph:
        ttsb = [ph.enter_context(nc.sbuf_tensor(f"tts{i}", [P, 3968], BF16))
                for i in range(4)]
        esb = [ph.enter_context(nc.sbuf_tensor(f"esb{i}", [P, 2 * QW], BF16))
               for i in range(3)]
        rrep = [ph.enter_context(nc.sbuf_tensor(f"rrep{i}", [HD, QW], F32))
                for i in range(2)]
        oscr = [ph.enter_context(nc.sbuf_tensor(f"oscr{i}", [HD, QW], F32))
                for i in range(2)]
        obias = [ph.enter_context(nc.sbuf_tensor(f"obias{i}", [HD, QW], F32))
                 for i in range(2)]
        ostg = ph.enter_context(nc.sbuf_tensor("ostg", [HD, QW], BF16))
        psS = [ph.enter_context(nc.psum_tensor(f"psS{i}", [P, 2 * QW],
                                               F32)) for i in range(2)]
        psUE = [ph.enter_context(nc.psum_tensor(f"psUE{h}", [P, QW], F32))
                for h in range(2)]
        psUB = [ph.enter_context(nc.psum_tensor(f"psUB{h}", [P, QW], F32))
                for h in range(2)]

        qzb = [[ph.enter_context(nc.sbuf_tensor(f"qz{i}{j}", [P, QW], BF16))
                for j in range(2)] for i in range(2)]
        t_qzz = sch.op("vector", lambda e: (
            e.memset(qzb[0][0][HD:P, :], 0.0)), token=False)
        sch.op("vector", lambda e: e.memset(qzb[0][1][0:HD, :], 0.0))
        sch.op("vector", lambda e: e.memset(qzb[1][0][HD:P, :], 0.0))
        t_qzz = sch.op("vector", lambda e: e.memset(qzb[1][1][0:HD, :], 0.0),
                       token=True)
        qz_free = [None, None]
        strip_tok = [None, None, None, None]
        strip_tok[0] = sch.dma("gpsimd", ttsb[0][:], tt_d[0])
        strip_tok[1] = sch.dma("gpsimd", ttsb[1][:], tt_d[1])
        pair_done = [None, None]
        t_exp = [None, None]
        t_av = [None] * 4
        ue_free = [None, None]
        ub_free = [None, None]
        oscr_free = [None, None]
        obias_free = [None, None]
        rrep_free = [None, None]
        t_ostg_dma = None
        rrow_free = [None] * 4
        fscr_free = [None] * 4

        for pr in range(8):
            par = pr % 2
            cur = [ttsb[par * 2], ttsb[par * 2 + 1]]
            cur_toks = [strip_tok[par * 2], strip_tok[par * 2 + 1]]
            if pr < 7:
                npar = (pr + 1) % 2
                w = [pair_done[npar]] if pair_done[npar] is not None else []
                strip_tok[npar * 2] = sch.dma(
                    "gpsimd", ttsb[npar * 2][:], tt_d[2 * pr + 2], waits=w)
                strip_tok[npar * 2 + 1] = sch.dma(
                    "gpsimd", ttsb[npar * 2 + 1][:], tt_d[2 * pr + 3])
            def emit_qz(pq, qt):
                qp = qt % 2
                w = [qz_free[qp]] if qz_free[qp] is not None else [t_qzz]
                return [
                    sch.op("scalar", (lambda pr_, qt_, qp_: (lambda e:
                           e.copy(
                        qzb[qp_][0][0:HD, :],
                        qsb[pr_][0:HD, qt_ * QW:(qt_ + 1) * QW])))(
                            pq, qt, qp),
                        waits=w, token=True),
                    sch.op("scalar", (lambda pr_, qt_, qp_: (lambda e:
                           e.copy(
                        qzb[qp_][1][HD:P, :],
                        qsb[pr_][HD:P, qt_ * QW:(qt_ + 1) * QW])))(
                            pq, qt, qp),
                        token=True)]

            if pr == 0:
                t_qz_next = emit_qz(0, 0)
            for qt in range(NQ):
                def emit_av(kc, t_e, want_token=False):
                    eb = esb[kc % 3]
                    for hh in range(2):
                        w = [t_e] if hh == 0 else []
                        if kc == 0:
                            if ue_free[hh] is not None:
                                w.append(ue_free[hh])
                            if ub_free[hh] is not None:
                                w.append(ub_free[hh])
                            w.extend(cur_toks)
                        col = (2 * pr + hh) * 65
                        ms = qt * QW - kc * P + 1920
                        sch.op("tensor", (lambda kc_, hh_, st_, sp_, b_,
                                          col_: (lambda e: e.matmul(
                            psUE[hh_][0:65, :],
                            vsb[kc_][:, col_:col_ + 65],
                            b_[:, hh_ * QW:(hh_ + 1) * QW],
                            start=st_, stop=sp_)))(
                                kc, hh, kc == 0, kc == 15, eb, col),
                            waits=w)
                        sch.op("tensor", (lambda kc_, hh_, st_, sp_, ms_, c_,
                                          col_: (lambda e: e.matmul(
                            psUB[hh_][0:65, :],
                            vsb[kc_][:, col_:col_ + 65],
                            c_[:, ms_:ms_ + QW],
                            start=st_, stop=sp_)))(
                                kc, hh, kc == 0, kc == 15, ms, cur[hh], col),
                            waits=[])
                    if want_token:
                        return sch.op("tensor", lambda e: e.drain(),
                                      token=True)
                    return None

                qp = qt % 2
                t_qz = t_qz_next
                pend = []
                for kc in range(16):
                    p2 = kc % 2
                    ps = psS[p2]
                    for hh in range(2):
                        w = []
                        if hh == 0 and t_exp[p2] is not None:
                            w.append(t_exp[p2])
                        if kc == 0 and hh == 0:
                            w.extend(t_qz)
                        sch.op("tensor", (lambda pr_, kc_, hh_, qz_,
                                          ps_: (lambda e: e.matmul(
                            ps_[:, hh_ * QW:(hh_ + 1) * QW],
                            ksb[pr_][:, kc_ * P:(kc_ + 1) * P],
                            qz_[:, 0:QW],
                            start=True, stop=True)))(pr, kc, hh,
                                                     qzb[qp][hh], ps),
                            waits=w)
                    t_qk = sch.op("tensor", lambda e: e.drain(),
                                  token=True)
                    qz_free[qp] = t_qk
                    if pend:
                        kcp, tep = pend.pop(0)
                        emit_av(kcp, tep)
                    w = [t_qk]
                    t_exp[p2] = sch.op(
                        "scalar", (lambda b_, ps_: (lambda e: e.activation(
                            b_[:], ps_[:], AF.Exp, bias=0.0,
                            scale=1.0)))(esb[kc % 3], ps),
                        waits=w, token=True)
                    pend.append((kc, t_exp[p2]))
                if qt < NQ - 1:
                    t_qz_next = emit_qz(pr, qt + 1)
                elif pr < 7:
                    t_qz_next = emit_qz(pr + 1, 0)
                t_pv = emit_av(*pend.pop(0), want_token=True)

                for hh in range(2):
                    slot = (qt * 2 + hh) % 4
                    wln = [t_pv]
                    if rrow_free[slot] is not None:
                        wln.append(rrow_free[slot])
                    sch.op("scalar", (lambda hh_, sl: (lambda e: e.activation(
                        rowbank[64:65, sl * QW:(sl + 1) * QW], psUE[hh_][64:65, :],
                        AF.Ln, bias=0.0, scale=1.0)))(hh, slot),
                        waits=wln)
                    w = [oscr_free[hh]] if oscr_free[hh] is not None else []
                    ue_free[hh] = sch.op(
                        "scalar", (lambda hh_: (lambda e: e.copy(
                            oscr[hh_][:], psUE[hh_][0:64, :])))(hh),
                        waits=w, token=True)
                    w = [obias_free[hh]] if obias_free[hh] is not None else []
                    ub_free[hh] = sch.op(
                        "scalar", (lambda hh_: (lambda e: e.copy(
                            obias[hh_][:], psUB[hh_][0:64, :])))(hh),
                        waits=w, token=True)
                    t_r = sch.op("scalar", (lambda sl: (lambda e: e.activation(
                        rowbank[64:65, sl * QW:(sl + 1) * QW],
                        rowbank[64:65, sl * QW:(sl + 1) * QW],
                        AF.Exp, bias=0.0, scale=-1.0)))(slot), token=True)
                    t_w = sch.dma("sync", rrow_d.ap()[slot:slot+1], rowbank[64:65, slot * QW:(slot + 1) * QW],
                        waits=[t_r])
                    rrow_free[slot] = t_w
                    wb = [t_w]
                    if rrep_free[hh] is not None:
                        wb.append(rrep_free[hh])
                    t_rr = sch.dma("sync", rrep[hh][:],
                                   rrow_d.ap()[slot:slot+1]
                                   .partition_broadcast(HD), waits=wb)
                    fslot = fscr[0:64, slot * QW:(slot + 1) * QW]
                    w = [t_rr, ue_free[hh]]
                    if fscr_free[slot] is not None:
                        w.append(fscr_free[slot])
                    t_mul = sch.op(
                        "vector", (lambda hh_, fs_: (lambda e: e.tensor_tensor(
                            fs_, oscr[hh_][:], rrep[hh_][:],
                            ALU.mult)))(hh, fslot),
                        waits=w, token=True)
                    rrep_free[hh] = t_mul
                    oscr_free[hh] = t_mul
                    if hh == 0:
                        t_o = sch.op(
                            "vector", (lambda pr_, qt_, fs_: (lambda e:
                                       e.tensor_tensor(
                                A[pr_][0:64, qt_ * QW:(qt_ + 1) * QW],
                                fs_, obias[0][:], ALU.add)))(
                                    pr, qt, fslot),
                            waits=[ub_free[0]], token=True)
                    else:
                        w2 = [ub_free[1], t_ostg_dma] if t_ostg_dma is not None \
                            else [ub_free[1]]
                        t_o = sch.op(
                            "vector", (lambda fs_: (lambda e: e.tensor_tensor(
                                ostg[:], fs_, obias[1][:],
                                ALU.add)))(fslot),
                            waits=w2, token=True)
                        t_ostg_dma = sch.dma(
                            "sync", A[pr][64:128, qt * QW:(qt + 1) * QW],
                            ostg[:], waits=[t_o])
                    obias_free[hh] = t_o
                    fscr_free[slot] = t_o
            pair_done[par] = sch.op("tensor", lambda e: e.drain(),
                                    token=True)
        t_attn_out = t_ostg_dma
        sch.barrier()
    big.close()
    late = ExitStack()
    X2 = [late.enter_context(nc.sbuf_tensor(f"X2{c}", [P, S], BF16))
          for c in range(NC)]
    x2_spill = []

    with ExitStack() as ph:
        abrep, t_ab, _ = ln_stats(
            ph, lambda c, r: (A[c][:], [t_attn_out]))
        t_prev = None
        for c in range(NC):
            sch.op("vector", (lambda c_, ab_: (lambda e: e.tensor_tensor(
                fscr[:], A[c_][:], ab_[:, 0:S], ALU.mult)))(c, abrep),
                waits=[t_ab] if c == 0 else ())
            sch.op("vector", (lambda ab_: (lambda e: e.tensor_tensor(
                fscr[:], fscr[:], ab_[:, S:2 * S], ALU.add)))(abrep))
            t_prev = sch.op("vector", (lambda c_: (lambda e: e.tensor_scalar(
                X2[c_][:], fscr[:], gam[:, c_:c_ + 1], bet[:, c_:c_ + 1],
                ALU.mult, ALU.add)))(c),
                waits=gb_toks if c == 0 else (), token=True)
        sch.barrier()
    with ExitStack() as ph:
        abrep, t_ab, _ = ln_stats(ph, lambda c, r: (X2[c][:], []))
        xb0 = ph.enter_context(nc.sbuf_tensor("xb0", [P, S], BF16))
        xb1 = ph.enter_context(nc.sbuf_tensor("xb1", [P, S], BF16))
        prev = [None, None]
        for c in range(NC):
            b = [xb0, xb1][c % 2]
            w = [prev[c % 2]] if prev[c % 2] is not None else []
            t_l = sch.dma("gpsimd", b[:], xT_d[c * P:(c + 1) * P, :], waits=w)
            sch.op("vector", (lambda c_, ab_: (lambda e: e.tensor_tensor(
                fscr[:], X2[c_][:], ab_[:, 0:S], ALU.mult)))(c, abrep),
                waits=[t_ab] if c == 0 else ())
            sch.op("vector", (lambda ab_: (lambda e: e.tensor_tensor(
                fscr[:], fscr[:], ab_[:, S:2 * S], ALU.add)))(abrep))
            prev[c % 2] = sch.op("vector", (lambda c_, b_: (lambda e:
                                 e.tensor_tensor(
                X2[c_][:], fscr[:], b_[:], ALU.add)))(c, b),
                waits=[t_l], token=True)
        sch.barrier()

    with ExitStack() as ph:
        abrep, t_ab, _ = ln_stats(ph, lambda c, r: (X2[c][:], []))
        t_mz = sch.op("vector", lambda e: e.memset(lmax[:, 0:2], 0.0),
                      token=True)
        t_prev = t_mz
        for c in range(NC):
            sch.op("vector", (lambda c_, ab_: (lambda e: e.tensor_tensor(
                fscr[:], X2[c_][:], ab_[:, 0:S], ALU.mult)))(c, abrep),
                waits=[t_ab, t_mz] if c == 0 else ())
            sch.op("vector", (lambda c_, ab_: (lambda e: e.tensor_tensor(
                A[c_][:], fscr[:], ab_[:, S:2 * S], ALU.add)))(c, abrep))
            sch.op("vector", (lambda c_: (lambda e: e.tensor_reduce(
                lmax[:, 2:3], A[c_][:], AX.X, ALU.max,
                apply_absolute_value=True)))(c))
            sch.op("vector", lambda e: e.drain())
            t_prev = sch.op("vector", lambda e: e.tensor_tensor(
                lmax[:, 0:1], lmax[:, 0:1], lmax[:, 2:3], ALU.max),
                token=True)
            x2_spill.append(sch.dma(
                "sync", x2_d.ap()[c * P:(c + 1) * P, :], X2[c][:],
                waits=[t_prev]))
        t_g = fold_partitions_maxes(t_prev, 0, [0])
        t_s1 = make_scale(t_g, 0, 10)
        if dbg2_d is not None:
            sch.dma("sync", dbg2_d[:, 0:16], gsc[:, 0:16], waits=[t_s1])
        for c in range(NC):
            tq_ = qdq_pair(A[c][:], A[c][:], 0, [t_s1] if c == 0 else [],
                           token=True)
            if xq_d is not None:
                sch.dma("sync", xq_d[c * P:(c + 1) * P, :], A[c][:],
                        waits=[tq_])
        sch.op("sync", None, waits=x2_spill)
        sch.barrier()
    late.close()

    big2 = ExitStack()
    wuT = big2.enter_context(nc.sbuf_tensor("wuT", [P, 48 * 512], BF16))
    wgT = big2.enter_context(nc.sbuf_tensor("wgT", [P, 48 * 512], BF16))

    def ug_stat(wT, kc, oc):
        off = oc * 1024 + (kc // 4) * 512 + (kc % 4) * P
        return wT[:, off:off + P]
    with ExitStack() as ph:
        evb = [ph.enter_context(nc.sbuf_tensor(f"ev{i}", [P, S], BF16))
               for i in range(2)]
        pj2 = [ph.enter_context(nc.psum_tensor(f"pj2{i}", [P, QW], F32))
               for i in range(4)]
        b0s = [ph.enter_context(nc.sbuf_tensor(f"s1b{i}", [P, S], BF16))
               for i in range(2)]
        fscr2 = ph.enter_context(nc.sbuf_tensor("fscr2", [P, S], F32))
        fscrC = ph.enter_context(nc.sbuf_tensor("fscrC", [P, S], F32))
        tg2b = ph.enter_context(nc.sbuf_tensor("tg2b", [P, S], BF16))
        t_mz = sch.op("vector", lambda e: e.memset(lmax[:], 0.0),
                      token=True)
        wg_toks = [sch.dma("sync", wgT[:, b * 512:(b + 1) * 512],
                           wgT_d.ap()[b * P:(b + 1) * P, :],
                           waits=spill["wgT"] if b == 0 else ())
                   for b in range(48)]
        wu_toks = [sch.dma("sync", wuT[:, b * 512:(b + 1) * 512],
                           wuT_d.ap()[b * P:(b + 1) * P, :],
                           waits=spill["wuT"] if b == 0 else ())
                   for b in range(48)]
        bank_free = [None] * 4
        ev_free = [None, None]

        def mm_group(wT, oc, first_toks, sp_d, key, acol, scol):
            eb = evb[oc % 2]
            for n in range(NQ):
                pt = pj2[n]
                for kc in range(NC):
                    w = []
                    if kc == 0 and bank_free[n] is not None:
                        w.append(bank_free[n])
                    if first_toks is not None and n == 0 and kc == 0:
                        w.extend(first_toks)
                    sch.op("tensor", (lambda wT_, kc_, oc_, n_, pt_, st_,
                                      sp_: (lambda e: e.matmul(
                        pt_[:], ug_stat(wT_, kc_, oc_),
                        A[kc_][:, n_ * QW:(n_ + 1) * QW],
                        start=st_, stop=sp_)))(
                            wT, kc, oc, n, pt, kc == 0, kc == NC - 1),
                        waits=w)
            t_mm = sch.op("tensor", lambda e: e.drain(), token=True)
            t_ev = None
            for n in range(NQ):
                w = [t_mm] if n == 0 else []
                if n == 0 and ev_free[oc % 2] is not None:
                    w.append(ev_free[oc % 2])
                t_ev = sch.op("scalar", (lambda b_, n_, pt_: (lambda e:
                                         e.copy(
                    b_[:, n_ * QW:(n_ + 1) * QW], pt_[:])))(
                        eb, n, pj2[n]),
                    waits=w, token=True)
                bank_free[n] = t_ev
            sch.op("vector", (lambda b_, sc: (lambda e: e.tensor_reduce(
                lmax[:, sc:sc + 1], b_[:], AX.X, ALU.max,
                apply_absolute_value=True)))(eb, scol),
                waits=[t_ev, t_mz])
            sch.op("vector", lambda e: e.drain())
            t_mx = sch.op("vector", (lambda ac, sc: (lambda e:
                          e.tensor_tensor(
                lmax[:, ac:ac + 1], lmax[:, ac:ac + 1], lmax[:, sc:sc + 1],
                ALU.max)))(acol, scol), token=True)
            ev_free[oc % 2] = sch.dma(
                "sync", sp_d.ap()[oc * P:(oc + 1) * P, :], eb[:],
                waits=[t_ev, t_mx])
            spill[(key, oc)] = ev_free[oc % 2]
            return t_mx, t_mm

        for oc in range(NI):
            t_mx_g, _ = mm_group(wgT, oc, wg_toks if oc == 0 else None,
                                 gt_d, "gt", 0, 2)
        t_g3 = fold_partitions_maxes(t_mx_g, 1, [0])
        t_s3 = make_scale(t_g3, 2, 10)

        prev = [None, None]
        t_tg_free = None
        for oc in range(NI):
            t_mx_u, _ = mm_group(wuT, oc, wu_toks if oc == 0 else None,
                                 up_d, "up", 1, 2)
            b0 = b0s[oc % 2]
            w = [prev[oc % 2]] if prev[oc % 2] is not None else []
            w.append(spill[("gt", oc)])
            t_l = sch.dma("sync", b0[:], gt_d.ap()[oc * P:(oc + 1) * P, :],
                          waits=w)
            sch.op("vector", (lambda b_: (lambda e: e.tensor_scalar(
                fscrC[:], b_[:], gsc[:, 5:6], MAGIC, ALU.mult, ALU.add)))(b0),
                waits=[t_l, t_s3] if oc == 0 else [t_l])
            prev[oc % 2] = sch.op("vector", lambda e: e.tensor_scalar(
                fscr[:], fscrC[:], MAGIC, gsc[:, 4:5], ALU.subtract, ALU.mult),
                token=True)
            t_gq = prev[oc % 2]
            t_sig = sch.op("scalar", lambda e: e.activation(
                fscr2[:], fscr[:], AF.Sigmoid, bias=0.0, scale=1.0),
                waits=[t_gq], token=True)
            sch.op("vector", lambda e: e.tensor_scalar(
                fscrC[:], fscr2[:], 65536.0, MAGIC, ALU.mult, ALU.add),
                waits=[t_sig])
            sch.op("vector", lambda e: e.tensor_scalar(
                fscr2[:], fscrC[:], MAGIC, 65535.0, ALU.subtract, ALU.min))
            w = [t_tg_free] if t_tg_free is not None else []
            t_t2 = sch.op("vector", lambda e: e.tensor_tensor(
                tg2b[:], fscr[:], fscr2[:], ALU.mult), waits=w, token=True)
            sch.op("vector", lambda e: e.tensor_reduce(
                lmax[:, 3:4], tg2b[:], AX.X, ALU.max,
                apply_absolute_value=True))
            sch.op("vector", lambda e: e.drain())
            t_mx_s1 = sch.op("vector", lambda e: e.tensor_tensor(
                lmax[:, 0:1], lmax[:, 0:1], lmax[:, 3:4], ALU.max),
                token=True)
            t_tg_free = sch.dma("sync", or_d.ap()[oc * P:(oc + 1) * P, :],
                                tg2b[:], waits=[t_t2, t_mx_s1])
            spill[("or", oc)] = t_tg_free
        t_g24 = fold_partitions_maxes(t_mx_s1, 2, [1, 0])
        t_s2 = make_scale(t_g24, 1, 10)
        t_s4 = make_scale(t_g24, 3, 11, pre=1.0 / 65536.0)
        sch.barrier()
    big2.close()

    with ExitStack() as ph:
        wdT = ph.enter_context(nc.sbuf_tensor("wdT", [P, 48 * 512], BF16))

        def wd_stat(icx, oc):
            off = ((oc * 3 + icx // 8) * 2 + (icx % 8) // 4) * 512 \
                + (icx % 4) * P
            return wdT[:, off:off + P]
        b0s = [ph.enter_context(nc.sbuf_tensor(f"s2b{i}", [P, S], BF16))
               for i in range(2)]
        upb = [ph.enter_context(nc.sbuf_tensor(f"upb{i}", [P, S], BF16))
               for i in range(2)]
        fscr2 = ph.enter_context(nc.sbuf_tensor("fscr2b", [P, S], F32))
        fscrC = ph.enter_context(nc.sbuf_tensor("fscrCb", [P, S], F32))
        ph9_out = ph.enter_context(nc.sbuf_tensor("ph9o", [P, S], BF16))
        wd_toks = [sch.dma("sync", wdT[:, b * 512:(b + 1) * 512],
                           wdT_d.ap()[b * P:(b + 1) * P, :],
                           waits=spill["wdT"] if b == 0 else ())
                   for b in range(48)]
        prev = [None, None]
        t_up_free = [None, None]
        t_out_free = None
        for oc in range(NI):
            b0 = b0s[oc % 2]
            w = [prev[oc % 2]] if prev[oc % 2] is not None else []
            w.append(spill[("or", oc)])
            t_l = sch.dma("sync", b0[:], or_d.ap()[oc * P:(oc + 1) * P, :],
                          waits=w)
            w = [t_up_free[oc % 2]] if t_up_free[oc % 2] is not None else []
            w.append(spill[("up", oc)])
            t_lu = sch.dma("sync", upb[oc % 2][:],
                           up_d.ap()[oc * P:(oc + 1) * P, :], waits=w)
            sch.op("vector", (lambda b_: (lambda e: e.tensor_scalar(
                fscrC[:], b_[:], gsc[:, 7:8], MAGIC, ALU.mult, ALU.add)))(b0),
                waits=([t_l, t_s4, t_s2] if oc == 0 else [t_l]))
            prev[oc % 2] = sch.op("vector", lambda e: e.tensor_scalar(
                fscr[:], fscrC[:], MAGIC, gsc[:, 6:7], ALU.subtract, ALU.mult),
                token=True)
            sch.op("vector", (lambda u_: (lambda e: e.tensor_scalar(
                fscrC[:], u_[:], gsc[:, 3:4], MAGIC, ALU.mult,
                ALU.add)))(upb[oc % 2]), waits=[t_lu])
            t_up_free[oc % 2] = sch.op("vector", lambda e: e.tensor_scalar(
                fscr2[:], fscrC[:], MAGIC, gsc[:, 2:3],
                ALU.subtract, ALU.mult), token=True)
            w = [t_out_free] if t_out_free is not None else []
            t_o = sch.op("vector", lambda e: e.tensor_tensor(
                ph9_out[:], fscr[:], fscr2[:], ALU.mult), waits=w, token=True)
            sch.op("vector", lambda e: e.tensor_reduce(
                lmax[:, 2:3], ph9_out[:], AX.X, ALU.max,
                apply_absolute_value=True))
            sch.op("vector", lambda e: e.drain())
            t_mx = sch.op("vector", lambda e: e.tensor_tensor(
                lmax[:, 5:6], lmax[:, 5:6], lmax[:, 2:3], ALU.max),
                token=True)
            t_out_free = sch.dma("sync", gt_d.ap()[oc * P:(oc + 1) * P, :],
                                 ph9_out[:], waits=[t_o, t_mx])
            spill[("og", oc)] = t_out_free
        t_g = fold_partitions_maxes(t_mx, 3, [5])
        t_s5 = make_scale(t_g, 4, 10)
        ob = [ph.enter_context(nc.sbuf_tensor(f"ob{i}", [P, QW], BF16))
              for i in range(NI)]
        orb = [ph.enter_context(nc.sbuf_tensor(f"orb{i}", [P, QW], BF16))
               for i in range(2)]
        pj3 = [ph.enter_context(nc.psum_tensor(f"pj3{i}", [P, QW], F32))
               for i in range(NC)]
        bank_free = [None] * NC
        ob_free = [None] * NI

        prev = [None, None]
        for n in range(NQ):
            for icx in range(NI):
                b0 = orb[icx % 2]
                w = [prev[icx % 2]] if prev[icx % 2] is not None else []
                w.append(spill[("og", icx)])
                t_l = sch.dma("sync", b0[:, 0:QW],
                              gt_d.ap()[icx * P:(icx + 1) * P,
                                        n * QW:(n + 1) * QW], waits=w)
                w = [t_l] + ([t_s5] if (n == 0 and icx == 0) else [])
                if ob_free[icx] is not None:
                    w.append(ob_free[icx])
                sch.op("vector", (lambda b_: (lambda e: e.tensor_scalar(
                    fscr[:, 0:QW], b_[:, 0:QW], gsc[:, 9:10], MAGIC,
                    ALU.mult, ALU.add)))(b0), waits=w)
                t_ov = sch.op(
                    "vector", (lambda icx_: (lambda e: e.tensor_scalar(
                        ob[icx_][:], fscr[:, 0:QW], MAGIC, gsc[:, 8:9],
                        ALU.subtract, ALU.mult)))(icx), token=True)
                prev[icx % 2] = t_ov
                for oc in range(NC):
                    w = []
                    if oc == 0:
                        w = [t_ov] + list(
                            [wd_toks[-1]] if (n == 0 and icx == 0) else [])
                    if icx == 0:
                        if bank_free[oc] is not None:
                            w.append(bank_free[oc])
                    sch.op("tensor", (lambda icx_, oc_, st_, sp_:
                                      (lambda e: e.matmul(
                        pj3[oc_][:], wd_stat(icx_, oc_),
                        ob[icx_][:], start=st_, stop=sp_)))(
                            icx, oc, icx == 0, icx == NI - 1),
                        waits=w)
                ob_free[icx] = sch.op("tensor", lambda e: e.drain(),
                                      token=True)
            t_mm = ob_free[NI - 1]
            for oc in range(NC):
                t_ev = sch.op("scalar", (lambda oc_, n_: (lambda e:
                              e.copy(
                    A[oc_][:, n_ * QW:(n_ + 1) * QW], pj3[oc_][:])))(oc, n),
                    waits=[t_mm] if oc == 0 else (), token=True)
                bank_free[oc] = t_ev
                spill.setdefault(("y", oc), []).append(t_ev)
        sch.barrier()

    with ExitStack() as ph:
        obuf = [ph.enter_context(nc.sbuf_tensor(f"obf{i}", [P, S], F32))
                for i in range(2)]
        xrb = [ph.enter_context(nc.sbuf_tensor(f"xr{i}", [P, S], BF16))
               for i in range(2)]

        def get_y(c, reuse):
            return A[c][:], list(spill[("y", c)])

        abrep, t_ab, cons = ln_stats(ph, get_y)
        prev = [None, None]
        oprev = [None, None]
        for c in range(NC):
            okb = obuf[c % 2]
            w = [prev[c % 2]] if prev[c % 2] is not None else []
            t_x2 = sch.dma("sync", xrb[c % 2][:],
                           x2_d.ap()[c * P:(c + 1) * P, :], waits=w)
            sch.op("vector", (lambda c_, ab_: (lambda e: e.tensor_tensor(
                fscr[:], A[c_][:], ab_[:, 0:S], ALU.mult)))(c, abrep),
                waits=[t_ab])
            sch.op("vector", (lambda ab_: (lambda e: e.tensor_tensor(
                fscr[:], fscr[:], ab_[:, S:2 * S], ALU.add)))(abrep))
            w = [oprev[c % 2], t_x2] if oprev[c % 2] is not None else [t_x2]
            prev[c % 2] = sch.op(
                "vector", (lambda c_, o_: (lambda e: e.tensor_tensor(
                    o_[:], fscr[:], xrb[c_ % 2][:], ALU.add)))(c, okb),
                waits=w, token=True)
            oprev[c % 2] = sch.dma(
                "sync", out_d[c * P:(c + 1) * P, :], obuf[c % 2][:],
                waits=[prev[c % 2]])
        if dbg_d is not None:
            sch.dma("sync", dbg_d[:, 0:12], gsc[:, 0:12])
            sch.dma("sync", dbg_d[:, 12:16], lmax[:])
        total = sch.qn["sync"]
        fin = []
        for r in range(DMA_ROT):
            n_r = (total - r + DMA_ROT - 1) // DMA_ROT if total > r else 0
            if n_r:
                fin.append(Tok(f"q_sync{r}", 16 * n_r))
        sch.op("sync", None, waits=fin)

    emit(nc, sch)
    gstack.close()
    return nc


def emit(nc, sch):
    with ExitStack() as es:
        names = ["t_" + e for e in ENGINES] + ["B", "C"]
        for q in ("sync", "gpsimd", "scalar"):
            for r in range(DMA_ROT):
                names.append(f"q_{q}{r}")
        sems = {n: es.enter_context(nc.semaphore(n)) for n in names}
        with nc.Block() as block:
            def runner(eng_name):
                def f(eng):
                    for make, waits, tok in sch.items[eng_name]:
                        for t in waits:
                            eng.wait_ge(sems[t.sem], t.val)
                        if make is None:
                            continue
                        inst = make(eng)
                        if tok is not None:
                            if tok == "C":
                                inst.then_inc(sems["C"], 1)
                            elif tok.sem.startswith("q_"):
                                inst.then_inc(sems[tok.sem], 16)
                            elif eng_name in ("vector", "scalar") and \
                                    tok.sem != "B":
                                eng.drain().then_inc(sems[tok.sem], 1)
                            else:
                                inst.then_inc(sems[tok.sem], 1)

                return f
            block.sync(runner("sync"))
            block.gpsimd(runner("gpsimd"))
            block.vector(runner("vector"))
            block.scalar(runner("scalar"))
            block.tensor(runner("tensor"))


def kernel(L0_attn_x, L0_attn_Wq, L0_attn_Wk, L0_attn_Wv,
           L0_attn_rel_bias_table, L0_attn_ln_gamma, L0_attn_ln_beta,
           L0_ffn_w_gate, L0_ffn_w_up, L0_ffn_w_down):
    x = np.asarray(L0_attn_x)
    table = np.asarray(L0_attn_rel_bias_table)
    idx = 127 + np.arange(3968)[None, :] - np.arange(128)[:, None]
    ttarr = np.ascontiguousarray(np.transpose(table[idx, :], (2, 0, 1)))
    base = {
        "wqT": np.ascontiguousarray(np.asarray(L0_attn_Wq).T),
        "wkT": np.ascontiguousarray(np.asarray(L0_attn_Wk).T),
        "wvT": np.ascontiguousarray(np.asarray(L0_attn_Wv).T),
        "tt": ttarr,
        "gam": np.ascontiguousarray(
            np.asarray(L0_attn_ln_gamma).reshape(NC, P)),
        "bet": np.ascontiguousarray(
            np.asarray(L0_attn_ln_beta).reshape(NC, P)),
        "wg": np.ascontiguousarray(np.asarray(L0_ffn_w_gate)),
        "wu": np.ascontiguousarray(np.asarray(L0_ffn_w_up)),
        "wd": np.ascontiguousarray(np.asarray(L0_ffn_w_down)),
        "ident": np.eye(P, dtype=ml_dtypes.bfloat16),
    }
    in_maps = []
    for b in range(8):
        m = dict(base)
        m["xT"] = np.ascontiguousarray(x[b].T)
        in_maps.append(m)
    nc = build_program()
    res = run_bass_kernel_spmd(nc, in_maps, core_ids=list(range(8)))
    outs = [np.asarray(r["out"]).T for r in res.results]
    return np.ascontiguousarray(np.stack(outs, axis=0))


if __name__ == "__main__":
    print("building...")
    nc = build_program()
    print("built ok")


# version 29 (final)
# speedup vs baseline: 1.0555x; 1.0124x over previous
import os
import numpy as np
import ml_dtypes
import concourse.bass as bass
import concourse.mybir as mybir
from concourse.bass_utils import run_bass_kernel_spmd
from contextlib import ExitStack

F32 = mybir.dt.float32
BF16 = mybir.dt.bfloat16
AF = mybir.ActivationFunctionType
ALU = mybir.AluOpType
AX = mybir.AxisListType

P = 128
S = 2048
D = 1024
H = 16
HD = 64
NC = D // P
NT = S // P
I3 = 3072
NI = I3 // P
NQ = 4
QW = 512
MAGIC = 3 * 2.0**22
EPS = 1e-5
QMAX = 32767.0
ATT_SCALE = 1.0 / 32.0

ENGINES = ("sync", "gpsimd", "vector", "scalar", "tensor")


class Tok:
    __slots__ = ("sem", "val")

    def __init__(self, sem, val):
        self.sem = sem
        self.val = val


DMA_ROT = 16


class Sched:
    def __init__(self):
        self.items = {e: [] for e in ENGINES}
        self.tokn = {e: 0 for e in ENGINES}
        self.qn = {"sync": 0, "gpsimd": 0, "scalar": 0}
        self.bn = 0

    def op(self, eng, make, waits=(), token=False):
        tok = None
        if token:
            self.tokn[eng] += 1
            tok = Tok("t_" + eng, self.tokn[eng])
        self.items[eng].append((make, list(waits), tok))
        return tok

    def dma(self, eng, out_ap, in_ap, waits=()):
        i = self.qn[eng]
        self.qn[eng] += 1
        sem = f"q_{eng}{i % DMA_ROT}"
        occ = i // DMA_ROT
        w = list(waits)
        if occ > 0:
            w.append(Tok(sem, 16 * occ))
        tok = Tok(sem, 16 * (occ + 1))
        mk = (lambda o, i_: (lambda e: e.dma_start(out=o, in_=i_)))(out_ap, in_ap)
        self.items[eng].append((mk, w, tok))
        return tok

    def cc(self, make, waits=()):
        self.items["gpsimd"].append((make, list(waits), "C"))
        self.ccn = getattr(self, "ccn", 0) + 1
        return Tok("C", self.ccn)

    def barrier(self, engines=ENGINES):
        self.bn += len(engines)
        tgt = self.bn
        for e in engines:
            self.items[e].append((lambda en: en.drain(), [], Tok("B", tgt)))
        for e in engines:
            self.items[e].append((None, [Tok("B", tgt)], None))


def build_program():
    nc = bass.Bass("TRN2", target_bir_lowering=False,
                   detect_race_conditions=False)

    xT_d = nc.dram_tensor("xT", [D, S], F32, kind="ExternalInput").ap()
    wqT_d = nc.dram_tensor("wqT", [D, D], F32, kind="ExternalInput").ap()
    wkT_d = nc.dram_tensor("wkT", [D, D], F32, kind="ExternalInput").ap()
    wvT_d = nc.dram_tensor("wvT", [D, D], F32, kind="ExternalInput").ap()
    tt_d = nc.dram_tensor("tt", [H, P, 3968], F32, kind="ExternalInput").ap()
    gam_d = nc.dram_tensor("gam", [NC, P], F32, kind="ExternalInput").ap()
    bet_d = nc.dram_tensor("bet", [NC, P], F32, kind="ExternalInput").ap()
    wg_d = nc.dram_tensor("wg", [I3, D], F32, kind="ExternalInput").ap()
    wu_d = nc.dram_tensor("wu", [I3, D], F32, kind="ExternalInput").ap()
    wd_d = nc.dram_tensor("wd", [D, I3], F32, kind="ExternalInput").ap()
    id_d = nc.dram_tensor("ident", [P, P], BF16, kind="ExternalInput").ap()
    out_d = nc.dram_tensor("out", [D, S], F32, kind="ExternalOutput").ap()

    dbg_d = (nc.dram_tensor("dbg", [P, 16], F32, kind="ExternalOutput").ap()
             if os.environ.get("KDBG") else None)
    dbg2_d = (nc.dram_tensor("dbg2", [P, 16], F32, kind="ExternalOutput").ap()
              if os.environ.get("KDBG") else None)
    xq_d = (nc.dram_tensor("xq_dbg", [D, S], BF16, kind="ExternalOutput").ap()
            if os.environ.get("KDBG") else None)
    bscr_d = nc.dram_tensor("bscr", [8, 4096], F32)
    lscr_d = nc.dram_tensor("lscr", [12, 128], F32)
    rrow_d = nc.dram_tensor("rrow", [4, 512], F32)
    cci = [nc.dram_tensor(f"cci{i}", [1, 2], F32) for i in range(5)]
    cco = [nc.dram_tensor(f"cco{i}", [1, 2], F32) for i in range(5)]
    wdT_d = nc.dram_tensor("wdT_s", [48 * P, 512], BF16)
    wuT_d = nc.dram_tensor("wuT_s", [48 * P, 512], BF16)
    wgT_d = nc.dram_tensor("wgT_s", [48 * P, 512], BF16)
    spill = {}
    x2_d = nc.dram_tensor("x2_s", [D, S], BF16)
    up_d = nc.dram_tensor("up_s", [I3, S], BF16, kind="ExternalOutput" if os.environ.get("KDBG") else "Internal")
    gt_d = nc.dram_tensor("gt_s", [I3, S], BF16, kind="ExternalOutput" if os.environ.get("KDBG") else "Internal")
    or_d = nc.dram_tensor("or_s", [I3, S], BF16, kind="ExternalOutput" if os.environ.get("KDBG") else "Internal")

    sch = Sched()
    gstack = ExitStack()

    def gsb(name, shape, dt=F32):
        return gstack.enter_context(nc.sbuf_tensor(name, shape, dt))

    A = [gsb(f"A{c}", [P, S], BF16) for c in range(NC)]
    csum = gsb("csum", [P, 1], BF16)
    idn = gsb("idn", [P, P], BF16)
    gam = gsb("gam_s", [P, NC], F32)
    bet = gsb("bet_s", [P, NC], F32)
    fscr = gsb("fscr", [P, S], F32)
    rowbank = gsb("rowbank", [P, 4256], F32)
    lmax = gsb("lmax", [P, 6], F32)
    gsc = gsb("gsc", [P, 20], F32)

    sch.op("vector", lambda e: e.memset(gsc[:, 16:17], 2.0))
    t_csum0 = sch.op("vector", lambda e: e.memset(csum[:], 1.0 / 1024.0),
                     token=True)
    t_id = sch.dma("sync", idn[:], id_d)
    gb_toks = []
    for c in range(NC):
        gb_toks.append(sch.dma("sync", gam[:, c:c + 1],
                               gam_d[c:c+1].rearrange("a p -> p a")))
        gb_toks.append(sch.dma("sync", bet[:, c:c + 1],
                               bet_d[c:c+1].rearrange("a p -> p a")))
    t_gb = gb_toks[-1]

    ln_n = [0]

    def ln_stats(stack, get_chunk):
        ln_n[0] += 1
        nn = ln_n[0]
        pm = stack.enter_context(nc.psum_tensor(f"pm{nn}", [1, S], F32))
        pq = stack.enter_context(nc.psum_tensor(f"pq{nn}", [1, S], F32))
        sq0 = stack.enter_context(nc.sbuf_tensor(f"sq{nn}a", [P, S], BF16))
        sq1 = stack.enter_context(nc.sbuf_tensor(f"sq{nn}b", [P, S], BF16))
        sqs = [sq0, sq1]
        sq_cons = [None, None]
        cons = {}
        for c in range(NC):
            ap, toks = get_chunk(c, cons.get(c - 2))
            sq = sqs[c % 2]
            w = list(toks)
            if sq_cons[c % 2] is not None:
                w.append(sq_cons[c % 2])
            t_sq = sch.op("scalar", (lambda a_, q_: (lambda e: e.activation(
                q_[:], a_, AF.Square, bias=0.0, scale=1.0)))(ap, sq),
                waits=w, token=True)
            for n in range(NQ):
                st, sp = (c == 0), (c == NC - 1)
                sch.op("tensor", (lambda a_, n_, st_, sp_: (lambda e: e.matmul(
                    pm[0:1, n_ * QW:(n_ + 1) * QW], csum[:],
                    a_[:, n_ * QW:(n_ + 1) * QW], start=st_, stop=sp_)))(
                        ap, n, st, sp),
                    waits=list(toks) if n == 0 else ())
                cons[c] = sch.op("tensor", (lambda q_, n_, st_, sp_:
                                            (lambda e: e.matmul(
                    pq[0:1, n_ * QW:(n_ + 1) * QW], csum[:],
                    q_[:, n_ * QW:(n_ + 1) * QW], start=st_, stop=sp_)))(
                        sq, n, st, sp),
                    waits=[t_sq] if n == 0 else (), token=(n == NQ - 1))
            sq_cons[c % 2] = cons[c]
        t_mm = cons[NC - 1]
        t_sqm = sch.op("scalar", lambda e: e.activation(
            fscr[0:1, 0:S], pm[0:1, :], AF.Square, bias=0.0, scale=1.0),
            waits=[t_mm], token=True)
        t_v = sch.op("vector", lambda e: e.scalar_tensor_tensor(
            fscr[0:1, 0:S], pq[0:1, :], EPS, fscr[0:1, 0:S],
            ALU.add, ALU.subtract), waits=[t_sqm], token=True)
        t_ln = sch.op("scalar", lambda e: e.activation(
            rowbank[0:1, 0:S], fscr[0:1, 0:S], AF.Ln, bias=0.0, scale=1.0),
            waits=[t_v], token=True)
        t_h = sch.op("vector", lambda e: e.tensor_scalar(
            rowbank[0:1, S:2 * S], rowbank[0:1, 0:S], -0.5, None, ALU.mult),
            waits=[t_ln], token=True)
        t_a = sch.op("scalar", lambda e: e.activation(
            rowbank[0:1, 0:S], rowbank[0:1, S:2 * S], AF.Exp, bias=0.0,
            scale=1.0), waits=[t_h], token=True)
        t_b = sch.op("vector", lambda e: e.scalar_tensor_tensor(
            rowbank[0:1, S:2 * S], pm[0:1, :], -1.0, rowbank[0:1, 0:S],
            ALU.mult, ALU.mult), waits=[t_a], token=True)
        slot = nn % 8
        t_w = sch.dma("sync", bscr_d.ap()[slot:slot+1],
                      rowbank[0:1, 0:4096], waits=[t_b])
        abrep = stack.enter_context(
            nc.sbuf_tensor(f"abrep{nn}", [P, 4096], F32))
        t_bc = sch.dma("sync", abrep[:],
                       bscr_d.ap()[slot:slot+1]
                       .partition_broadcast(P), waits=[t_w])
        return abrep, t_bc, cons

    def fold_partitions_maxes(t_ready, round_idx, cols):
        toks = []
        nvals = len(cols)
        for j, c in enumerate(cols):
            t1 = sch.dma("sync",
                         lscr_d.ap()[2 * round_idx + j:2 * round_idx + j + 1, 0:P],
                         lmax[:, c:c + 1], waits=[t_ready] if t_ready else [])
            t2 = sch.dma("sync", rowbank[0:1, 4104:4104 + P],
                         lscr_d.ap()[2 * round_idx + j:2 * round_idx + j + 1, 0:P],
                         waits=[t1] + toks[-1:])
            tv = sch.op("vector", (lambda j_: (lambda e: e.tensor_reduce(
                rowbank[0:1, 4096 + j_:4097 + j_], rowbank[0:1, 4104:4104 + P],
                AX.X, ALU.max)))(j), waits=[t2], token=True)
            toks.append(tv)
        t3 = sch.dma("gpsimd", cci[round_idx].ap()[0:1, 0:nvals],
                     rowbank[0:1, 4096:4096 + nvals], waits=toks)
        t_cc = sch.cc((lambda ri, nv: (lambda e: e.collective_compute(
            "AllReduce", ALU.max, replica_groups=[list(range(8))],
            ins=[cci[ri].ap()[0:1, 0:nv].opt()],
            outs=[cco[ri].ap()[0:1, 0:nv].opt()])))(round_idx, nvals),
            waits=[t3])
        t4 = sch.dma("gpsimd", gsc[:, 10:10 + nvals],
                     cco[round_idx].ap()[0:1, 0:nvals]
                     .partition_broadcast(P), waits=[t_cc])
        return t4

    def make_scale(t_g, k, raw_col, pre=1.0):
        t_s = sch.op("vector", (lambda k_, rc, p_: (lambda e: e.tensor_scalar(
            gsc[:, 2 * k_:2 * k_ + 1], gsc[:, rc:rc + 1], p_ / QMAX, 1e-12,
            ALU.mult, ALU.max)))(k, raw_col, pre), waits=[t_g], token=True)
        sch.op("scalar", (lambda k_: (lambda e: e.activation(
            gsc[:, 12:13], gsc[:, 2 * k_:2 * k_ + 1], AF.Ln,
            bias=0.0, scale=1.0)))(k), waits=[t_s])
        t_r0 = sch.op("scalar", (lambda k_: (lambda e: e.activation(
            gsc[:, 13:14], gsc[:, 12:13], AF.Exp,
            bias=0.0, scale=-1.0)))(k), token=True)
        sch.op("vector", (lambda k_: (lambda e: e.tensor_tensor(
            gsc[:, 14:15], gsc[:, 2 * k_:2 * k_ + 1],
            gsc[:, 13:14], ALU.mult)))(k), waits=[t_r0])
        sch.op("vector", lambda e: e.drain())
        sch.op("vector", lambda e: e.tensor_tensor(
            gsc[:, 15:16], gsc[:, 16:17], gsc[:, 14:15], ALU.subtract))
        sch.op("vector", lambda e: e.drain())
        t_fin = sch.op("vector", (lambda k_: (lambda e: e.tensor_tensor(
            gsc[:, 2 * k_ + 1:2 * k_ + 2], gsc[:, 13:14], gsc[:, 15:16],
            ALU.mult)))(k), token=(pre == 1.0))
        if pre != 1.0:
            sch.op("vector", (lambda p_: (lambda e: e.memset(
                gsc[:, 17:18], p_)))(pre))
            sch.op("vector", lambda e: e.drain())
            t_fin = sch.op("vector", (lambda k_: (lambda e: e.tensor_tensor(
                gsc[:, 2 * k_ + 1:2 * k_ + 2],
                gsc[:, 2 * k_ + 1:2 * k_ + 2],
                gsc[:, 17:18], ALU.mult)))(k), token=True)
        return t_fin

    def qdq_pair(src_ap, dst_ap, k, waits, token=False):
        sch.op("vector", (lambda s_, k_: (lambda e: e.tensor_scalar(
            fscr[:, 0:s_ap_w(s_)], s_, gsc[:, 2 * k_ + 1:2 * k_ + 2], MAGIC,
            ALU.mult, ALU.add)))(src_ap, k), waits=waits)
        return sch.op("vector", (lambda s_, d_, k_: (lambda e: e.tensor_scalar(
            d_, fscr[:, 0:s_ap_w(s_)], MAGIC, gsc[:, 2 * k_:2 * k_ + 1],
            ALU.subtract, ALU.mult)))(src_ap, dst_ap, k), token=token)

    def s_ap_w(ap):
        return ap.shape[-1]

    t_csum = t_csum0
    with ExitStack() as ph:
        xs0 = ph.enter_context(nc.sbuf_tensor("xs0", [P, S], BF16))
        xs1 = ph.enter_context(nc.sbuf_tensor("xs1", [P, S], BF16))
        bufs = [xs0, xs1]

        def get_chunk(c, reuse):
            b = bufs[c % 2]
            w = [reuse] if reuse is not None else []
            t = sch.dma("gpsimd", b[:], xT_d[c * P:(c + 1) * P, :], waits=w)
            toks = [t] + ([t_csum] if c == 0 else [])
            return b[:], toks

        abrep, t_ab, cons = ln_stats(ph, get_chunk)

        prev = [cons[NC - 2], cons[NC - 1]]
        for c in range(NC):
            b = bufs[c % 2]
            w = [prev[c % 2]] if prev[c % 2] is not None else []
            t_l = sch.dma("gpsimd", b[:], xT_d[c * P:(c + 1) * P, :],
                          waits=w)
            sch.op("vector", (lambda b_, ab_: (lambda e: e.tensor_tensor(
                fscr[:], b_[:], ab_[:, 0:S], ALU.mult)))(b, abrep),
                waits=[t_l, t_ab])
            prev[c % 2] = sch.op("vector", (lambda c_, ab_: (lambda e: e.tensor_tensor(
                A[c_][:], fscr[:], ab_[:, S:2 * S], ALU.add)))(c, abrep),
                token=True)
        sch.barrier()

    big = ExitStack()
    qsb = [big.enter_context(nc.sbuf_tensor(f"q{c}", [P, S], BF16))
           for c in range(NC)]
    ksb = [big.enter_context(nc.sbuf_tensor(f"k{c}", [P, S], BF16))
           for c in range(NC)]
    vsb = [big.enter_context(nc.sbuf_tensor(f"v{t}", [P, H * 65], BF16))
           for t in range(NT)]
    with ExitStack() as ph:
        wfull = [ph.enter_context(nc.sbuf_tensor(f"wf{c}", [P, D], BF16))
                 for c in range(NC)]
        pj = [ph.enter_context(nc.psum_tensor(f"pj{i}", [P, QW], F32))
              for i in range(4)]
        t_vones = None
        for t in range(NT):
            t_vones = sch.op("vector", (lambda t_: (lambda e: e.memset(
                vsb[t_].ap().rearrange("p (h x) -> p h x", x=65)[:, :, 64:65],
                1.0)))(t), token=(t == NT - 1))
        bank_free = [None] * 4

        wslab = [ph.enter_context(nc.sbuf_tensor(f"wsl{i}", [P, D], F32))
                 for i in range(2)]
        wsl2 = ph.enter_context(nc.sbuf_tensor("wsl2", [P, D], F32))
        wqb = [ph.enter_context(nc.sbuf_tensor(f"wqb{i}", [P, D], BF16))
               for i in range(2)]
        Ssc = ph.enter_context(nc.sbuf_tensor("Ssc", [P, 32], F32))
        Sd = ph.enter_context(nc.sbuf_tensor("Sd", [P, 32], F32))
        Rsc = ph.enter_context(nc.sbuf_tensor("Rsc", [P, 32], F32))
        Rs2 = ph.enter_context(nc.sbuf_tensor("Rs2", [P, 32], F32))
        RsB = ph.enter_context(nc.sbuf_tensor("RsB", [P, 32], F32))
        tstg = [ph.enter_context(nc.sbuf_tensor(f"tstg{i}", [P, 512], BF16))
                for i in range(2)]
        psT = [ph.enter_context(nc.psum_tensor(f"psT{i}", [P, QW], BF16))
               for i in range(2)]
        deq_spills = []

        def b3(t):
            return t.ap()[:, 0:D].rearrange("p (a b) -> p a b", b=32)

        def deq_gen():
            t_q_hist = [None, None]
            t_psT_free = [None, None]
            t_stg_free = [None, None]
            t_tr_bypar = [None, None]
            units = []
            for src_d, dst_hbm, key, nsl, ncof in (
                    (wu_d, wuT_d, "wuT", NI, 1),
                    (wg_d, wgT_d, "wgT", NI, 1),
                    (wd_d, wdT_d, "wdT", NC, 3)):
                for widx in range(nsl):
                    for coff in range(ncof):
                        units.append((src_d, dst_hbm, key, widx, coff,
                                      ncof))
            prev = None
            for ui, unit in enumerate(units + [None, None]):
                if unit is not None:
                    src_d, dst_hbm, key, widx, coff, ncof = unit
                    sl = wslab[ui % 2]
                    w = ([t_q_hist[ui % 2]]
                         if t_q_hist[ui % 2] is not None else [])
                    t_l = sch.dma("sync", sl[:],
                                  src_d[widx * P:(widx + 1) * P,
                                        coff * D:(coff + 1) * D], waits=w)
                    sch.op("vector", (lambda s_: (lambda e: e.tensor_reduce(
                        Ssc[:], b3(s_), AX.X, ALU.max,
                        apply_absolute_value=True)))(sl), waits=[t_l])
                    sch.op("vector", lambda e: e.drain())
                    t_ss = sch.op("vector", lambda e: e.tensor_scalar(
                        Sd[:], Ssc[:], 1.0 / 7.0, 1e-12,
                        ALU.mult, ALU.max), token=True)
                    sch.op("scalar", lambda e: e.activation(
                        Rs2[:], Sd[:], AF.Ln, bias=0.0, scale=1.0),
                        waits=[t_ss])
                    t_r0 = sch.op("scalar", lambda e: e.activation(
                        Rsc[:], Rs2[:], AF.Exp, bias=0.0, scale=-1.0),
                        token=True)
                    sch.op("vector", lambda e: e.tensor_tensor(
                        Rs2[:], Sd[:], Rsc[:], ALU.mult), waits=[t_r0])
                    sch.op("vector", lambda e: e.drain())
                    sch.op("vector", lambda e: e.tensor_scalar(
                        RsB[:], Rs2[:], -1.0, 2.0, ALU.mult, ALU.add))
                    sch.op("vector", lambda e: e.drain())
                    sch.op("vector", lambda e: e.tensor_tensor(
                        Rsc[:], Rsc[:], RsB[:], ALU.mult))
                    sch.op("vector", lambda e: e.drain())
                    sch.op("vector", (lambda s_: (lambda e: e.tensor_tensor(
                        b3(wsl2), b3(s_),
                        Rsc[:, :, None].broadcast_to([P, 32, 32]),
                        ALU.mult)))(sl))
                    sch.op("vector", (lambda s_: (lambda e: e.tensor_scalar(
                        s_[:], wsl2[:], MAGIC, MAGIC,
                        ALU.add, ALU.subtract)))(sl))
                    t_q = sch.op("vector", (lambda s_, qb_: (lambda e:
                                 e.tensor_tensor(
                        b3(qb_), b3(s_),
                        Sd[:, :, None].broadcast_to([P, 32, 32]),
                        ALU.mult)))(sl, wqb[ui % 2]),
                        waits=[t_tr_bypar[ui % 2]]
                        if t_tr_bypar[ui % 2] is not None else (),
                        token=True)
                    t_q_hist[ui % 2] = t_q
                    nxt = (ui, t_q, dst_hbm, key,
                           (widx * ncof + coff) * 2)
                else:
                    nxt = None
                if prev is not None:
                    pui, t_q, dst_hbm, key, bib = prev
                    for jg in range(2):
                        pT = psT[jg]
                        stg = tstg[jg]
                        for jj in range(4):
                            j = 4 * jg + jj
                            w = [t_q] if jj == 0 else []
                            if jj == 0 and t_psT_free[jg] is not None:
                                w.append(t_psT_free[jg])
                            sch.op("tensor", (lambda j_, jj_, pT_, qb_:
                                              (lambda e: e.transpose(
                                pT_[:, jj_ * P:(jj_ + 1) * P],
                                qb_[:, j_ * P:(j_ + 1) * P], idn[:])))(
                                    j, jj, pT, wqb[pui % 2]),
                                waits=w + ([t_id] if jg == 0 and pui == 0
                                           else []))
                        t_tr = sch.op("tensor", lambda e: e.drain(),
                                      token=True)
                        t_tr_bypar[pui % 2] = t_tr
                        wev = [t_tr] + ([t_stg_free[jg]]
                                        if t_stg_free[jg] is not None
                                        else [])
                        t_ev = sch.op("scalar", (lambda pT_, stg_:
                                      (lambda e: e.copy(
                            stg_[:], pT_[:])))(pT, stg),
                            waits=wev, token=True)
                        t_psT_free[jg] = t_ev
                        bi = bib + jg
                        t_stg_free[jg] = sch.dma(
                            "sync", dst_hbm[bi * P:(bi + 1) * P, :],
                            stg[:], waits=[t_ev])
                        spill.setdefault(key, []).append(t_stg_free[jg])
                        deq_spills.append(t_stg_free[jg])
                prev = nxt
                yield

        deq = deq_gen()
        deq_n = [0, 0]

        def pace():
            deq_n[0] += 1
            tgt = min(72, deq_n[0] * 72 // 96 + 1)
            while deq_n[1] < tgt:
                if next(deq, "END") == "END":
                    break
                deq_n[1] += 1

        for mi, (wd, dst) in enumerate(((wqT_d, qsb), (wkT_d, ksb))):
            wl = ([t for t in bank_free if t is not None] if mi else [])
            wtoks = [sch.dma("gpsimd", wfull[c][:], wd[c * P:(c + 1) * P, :],
                             waits=wl if c == 0 else ())
                     for c in range(NC)]
            ti = 0
            for oc in range(NC):
                for n in range(NQ):
                    bk = ti % 4
                    ti += 1
                    pt = pj[bk]
                    for kc in range(NC):
                        w = []
                        if kc == 0:
                            w = list(wtoks)
                            if bank_free[bk] is not None:
                                w.append(bank_free[bk])
                        sch.op("tensor", (lambda kc_, oc_, n_, pt_, st_, sp_:
                                          (lambda e: e.matmul(
                            pt_[:], wfull[kc_][:, oc_ * P:(oc_ + 1) * P],
                            A[kc_][:, n_ * QW:(n_ + 1) * QW],
                            start=st_, stop=sp_)))(
                                kc, oc, n, pt, kc == 0, kc == NC - 1),
                            waits=w)
                    t_mm = sch.op("tensor", lambda e: e.drain(),
                                  token=True)
                    if mi == 0:
                        bank_free[bk] = sch.op(
                            "vector", (lambda d_, oc_, n_, pt_: (lambda e:
                                       e.tensor_scalar(
                                d_[oc_][:, n_ * QW:(n_ + 1) * QW], pt_[:],
                                ATT_SCALE, None, ALU.mult)))(dst, oc, n, pt),
                            waits=[t_mm], token=True)
                    else:
                        bank_free[bk] = sch.op(
                            "scalar", (lambda d_, oc_, n_, pt_: (lambda e:
                                       e.copy(
                                d_[oc_][:, n_ * QW:(n_ + 1) * QW], pt_[:])))(
                                    dst, oc, n, pt),
                            waits=[t_mm], token=True)
                    pace()
        wtoks = [sch.dma("gpsimd", wfull[c][:], wvT_d[c * P:(c + 1) * P, :],
                         waits=[bank_free[i] for i in range(4)] if c == 0
                         else ())
                 for c in range(NC)]
        ti = 0
        for t in range(NT):
            for half in range(2):
                bk = ti % 4
                ti += 1
                pt = pj[bk]
                for kc in range(NC):
                    w = []
                    if kc == 0:
                        w = list(wtoks)
                        if bank_free[bk] is not None:
                            w.append(bank_free[bk])
                    sch.op("tensor", (lambda kc_, t_, h_, pt_, st_, sp_:
                                      (lambda e: e.matmul(
                        pt_[:], A[kc_][:, t_ * P:(t_ + 1) * P],
                        wfull[kc_][:, h_ * QW:(h_ + 1) * QW],
                        start=st_, stop=sp_)))(
                            kc, t, half, pt, kc == 0, kc == NC - 1),
                        waits=w)
                t_mm = sch.op("tensor", lambda e: e.drain(), token=True)
                bank_free[bk] = sch.op(
                    "scalar", (lambda t_, h_, pt_: (lambda e: e.copy(
                        vsb[t_].ap().rearrange("p (hh x) -> p hh x", x=65)[
                            :, h_ * 8:(h_ + 1) * 8, 0:64],
                        pt_.ap().rearrange("p (hh x) -> p hh x", x=64))))(
                            t, half, pt),
                    waits=[t_mm, t_vones], token=True)
                pace()
        while next(deq, "END") != "END":
            pass
        mxs = {}
        for tk in deq_spills:
            mxs[tk.sem] = max(mxs.get(tk.sem, 0), tk.val)
        sch.op("sync", None, waits=[Tok(s_, v_) for s_, v_ in mxs.items()])
        sch.barrier()

    with ExitStack() as ph:
        ttsb = [ph.enter_context(nc.sbuf_tensor(f"tts{i}", [P, 3968], BF16))
                for i in range(4)]
        esb = [ph.enter_context(nc.sbuf_tensor(f"esb{i}", [P, 2 * QW], BF16))
               for i in range(3)]
        rrep = [ph.enter_context(nc.sbuf_tensor(f"rrep{i}", [HD, QW], F32))
                for i in range(2)]
        oscr = [ph.enter_context(nc.sbuf_tensor(f"oscr{i}", [HD, QW], F32))
                for i in range(2)]
        obias = [ph.enter_context(nc.sbuf_tensor(f"obias{i}", [HD, QW], F32))
                 for i in range(2)]
        ostg = ph.enter_context(nc.sbuf_tensor("ostg", [HD, QW], BF16))
        psS = [ph.enter_context(nc.psum_tensor(f"psS{i}", [P, 2 * QW],
                                               F32)) for i in range(2)]
        psUE = [ph.enter_context(nc.psum_tensor(f"psUE{h}", [P, QW], F32))
                for h in range(2)]
        psUB = [ph.enter_context(nc.psum_tensor(f"psUB{h}", [P, QW], F32))
                for h in range(2)]

        qzb = [[ph.enter_context(nc.sbuf_tensor(f"qz{i}{j}", [P, QW], BF16))
                for j in range(2)] for i in range(2)]
        t_qzz = sch.op("vector", lambda e: (
            e.memset(qzb[0][0][HD:P, :], 0.0)), token=False)
        sch.op("vector", lambda e: e.memset(qzb[0][1][0:HD, :], 0.0))
        sch.op("vector", lambda e: e.memset(qzb[1][0][HD:P, :], 0.0))
        t_qzz = sch.op("vector", lambda e: e.memset(qzb[1][1][0:HD, :], 0.0),
                       token=True)
        qz_free = [None, None]
        strip_tok = [None, None, None, None]
        strip_tok[0] = sch.dma("gpsimd", ttsb[0][:], tt_d[0])
        strip_tok[1] = sch.dma("gpsimd", ttsb[1][:], tt_d[1])
        pair_done = [None, None]
        t_exp = [None, None]
        t_av = [None] * 4
        ue_free = [None, None]
        ub_free = [None, None]
        oscr_free = [None, None]
        obias_free = [None, None]
        rrep_free = [None, None]
        t_ostg_dma = None
        rrow_free = [None] * 4
        fscr_free = [None] * 4

        for pr in range(8):
            par = pr % 2
            cur = [ttsb[par * 2], ttsb[par * 2 + 1]]
            cur_toks = [strip_tok[par * 2], strip_tok[par * 2 + 1]]
            if pr < 7:
                npar = (pr + 1) % 2
                w = [pair_done[npar]] if pair_done[npar] is not None else []
                strip_tok[npar * 2] = sch.dma(
                    "gpsimd", ttsb[npar * 2][:], tt_d[2 * pr + 2], waits=w)
                strip_tok[npar * 2 + 1] = sch.dma(
                    "gpsimd", ttsb[npar * 2 + 1][:], tt_d[2 * pr + 3])
            def emit_qz(pq, qt):
                qp = qt % 2
                w = [qz_free[qp]] if qz_free[qp] is not None else [t_qzz]
                return [
                    sch.op("scalar", (lambda pr_, qt_, qp_: (lambda e:
                           e.copy(
                        qzb[qp_][0][0:HD, :],
                        qsb[pr_][0:HD, qt_ * QW:(qt_ + 1) * QW])))(
                            pq, qt, qp),
                        waits=w, token=True),
                    sch.op("scalar", (lambda pr_, qt_, qp_: (lambda e:
                           e.copy(
                        qzb[qp_][1][HD:P, :],
                        qsb[pr_][HD:P, qt_ * QW:(qt_ + 1) * QW])))(
                            pq, qt, qp),
                        token=True)]

            if pr == 0:
                t_qz_next = emit_qz(0, 0)
            for qt in range(NQ):
                def emit_av(kc, t_e, want_token=False):
                    eb = esb[kc % 3]
                    for hh in range(2):
                        w = [t_e] if hh == 0 else []
                        if kc == 0:
                            if ue_free[hh] is not None:
                                w.append(ue_free[hh])
                            if ub_free[hh] is not None:
                                w.append(ub_free[hh])
                            w.extend(cur_toks)
                        col = (2 * pr + hh) * 65
                        ms = qt * QW - kc * P + 1920
                        sch.op("tensor", (lambda kc_, hh_, st_, sp_, b_,
                                          col_: (lambda e: e.matmul(
                            psUE[hh_][0:65, :],
                            vsb[kc_][:, col_:col_ + 65],
                            b_[:, hh_ * QW:(hh_ + 1) * QW],
                            start=st_, stop=sp_)))(
                                kc, hh, kc == 0, kc == 15, eb, col),
                            waits=w)
                        sch.op("tensor", (lambda kc_, hh_, st_, sp_, ms_, c_,
                                          col_: (lambda e: e.matmul(
                            psUB[hh_][0:65, :],
                            vsb[kc_][:, col_:col_ + 65],
                            c_[:, ms_:ms_ + QW],
                            start=st_, stop=sp_)))(
                                kc, hh, kc == 0, kc == 15, ms, cur[hh], col),
                            waits=[])
                    if want_token:
                        return sch.op("tensor", lambda e: e.drain(),
                                      token=True)
                    return None

                qp = qt % 2
                t_qz = t_qz_next
                pend = []
                for kc in range(16):
                    p2 = kc % 2
                    ps = psS[p2]
                    for hh in range(2):
                        w = []
                        if hh == 0 and t_exp[p2] is not None:
                            w.append(t_exp[p2])
                        if kc == 0 and hh == 0:
                            w.extend(t_qz)
                        sch.op("tensor", (lambda pr_, kc_, hh_, qz_,
                                          ps_: (lambda e: e.matmul(
                            ps_[:, hh_ * QW:(hh_ + 1) * QW],
                            ksb[pr_][:, kc_ * P:(kc_ + 1) * P],
                            qz_[:, 0:QW],
                            start=True, stop=True)))(pr, kc, hh,
                                                     qzb[qp][hh], ps),
                            waits=w)
                    t_qk = sch.op("tensor", lambda e: e.drain(),
                                  token=True)
                    qz_free[qp] = t_qk
                    if pend:
                        kcp, tep = pend.pop(0)
                        emit_av(kcp, tep)
                    w = [t_qk]
                    t_exp[p2] = sch.op(
                        "scalar", (lambda b_, ps_: (lambda e: e.activation(
                            b_[:], ps_[:], AF.Exp, bias=0.0,
                            scale=1.0)))(esb[kc % 3], ps),
                        waits=w, token=True)
                    pend.append((kc, t_exp[p2]))
                if qt < NQ - 1:
                    t_qz_next = emit_qz(pr, qt + 1)
                elif pr < 7:
                    t_qz_next = emit_qz(pr + 1, 0)
                t_pv = emit_av(*pend.pop(0), want_token=True)

                for hh in range(2):
                    slot = (qt * 2 + hh) % 4
                    wln = [t_pv]
                    if rrow_free[slot] is not None:
                        wln.append(rrow_free[slot])
                    sch.op("scalar", (lambda hh_, sl: (lambda e: e.activation(
                        rowbank[64:65, sl * QW:(sl + 1) * QW], psUE[hh_][64:65, :],
                        AF.Ln, bias=0.0, scale=1.0)))(hh, slot),
                        waits=wln)
                    w = [oscr_free[hh]] if oscr_free[hh] is not None else []
                    ue_free[hh] = sch.op(
                        "scalar", (lambda hh_: (lambda e: e.copy(
                            oscr[hh_][:], psUE[hh_][0:64, :])))(hh),
                        waits=w, token=True)
                    w = [obias_free[hh]] if obias_free[hh] is not None else []
                    ub_free[hh] = sch.op(
                        "scalar", (lambda hh_: (lambda e: e.copy(
                            obias[hh_][:], psUB[hh_][0:64, :])))(hh),
                        waits=w, token=True)
                    t_r = sch.op("scalar", (lambda sl: (lambda e: e.activation(
                        rowbank[64:65, sl * QW:(sl + 1) * QW],
                        rowbank[64:65, sl * QW:(sl + 1) * QW],
                        AF.Exp, bias=0.0, scale=-1.0)))(slot), token=True)
                    t_w = sch.dma("sync", rrow_d.ap()[slot:slot+1], rowbank[64:65, slot * QW:(slot + 1) * QW],
                        waits=[t_r])
                    rrow_free[slot] = t_w
                    wb = [t_w]
                    if rrep_free[hh] is not None:
                        wb.append(rrep_free[hh])
                    t_rr = sch.dma("sync", rrep[hh][:],
                                   rrow_d.ap()[slot:slot+1]
                                   .partition_broadcast(HD), waits=wb)
                    fslot = fscr[0:64, slot * QW:(slot + 1) * QW]
                    w = [t_rr, ue_free[hh]]
                    if fscr_free[slot] is not None:
                        w.append(fscr_free[slot])
                    t_mul = sch.op(
                        "vector", (lambda hh_, fs_: (lambda e: e.tensor_tensor(
                            fs_, oscr[hh_][:], rrep[hh_][:],
                            ALU.mult)))(hh, fslot),
                        waits=w, token=True)
                    rrep_free[hh] = t_mul
                    oscr_free[hh] = t_mul
                    if hh == 0:
                        t_o = sch.op(
                            "vector", (lambda pr_, qt_, fs_: (lambda e:
                                       e.tensor_tensor(
                                A[pr_][0:64, qt_ * QW:(qt_ + 1) * QW],
                                fs_, obias[0][:], ALU.add)))(
                                    pr, qt, fslot),
                            waits=[ub_free[0]], token=True)
                    else:
                        w2 = [ub_free[1], t_ostg_dma] if t_ostg_dma is not None \
                            else [ub_free[1]]
                        t_o = sch.op(
                            "vector", (lambda fs_: (lambda e: e.tensor_tensor(
                                ostg[:], fs_, obias[1][:],
                                ALU.add)))(fslot),
                            waits=w2, token=True)
                        t_ostg_dma = sch.dma(
                            "sync", A[pr][64:128, qt * QW:(qt + 1) * QW],
                            ostg[:], waits=[t_o])
                    obias_free[hh] = t_o
                    fscr_free[slot] = t_o
            pair_done[par] = sch.op("tensor", lambda e: e.drain(),
                                    token=True)
        t_attn_out = t_ostg_dma
        sch.barrier()
    big.close()
    late = ExitStack()
    X2 = [late.enter_context(nc.sbuf_tensor(f"X2{c}", [P, S], BF16))
          for c in range(NC)]
    x2_spill = []

    with ExitStack() as ph:
        abrep, t_ab, _ = ln_stats(
            ph, lambda c, r: (A[c][:], [t_attn_out]))
        t_prev = None
        for c in range(NC):
            sch.op("vector", (lambda c_, ab_: (lambda e: e.tensor_tensor(
                fscr[:], A[c_][:], ab_[:, 0:S], ALU.mult)))(c, abrep),
                waits=[t_ab] if c == 0 else ())
            sch.op("vector", (lambda ab_: (lambda e: e.tensor_tensor(
                fscr[:], fscr[:], ab_[:, S:2 * S], ALU.add)))(abrep))
            t_prev = sch.op("vector", (lambda c_: (lambda e: e.tensor_scalar(
                X2[c_][:], fscr[:], gam[:, c_:c_ + 1], bet[:, c_:c_ + 1],
                ALU.mult, ALU.add)))(c),
                waits=gb_toks if c == 0 else (), token=True)
        sch.barrier()
    with ExitStack() as ph:
        abrep, t_ab, _ = ln_stats(ph, lambda c, r: (X2[c][:], []))
        xb0 = ph.enter_context(nc.sbuf_tensor("xb0", [P, S], BF16))
        xb1 = ph.enter_context(nc.sbuf_tensor("xb1", [P, S], BF16))
        prev = [None, None]
        for c in range(NC):
            b = [xb0, xb1][c % 2]
            w = [prev[c % 2]] if prev[c % 2] is not None else []
            t_l = sch.dma("gpsimd", b[:], xT_d[c * P:(c + 1) * P, :], waits=w)
            sch.op("vector", (lambda c_, ab_: (lambda e: e.tensor_tensor(
                fscr[:], X2[c_][:], ab_[:, 0:S], ALU.mult)))(c, abrep),
                waits=[t_ab] if c == 0 else ())
            sch.op("vector", (lambda ab_: (lambda e: e.tensor_tensor(
                fscr[:], fscr[:], ab_[:, S:2 * S], ALU.add)))(abrep))
            prev[c % 2] = sch.op("vector", (lambda c_, b_: (lambda e:
                                 e.tensor_tensor(
                X2[c_][:], fscr[:], b_[:], ALU.add)))(c, b),
                waits=[t_l], token=True)
        sch.barrier()

    with ExitStack() as ph:
        abrep, t_ab, _ = ln_stats(ph, lambda c, r: (X2[c][:], []))
        t_mz = sch.op("vector", lambda e: e.memset(lmax[:, 0:2], 0.0),
                      token=True)
        t_prev = t_mz
        for c in range(NC):
            sch.op("vector", (lambda c_, ab_: (lambda e: e.tensor_tensor(
                fscr[:], X2[c_][:], ab_[:, 0:S], ALU.mult)))(c, abrep),
                waits=[t_ab, t_mz] if c == 0 else ())
            sch.op("vector", (lambda c_, ab_: (lambda e: e.tensor_tensor(
                A[c_][:], fscr[:], ab_[:, S:2 * S], ALU.add)))(c, abrep))
            sch.op("vector", (lambda c_: (lambda e: e.tensor_reduce(
                lmax[:, 2:3], A[c_][:], AX.X, ALU.max,
                apply_absolute_value=True)))(c))
            sch.op("vector", lambda e: e.drain())
            t_prev = sch.op("vector", lambda e: e.tensor_tensor(
                lmax[:, 0:1], lmax[:, 0:1], lmax[:, 2:3], ALU.max),
                token=True)
            x2_spill.append(sch.dma(
                "sync", x2_d.ap()[c * P:(c + 1) * P, :], X2[c][:],
                waits=[t_prev]))
        t_g = fold_partitions_maxes(t_prev, 0, [0])
        t_s1 = make_scale(t_g, 0, 10)
        if dbg2_d is not None:
            sch.dma("sync", dbg2_d[:, 0:16], gsc[:, 0:16], waits=[t_s1])
        for c in range(NC):
            tq_ = qdq_pair(A[c][:], A[c][:], 0, [t_s1] if c == 0 else [],
                           token=True)
            if xq_d is not None:
                sch.dma("sync", xq_d[c * P:(c + 1) * P, :], A[c][:],
                        waits=[tq_])
        sch.op("sync", None, waits=x2_spill)
        sch.barrier()
    late.close()

    big2 = ExitStack()
    wuT = big2.enter_context(nc.sbuf_tensor("wuT", [P, 48 * 512], BF16))
    wgT = big2.enter_context(nc.sbuf_tensor("wgT", [P, 48 * 512], BF16))

    def ug_stat(wT, kc, oc):
        off = oc * 1024 + (kc // 4) * 512 + (kc % 4) * P
        return wT[:, off:off + P]
    with ExitStack() as ph:
        evb = [ph.enter_context(nc.sbuf_tensor(f"ev{i}", [P, S], BF16))
               for i in range(2)]
        pj2 = [ph.enter_context(nc.psum_tensor(f"pj2{i}", [P, QW], F32))
               for i in range(4)]
        b0s = [ph.enter_context(nc.sbuf_tensor(f"s1b{i}", [P, S], BF16))
               for i in range(2)]
        fscr2 = ph.enter_context(nc.sbuf_tensor("fscr2", [P, S], F32))
        fscrC = ph.enter_context(nc.sbuf_tensor("fscrC", [P, S], F32))
        tg2b = ph.enter_context(nc.sbuf_tensor("tg2b", [P, S], BF16))
        t_mz = sch.op("vector", lambda e: e.memset(lmax[:], 0.0),
                      token=True)
        wg_toks = [sch.dma("sync", wgT[:, b * 512:(b + 1) * 512],
                           wgT_d.ap()[b * P:(b + 1) * P, :],
                           waits=spill["wgT"] if b == 0 else ())
                   for b in range(48)]
        wu_toks = [sch.dma("sync", wuT[:, b * 512:(b + 1) * 512],
                           wuT_d.ap()[b * P:(b + 1) * P, :],
                           waits=spill["wuT"] if b == 0 else ())
                   for b in range(48)]
        bank_free = [None] * 4
        ev_free = [None, None]

        def mm_group(wT, oc, first_toks, sp_d, key, acol, scol):
            eb = evb[oc % 2]
            for n in range(NQ):
                pt = pj2[n]
                for kc in range(NC):
                    w = []
                    if kc == 0 and bank_free[n] is not None:
                        w.append(bank_free[n])
                    if first_toks is not None and n == 0 and kc == 0:
                        w.extend(first_toks)
                    sch.op("tensor", (lambda wT_, kc_, oc_, n_, pt_, st_,
                                      sp_: (lambda e: e.matmul(
                        pt_[:], ug_stat(wT_, kc_, oc_),
                        A[kc_][:, n_ * QW:(n_ + 1) * QW],
                        start=st_, stop=sp_)))(
                            wT, kc, oc, n, pt, kc == 0, kc == NC - 1),
                        waits=w)
            t_mm = sch.op("tensor", lambda e: e.drain(), token=True)
            t_ev = None
            for n in range(NQ):
                w = [t_mm] if n == 0 else []
                if n == 0 and ev_free[oc % 2] is not None:
                    w.append(ev_free[oc % 2])
                t_ev = sch.op("scalar", (lambda b_, n_, pt_: (lambda e:
                                         e.copy(
                    b_[:, n_ * QW:(n_ + 1) * QW], pt_[:])))(
                        eb, n, pj2[n]),
                    waits=w, token=True)
                bank_free[n] = t_ev
            sch.op("vector", (lambda b_, sc: (lambda e: e.tensor_reduce(
                lmax[:, sc:sc + 1], b_[:], AX.X, ALU.max,
                apply_absolute_value=True)))(eb, scol),
                waits=[t_ev, t_mz])
            sch.op("vector", lambda e: e.drain())
            t_mx = sch.op("vector", (lambda ac, sc: (lambda e:
                          e.tensor_tensor(
                lmax[:, ac:ac + 1], lmax[:, ac:ac + 1], lmax[:, sc:sc + 1],
                ALU.max)))(acol, scol), token=True)
            ev_free[oc % 2] = sch.dma(
                "sync", sp_d.ap()[oc * P:(oc + 1) * P, :], eb[:],
                waits=[t_ev, t_mx])
            spill[(key, oc)] = ev_free[oc % 2]
            return t_mx, t_mm

        for oc in range(NI):
            t_mx_g, _ = mm_group(wgT, oc, wg_toks if oc == 0 else None,
                                 gt_d, "gt", 0, 2)
        t_g3 = fold_partitions_maxes(t_mx_g, 1, [0])
        t_s3 = make_scale(t_g3, 2, 10)

        prev = [None, None]
        t_tg_free = None
        for oc in range(NI):
            t_mx_u, _ = mm_group(wuT, oc, wu_toks if oc == 0 else None,
                                 up_d, "up", 1, 2)
            b0 = b0s[oc % 2]
            w = [prev[oc % 2]] if prev[oc % 2] is not None else []
            w.append(spill[("gt", oc)])
            t_l = sch.dma("sync", b0[:], gt_d.ap()[oc * P:(oc + 1) * P, :],
                          waits=w)
            sch.op("vector", (lambda b_: (lambda e: e.tensor_scalar(
                fscrC[:], b_[:], gsc[:, 5:6], MAGIC, ALU.mult, ALU.add)))(b0),
                waits=[t_l, t_s3] if oc == 0 else [t_l])
            prev[oc % 2] = sch.op("vector", lambda e: e.tensor_scalar(
                fscr[:], fscrC[:], MAGIC, gsc[:, 4:5], ALU.subtract, ALU.mult),
                token=True)
            t_gq = prev[oc % 2]
            t_sig = sch.op("scalar", lambda e: e.activation(
                fscr2[:], fscr[:], AF.Sigmoid, bias=0.0, scale=1.0),
                waits=[t_gq], token=True)
            sch.op("vector", lambda e: e.tensor_scalar(
                fscrC[:], fscr2[:], 65536.0, MAGIC, ALU.mult, ALU.add),
                waits=[t_sig])
            sch.op("vector", lambda e: e.tensor_scalar(
                fscr2[:], fscrC[:], MAGIC, 65535.0, ALU.subtract, ALU.min))
            w = [t_tg_free] if t_tg_free is not None else []
            t_t2 = sch.op("vector", lambda e: e.tensor_tensor(
                tg2b[:], fscr[:], fscr2[:], ALU.mult), waits=w, token=True)
            sch.op("vector", lambda e: e.tensor_reduce(
                lmax[:, 3:4], tg2b[:], AX.X, ALU.max,
                apply_absolute_value=True))
            sch.op("vector", lambda e: e.drain())
            t_mx_s1 = sch.op("vector", lambda e: e.tensor_tensor(
                lmax[:, 0:1], lmax[:, 0:1], lmax[:, 3:4], ALU.max),
                token=True)
            t_tg_free = sch.dma("sync", or_d.ap()[oc * P:(oc + 1) * P, :],
                                tg2b[:], waits=[t_t2, t_mx_s1])
            spill[("or", oc)] = t_tg_free
        t_g24 = fold_partitions_maxes(t_mx_s1, 2, [1, 0])
        t_s2 = make_scale(t_g24, 1, 10)
        t_s4 = make_scale(t_g24, 3, 11, pre=1.0 / 65536.0)
        sch.barrier()
    big2.close()

    with ExitStack() as ph:
        wdT = ph.enter_context(nc.sbuf_tensor("wdT", [P, 48 * 512], BF16))

        def wd_stat(icx, oc):
            off = ((oc * 3 + icx // 8) * 2 + (icx % 8) // 4) * 512 \
                + (icx % 4) * P
            return wdT[:, off:off + P]
        b0s = [ph.enter_context(nc.sbuf_tensor(f"s2b{i}", [P, S], BF16))
               for i in range(2)]
        upb = [ph.enter_context(nc.sbuf_tensor(f"upb{i}", [P, S], BF16))
               for i in range(2)]
        fscr2 = ph.enter_context(nc.sbuf_tensor("fscr2b", [P, S], F32))
        fscrC = ph.enter_context(nc.sbuf_tensor("fscrCb", [P, S], F32))
        ph9_out = ph.enter_context(nc.sbuf_tensor("ph9o", [P, S], BF16))
        wd_toks = [sch.dma("sync", wdT[:, b * 512:(b + 1) * 512],
                           wdT_d.ap()[b * P:(b + 1) * P, :],
                           waits=spill["wdT"] if b == 0 else ())
                   for b in range(48)]
        prev = [None, None]
        t_up_free = [None, None]
        t_out_free = None
        for oc in range(NI):
            b0 = b0s[oc % 2]
            w = [prev[oc % 2]] if prev[oc % 2] is not None else []
            w.append(spill[("or", oc)])
            t_l = sch.dma("sync", b0[:], or_d.ap()[oc * P:(oc + 1) * P, :],
                          waits=w)
            w = [t_up_free[oc % 2]] if t_up_free[oc % 2] is not None else []
            w.append(spill[("up", oc)])
            t_lu = sch.dma("sync", upb[oc % 2][:],
                           up_d.ap()[oc * P:(oc + 1) * P, :], waits=w)
            sch.op("vector", (lambda b_: (lambda e: e.tensor_scalar(
                fscrC[:], b_[:], gsc[:, 7:8], MAGIC, ALU.mult, ALU.add)))(b0),
                waits=([t_l, t_s4, t_s2] if oc == 0 else [t_l]))
            prev[oc % 2] = sch.op("vector", lambda e: e.tensor_scalar(
                fscr[:], fscrC[:], MAGIC, gsc[:, 6:7], ALU.subtract, ALU.mult),
                token=True)
            sch.op("vector", (lambda u_: (lambda e: e.tensor_scalar(
                fscrC[:], u_[:], gsc[:, 3:4], MAGIC, ALU.mult,
                ALU.add)))(upb[oc % 2]), waits=[t_lu])
            t_up_free[oc % 2] = sch.op("vector", lambda e: e.tensor_scalar(
                fscr2[:], fscrC[:], MAGIC, gsc[:, 2:3],
                ALU.subtract, ALU.mult), token=True)
            w = [t_out_free] if t_out_free is not None else []
            t_o = sch.op("vector", lambda e: e.tensor_tensor(
                ph9_out[:], fscr[:], fscr2[:], ALU.mult), waits=w, token=True)
            sch.op("vector", lambda e: e.tensor_reduce(
                lmax[:, 2:3], ph9_out[:], AX.X, ALU.max,
                apply_absolute_value=True))
            sch.op("vector", lambda e: e.drain())
            t_mx = sch.op("vector", lambda e: e.tensor_tensor(
                lmax[:, 5:6], lmax[:, 5:6], lmax[:, 2:3], ALU.max),
                token=True)
            t_out_free = sch.dma("sync", gt_d.ap()[oc * P:(oc + 1) * P, :],
                                 ph9_out[:], waits=[t_o, t_mx])
            spill[("og", oc)] = t_out_free
        t_g = fold_partitions_maxes(t_mx, 3, [5])
        t_s5 = make_scale(t_g, 4, 10)
        ob = [ph.enter_context(nc.sbuf_tensor(f"ob{i}", [P, QW], BF16))
              for i in range(NI)]
        orb = [ph.enter_context(nc.sbuf_tensor(f"orb{i}", [P, QW], BF16))
               for i in range(2)]
        pj3 = [ph.enter_context(nc.psum_tensor(f"pj3{i}", [P, QW], F32))
               for i in range(NC)]
        bank_free = [None] * NC
        ob_free = [None] * NI

        prev = [None, None]
        for n in range(NQ):
            for icx in range(NI):
                b0 = orb[icx % 2]
                w = [prev[icx % 2]] if prev[icx % 2] is not None else []
                w.append(spill[("og", icx)])
                t_l = sch.dma("sync", b0[:, 0:QW],
                              gt_d.ap()[icx * P:(icx + 1) * P,
                                        n * QW:(n + 1) * QW], waits=w)
                w = [t_l] + ([t_s5] if (n == 0 and icx == 0) else [])
                if ob_free[icx] is not None:
                    w.append(ob_free[icx])
                sch.op("vector", (lambda b_: (lambda e: e.tensor_scalar(
                    fscr[:, 0:QW], b_[:, 0:QW], gsc[:, 9:10], MAGIC,
                    ALU.mult, ALU.add)))(b0), waits=w)
                t_ov = sch.op(
                    "vector", (lambda icx_: (lambda e: e.tensor_scalar(
                        ob[icx_][:], fscr[:, 0:QW], MAGIC, gsc[:, 8:9],
                        ALU.subtract, ALU.mult)))(icx), token=True)
                prev[icx % 2] = t_ov
                for oc in range(NC):
                    w = []
                    if oc == 0:
                        w = [t_ov] + list(
                            [wd_toks[-1]] if (n == 0 and icx == 0) else [])
                    if icx == 0:
                        if bank_free[oc] is not None:
                            w.append(bank_free[oc])
                    sch.op("tensor", (lambda icx_, oc_, st_, sp_:
                                      (lambda e: e.matmul(
                        pj3[oc_][:], wd_stat(icx_, oc_),
                        ob[icx_][:], start=st_, stop=sp_)))(
                            icx, oc, icx == 0, icx == NI - 1),
                        waits=w)
                if icx % 4 == 3:
                    tok4 = sch.op("tensor", lambda e: e.drain(),
                                  token=True)
                    for i4 in range(icx - 3, icx + 1):
                        ob_free[i4] = tok4
            t_mm = ob_free[NI - 1]
            for oc in range(NC):
                t_ev = sch.op("scalar", (lambda oc_, n_: (lambda e:
                              e.copy(
                    A[oc_][:, n_ * QW:(n_ + 1) * QW], pj3[oc_][:])))(oc, n),
                    waits=[t_mm] if oc == 0 else (), token=True)
                bank_free[oc] = t_ev
                spill.setdefault(("y", oc), []).append(t_ev)
        sch.barrier()

    with ExitStack() as ph:
        obuf = [ph.enter_context(nc.sbuf_tensor(f"obf{i}", [P, S], F32))
                for i in range(2)]
        xrb = [ph.enter_context(nc.sbuf_tensor(f"xr{i}", [P, S], BF16))
               for i in range(2)]

        def get_y(c, reuse):
            return A[c][:], list(spill[("y", c)])

        abrep, t_ab, cons = ln_stats(ph, get_y)
        prev = [None, None]
        oprev = [None, None]
        for c in range(NC):
            okb = obuf[c % 2]
            w = [prev[c % 2]] if prev[c % 2] is not None else []
            t_x2 = sch.dma("sync", xrb[c % 2][:],
                           x2_d.ap()[c * P:(c + 1) * P, :], waits=w)
            sch.op("vector", (lambda c_, ab_: (lambda e: e.tensor_tensor(
                fscr[:], A[c_][:], ab_[:, 0:S], ALU.mult)))(c, abrep),
                waits=[t_ab])
            sch.op("vector", (lambda ab_: (lambda e: e.tensor_tensor(
                fscr[:], fscr[:], ab_[:, S:2 * S], ALU.add)))(abrep))
            w = [oprev[c % 2], t_x2] if oprev[c % 2] is not None else [t_x2]
            prev[c % 2] = sch.op(
                "vector", (lambda c_, o_: (lambda e: e.tensor_tensor(
                    o_[:], fscr[:], xrb[c_ % 2][:], ALU.add)))(c, okb),
                waits=w, token=True)
            oprev[c % 2] = sch.dma(
                "sync", out_d[c * P:(c + 1) * P, :], obuf[c % 2][:],
                waits=[prev[c % 2]])
        if dbg_d is not None:
            sch.dma("sync", dbg_d[:, 0:12], gsc[:, 0:12])
            sch.dma("sync", dbg_d[:, 12:16], lmax[:])
        total = sch.qn["sync"]
        fin = []
        for r in range(DMA_ROT):
            n_r = (total - r + DMA_ROT - 1) // DMA_ROT if total > r else 0
            if n_r:
                fin.append(Tok(f"q_sync{r}", 16 * n_r))
        sch.op("sync", None, waits=fin)

    emit(nc, sch)
    gstack.close()
    return nc


def emit(nc, sch):
    with ExitStack() as es:
        names = ["t_" + e for e in ENGINES] + ["B", "C"]
        for q in ("sync", "gpsimd", "scalar"):
            for r in range(DMA_ROT):
                names.append(f"q_{q}{r}")
        sems = {n: es.enter_context(nc.semaphore(n)) for n in names}
        with nc.Block() as block:
            def runner(eng_name):
                def f(eng):
                    for make, waits, tok in sch.items[eng_name]:
                        for t in waits:
                            eng.wait_ge(sems[t.sem], t.val)
                        if make is None:
                            continue
                        inst = make(eng)
                        if tok is not None:
                            if tok == "C":
                                inst.then_inc(sems["C"], 1)
                            elif tok.sem.startswith("q_"):
                                inst.then_inc(sems[tok.sem], 16)
                            elif eng_name in ("vector", "scalar") and \
                                    tok.sem != "B":
                                eng.drain().then_inc(sems[tok.sem], 1)
                            else:
                                inst.then_inc(sems[tok.sem], 1)

                return f
            block.sync(runner("sync"))
            block.gpsimd(runner("gpsimd"))
            block.vector(runner("vector"))
            block.scalar(runner("scalar"))
            block.tensor(runner("tensor"))


def kernel(L0_attn_x, L0_attn_Wq, L0_attn_Wk, L0_attn_Wv,
           L0_attn_rel_bias_table, L0_attn_ln_gamma, L0_attn_ln_beta,
           L0_ffn_w_gate, L0_ffn_w_up, L0_ffn_w_down):
    x = np.asarray(L0_attn_x)
    table = np.asarray(L0_attn_rel_bias_table)
    idx = 127 + np.arange(3968)[None, :] - np.arange(128)[:, None]
    ttarr = np.ascontiguousarray(np.transpose(table[idx, :], (2, 0, 1)))
    base = {
        "wqT": np.ascontiguousarray(np.asarray(L0_attn_Wq).T),
        "wkT": np.ascontiguousarray(np.asarray(L0_attn_Wk).T),
        "wvT": np.ascontiguousarray(np.asarray(L0_attn_Wv).T),
        "tt": ttarr,
        "gam": np.ascontiguousarray(
            np.asarray(L0_attn_ln_gamma).reshape(NC, P)),
        "bet": np.ascontiguousarray(
            np.asarray(L0_attn_ln_beta).reshape(NC, P)),
        "wg": np.ascontiguousarray(np.asarray(L0_ffn_w_gate)),
        "wu": np.ascontiguousarray(np.asarray(L0_ffn_w_up)),
        "wd": np.ascontiguousarray(np.asarray(L0_ffn_w_down)),
        "ident": np.eye(P, dtype=ml_dtypes.bfloat16),
    }
    in_maps = []
    for b in range(8):
        m = dict(base)
        m["xT"] = np.ascontiguousarray(x[b].T)
        in_maps.append(m)
    nc = build_program()
    res = run_bass_kernel_spmd(nc, in_maps, core_ids=list(range(8)))
    outs = [np.asarray(r["out"]).T for r in res.results]
    return np.ascontiguousarray(np.stack(outs, axis=0))


if __name__ == "__main__":
    print("building...")
    nc = build_program()
    print("built ok")


# version 30
# speedup vs baseline: 1.0722x; 1.0157x over previous
import os
import numpy as np
import ml_dtypes
import concourse.bass as bass
import concourse.mybir as mybir
from concourse.bass_utils import run_bass_kernel_spmd
from contextlib import ExitStack

F32 = mybir.dt.float32
BF16 = mybir.dt.bfloat16
AF = mybir.ActivationFunctionType
ALU = mybir.AluOpType
AX = mybir.AxisListType

P = 128
S = 2048
D = 1024
H = 16
HD = 64
NC = D // P
NT = S // P
I3 = 3072
NI = I3 // P
NQ = 4
QW = 512
MAGIC = 3 * 2.0**22
EPS = 1e-5
QMAX = 32767.0
ATT_SCALE = 1.0 / 32.0

ENGINES = ("sync", "gpsimd", "vector", "scalar", "tensor")


class Tok:
    __slots__ = ("sem", "val")

    def __init__(self, sem, val):
        self.sem = sem
        self.val = val


DMA_ROT = 16


class Sched:
    def __init__(self):
        self.items = {e: [] for e in ENGINES}
        self.tokn = {e: 0 for e in ENGINES}
        self.qn = {"sync": 0, "gpsimd": 0, "scalar": 0}
        self.bn = 0

    def op(self, eng, make, waits=(), token=False):
        tok = None
        if token:
            self.tokn[eng] += 1
            tok = Tok("t_" + eng, self.tokn[eng])
        self.items[eng].append((make, list(waits), tok))
        return tok

    def dma(self, eng, out_ap, in_ap, waits=()):
        i = self.qn[eng]
        self.qn[eng] += 1
        sem = f"q_{eng}{i % DMA_ROT}"
        occ = i // DMA_ROT
        w = list(waits)
        if occ > 0:
            w.append(Tok(sem, 16 * occ))
        tok = Tok(sem, 16 * (occ + 1))
        mk = (lambda o, i_: (lambda e: e.dma_start(out=o, in_=i_)))(out_ap, in_ap)
        self.items[eng].append((mk, w, tok))
        return tok

    def cc(self, make, waits=()):
        self.items["gpsimd"].append((make, list(waits), "C"))
        self.ccn = getattr(self, "ccn", 0) + 1
        return Tok("C", self.ccn)

    def barrier(self, engines=ENGINES):
        self.bn += len(engines)
        tgt = self.bn
        for e in engines:
            self.items[e].append((lambda en: en.drain(), [], Tok("B", tgt)))
        for e in engines:
            self.items[e].append((None, [Tok("B", tgt)], None))


def build_program():
    nc = bass.Bass("TRN2", target_bir_lowering=False,
                   detect_race_conditions=False)

    xT_d = nc.dram_tensor("xT", [D, S], F32, kind="ExternalInput").ap()
    wqT_d = nc.dram_tensor("wqT", [D, D], F32, kind="ExternalInput").ap()
    wkT_d = nc.dram_tensor("wkT", [D, D], F32, kind="ExternalInput").ap()
    wvT_d = nc.dram_tensor("wvT", [D, D], F32, kind="ExternalInput").ap()
    tt_d = nc.dram_tensor("tt", [H, P, 3968], F32, kind="ExternalInput").ap()
    gam_d = nc.dram_tensor("gam", [NC, P], F32, kind="ExternalInput").ap()
    bet_d = nc.dram_tensor("bet", [NC, P], F32, kind="ExternalInput").ap()
    wg_d = nc.dram_tensor("wg", [I3, D], F32, kind="ExternalInput").ap()
    wu_d = nc.dram_tensor("wu", [I3, D], F32, kind="ExternalInput").ap()
    wd_d = nc.dram_tensor("wd", [D, I3], F32, kind="ExternalInput").ap()
    id_d = nc.dram_tensor("ident", [P, P], BF16, kind="ExternalInput").ap()
    out_d = nc.dram_tensor("out", [D, S], F32, kind="ExternalOutput").ap()

    dbg_d = (nc.dram_tensor("dbg", [P, 16], F32, kind="ExternalOutput").ap()
             if os.environ.get("KDBG") else None)
    dbg2_d = (nc.dram_tensor("dbg2", [P, 16], F32, kind="ExternalOutput").ap()
              if os.environ.get("KDBG") else None)
    xq_d = (nc.dram_tensor("xq_dbg", [D, S], BF16, kind="ExternalOutput").ap()
            if os.environ.get("KDBG") else None)
    bscr_d = nc.dram_tensor("bscr", [8, 4096], F32)
    lscr_d = nc.dram_tensor("lscr", [12, 128], F32)
    rrow_d = nc.dram_tensor("rrow", [4, 512], F32)
    cci = [nc.dram_tensor(f"cci{i}", [1, 2], F32) for i in range(5)]
    cco = [nc.dram_tensor(f"cco{i}", [1, 2], F32) for i in range(5)]
    wdT_d = nc.dram_tensor("wdT_s", [48 * P, 512], BF16)
    wuT_d = nc.dram_tensor("wuT_s", [48 * P, 512], BF16)
    wgT_d = nc.dram_tensor("wgT_s", [48 * P, 512], BF16)
    spill = {}
    x2_d = nc.dram_tensor("x2_s", [D, S], BF16)
    up_d = nc.dram_tensor("up_s", [I3, S], BF16, kind="ExternalOutput" if os.environ.get("KDBG") else "Internal")
    gt_d = nc.dram_tensor("gt_s", [I3, S], BF16, kind="ExternalOutput" if os.environ.get("KDBG") else "Internal")
    or_d = nc.dram_tensor("or_s", [I3, S], BF16, kind="ExternalOutput" if os.environ.get("KDBG") else "Internal")

    sch = Sched()
    gstack = ExitStack()

    def gsb(name, shape, dt=F32):
        return gstack.enter_context(nc.sbuf_tensor(name, shape, dt))

    A = [gsb(f"A{c}", [P, S], BF16) for c in range(NC)]
    csum = gsb("csum", [P, 1], BF16)
    idn = gsb("idn", [P, P], BF16)
    gam = gsb("gam_s", [P, NC], F32)
    bet = gsb("bet_s", [P, NC], F32)
    fscr = gsb("fscr", [P, S], F32)
    rowbank = gsb("rowbank", [P, 4256], F32)
    lmax = gsb("lmax", [P, 6], F32)
    gsc = gsb("gsc", [P, 20], F32)

    sch.op("vector", lambda e: e.memset(gsc[:, 16:17], 2.0))
    t_csum0 = sch.op("vector", lambda e: e.memset(csum[:], 1.0 / 1024.0),
                     token=True)
    t_id = sch.dma("sync", idn[:], id_d)
    gb_toks = []
    for c in range(NC):
        gb_toks.append(sch.dma("sync", gam[:, c:c + 1],
                               gam_d[c:c+1].rearrange("a p -> p a")))
        gb_toks.append(sch.dma("sync", bet[:, c:c + 1],
                               bet_d[c:c+1].rearrange("a p -> p a")))
    t_gb = gb_toks[-1]

    ln_n = [0]

    def ln_stats(stack, get_chunk):
        ln_n[0] += 1
        nn = ln_n[0]
        pm = stack.enter_context(nc.psum_tensor(f"pm{nn}", [1, S], F32))
        pq = stack.enter_context(nc.psum_tensor(f"pq{nn}", [1, S], F32))
        sq0 = stack.enter_context(nc.sbuf_tensor(f"sq{nn}a", [P, S], BF16))
        sq1 = stack.enter_context(nc.sbuf_tensor(f"sq{nn}b", [P, S], BF16))
        sqs = [sq0, sq1]
        sq_cons = [None, None]
        cons = {}
        for c in range(NC):
            ap, toks = get_chunk(c, cons.get(c - 2))
            sq = sqs[c % 2]
            w = list(toks)
            if sq_cons[c % 2] is not None:
                w.append(sq_cons[c % 2])
            t_sq = sch.op("scalar", (lambda a_, q_: (lambda e: e.activation(
                q_[:], a_, AF.Square, bias=0.0, scale=1.0)))(ap, sq),
                waits=w, token=True)
            for n in range(NQ):
                st, sp = (c == 0), (c == NC - 1)
                sch.op("tensor", (lambda a_, n_, st_, sp_: (lambda e: e.matmul(
                    pm[0:1, n_ * QW:(n_ + 1) * QW], csum[:],
                    a_[:, n_ * QW:(n_ + 1) * QW], start=st_, stop=sp_)))(
                        ap, n, st, sp),
                    waits=list(toks) if n == 0 else ())
                cons[c] = sch.op("tensor", (lambda q_, n_, st_, sp_:
                                            (lambda e: e.matmul(
                    pq[0:1, n_ * QW:(n_ + 1) * QW], csum[:],
                    q_[:, n_ * QW:(n_ + 1) * QW], start=st_, stop=sp_)))(
                        sq, n, st, sp),
                    waits=[t_sq] if n == 0 else (), token=(n == NQ - 1))
            sq_cons[c % 2] = cons[c]
        t_mm = cons[NC - 1]
        t_sqm = sch.op("scalar", lambda e: e.activation(
            fscr[0:1, 0:S], pm[0:1, :], AF.Square, bias=0.0, scale=1.0),
            waits=[t_mm], token=True)
        t_v = sch.op("vector", lambda e: e.scalar_tensor_tensor(
            fscr[0:1, 0:S], pq[0:1, :], EPS, fscr[0:1, 0:S],
            ALU.add, ALU.subtract), waits=[t_sqm], token=True)
        t_ln = sch.op("scalar", lambda e: e.activation(
            rowbank[0:1, 0:S], fscr[0:1, 0:S], AF.Ln, bias=0.0, scale=1.0),
            waits=[t_v], token=True)
        t_h = sch.op("vector", lambda e: e.tensor_scalar(
            rowbank[0:1, S:2 * S], rowbank[0:1, 0:S], -0.5, None, ALU.mult),
            waits=[t_ln], token=True)
        t_a = sch.op("scalar", lambda e: e.activation(
            rowbank[0:1, 0:S], rowbank[0:1, S:2 * S], AF.Exp, bias=0.0,
            scale=1.0), waits=[t_h], token=True)
        t_b = sch.op("vector", lambda e: e.scalar_tensor_tensor(
            rowbank[0:1, S:2 * S], pm[0:1, :], -1.0, rowbank[0:1, 0:S],
            ALU.mult, ALU.mult), waits=[t_a], token=True)
        slot = nn % 8
        t_w = sch.dma("sync", bscr_d.ap()[slot:slot+1],
                      rowbank[0:1, 0:4096], waits=[t_b])
        abrep = stack.enter_context(
            nc.sbuf_tensor(f"abrep{nn}", [P, 4096], F32))
        t_bc = sch.dma("sync", abrep[:],
                       bscr_d.ap()[slot:slot+1]
                       .partition_broadcast(P), waits=[t_w])
        return abrep, t_bc, cons

    def fold_partitions_maxes(t_ready, round_idx, cols):
        toks = []
        nvals = len(cols)
        for j, c in enumerate(cols):
            t1 = sch.dma("sync",
                         lscr_d.ap()[2 * round_idx + j:2 * round_idx + j + 1, 0:P],
                         lmax[:, c:c + 1], waits=[t_ready] if t_ready else [])
            t2 = sch.dma("sync", rowbank[0:1, 4104:4104 + P],
                         lscr_d.ap()[2 * round_idx + j:2 * round_idx + j + 1, 0:P],
                         waits=[t1] + toks[-1:])
            tv = sch.op("vector", (lambda j_: (lambda e: e.tensor_reduce(
                rowbank[0:1, 4096 + j_:4097 + j_], rowbank[0:1, 4104:4104 + P],
                AX.X, ALU.max)))(j), waits=[t2], token=True)
            toks.append(tv)
        t3 = sch.dma("gpsimd", cci[round_idx].ap()[0:1, 0:nvals],
                     rowbank[0:1, 4096:4096 + nvals], waits=toks)
        t_cc = sch.cc((lambda ri, nv: (lambda e: e.collective_compute(
            "AllReduce", ALU.max, replica_groups=[list(range(8))],
            ins=[cci[ri].ap()[0:1, 0:nv].opt()],
            outs=[cco[ri].ap()[0:1, 0:nv].opt()])))(round_idx, nvals),
            waits=[t3])
        t4 = sch.dma("gpsimd", gsc[:, 10:10 + nvals],
                     cco[round_idx].ap()[0:1, 0:nvals]
                     .partition_broadcast(P), waits=[t_cc])
        return t4

    def make_scale(t_g, k, raw_col, pre=1.0):
        t_s = sch.op("vector", (lambda k_, rc, p_: (lambda e: e.tensor_scalar(
            gsc[:, 2 * k_:2 * k_ + 1], gsc[:, rc:rc + 1], p_ / QMAX, 1e-12,
            ALU.mult, ALU.max)))(k, raw_col, pre), waits=[t_g], token=True)
        sch.op("scalar", (lambda k_: (lambda e: e.activation(
            gsc[:, 12:13], gsc[:, 2 * k_:2 * k_ + 1], AF.Ln,
            bias=0.0, scale=1.0)))(k), waits=[t_s])
        t_r0 = sch.op("scalar", (lambda k_: (lambda e: e.activation(
            gsc[:, 13:14], gsc[:, 12:13], AF.Exp,
            bias=0.0, scale=-1.0)))(k), token=True)
        sch.op("vector", (lambda k_: (lambda e: e.tensor_tensor(
            gsc[:, 14:15], gsc[:, 2 * k_:2 * k_ + 1],
            gsc[:, 13:14], ALU.mult)))(k), waits=[t_r0])
        sch.op("vector", lambda e: e.drain())
        sch.op("vector", lambda e: e.tensor_tensor(
            gsc[:, 15:16], gsc[:, 16:17], gsc[:, 14:15], ALU.subtract))
        sch.op("vector", lambda e: e.drain())
        t_fin = sch.op("vector", (lambda k_: (lambda e: e.tensor_tensor(
            gsc[:, 2 * k_ + 1:2 * k_ + 2], gsc[:, 13:14], gsc[:, 15:16],
            ALU.mult)))(k), token=(pre == 1.0))
        if pre != 1.0:
            sch.op("vector", (lambda p_: (lambda e: e.memset(
                gsc[:, 17:18], p_)))(pre))
            sch.op("vector", lambda e: e.drain())
            t_fin = sch.op("vector", (lambda k_: (lambda e: e.tensor_tensor(
                gsc[:, 2 * k_ + 1:2 * k_ + 2],
                gsc[:, 2 * k_ + 1:2 * k_ + 2],
                gsc[:, 17:18], ALU.mult)))(k), token=True)
        return t_fin

    def qdq_pair(src_ap, dst_ap, k, waits, token=False):
        sch.op("vector", (lambda s_, k_: (lambda e: e.tensor_scalar(
            fscr[:, 0:s_ap_w(s_)], s_, gsc[:, 2 * k_ + 1:2 * k_ + 2], MAGIC,
            ALU.mult, ALU.add)))(src_ap, k), waits=waits)
        return sch.op("vector", (lambda s_, d_, k_: (lambda e: e.tensor_scalar(
            d_, fscr[:, 0:s_ap_w(s_)], MAGIC, gsc[:, 2 * k_:2 * k_ + 1],
            ALU.subtract, ALU.mult)))(src_ap, dst_ap, k), token=token)

    def s_ap_w(ap):
        return ap.shape[-1]

    t_csum = t_csum0
    with ExitStack() as ph:
        xs0 = ph.enter_context(nc.sbuf_tensor("xs0", [P, S], BF16))
        xs1 = ph.enter_context(nc.sbuf_tensor("xs1", [P, S], BF16))
        bufs = [xs0, xs1]

        def get_chunk(c, reuse):
            b = bufs[c % 2]
            w = [reuse] if reuse is not None else []
            t = sch.dma("gpsimd", b[:], xT_d[c * P:(c + 1) * P, :], waits=w)
            toks = [t] + ([t_csum] if c == 0 else [])
            return b[:], toks

        abrep, t_ab, cons = ln_stats(ph, get_chunk)

        prev = [cons[NC - 2], cons[NC - 1]]
        for c in range(NC):
            b = bufs[c % 2]
            w = [prev[c % 2]] if prev[c % 2] is not None else []
            t_l = sch.dma("gpsimd", b[:], xT_d[c * P:(c + 1) * P, :],
                          waits=w)
            sch.op("vector", (lambda b_, ab_: (lambda e: e.tensor_tensor(
                fscr[:], b_[:], ab_[:, 0:S], ALU.mult)))(b, abrep),
                waits=[t_l, t_ab])
            prev[c % 2] = sch.op("vector", (lambda c_, ab_: (lambda e: e.tensor_tensor(
                A[c_][:], fscr[:], ab_[:, S:2 * S], ALU.add)))(c, abrep),
                token=True)
        sch.barrier()

    big = ExitStack()
    qsb = [big.enter_context(nc.sbuf_tensor(f"q{c}", [P, S], BF16))
           for c in range(NC)]
    ksb = [big.enter_context(nc.sbuf_tensor(f"k{c}", [P, S], BF16))
           for c in range(NC)]
    vsb = [big.enter_context(nc.sbuf_tensor(f"v{t}", [P, H * 65], BF16))
           for t in range(NT)]
    with ExitStack() as ph:
        wfull = [ph.enter_context(nc.sbuf_tensor(f"wf{c}", [P, D], BF16))
                 for c in range(NC)]
        pj = [ph.enter_context(nc.psum_tensor(f"pj{i}", [P, QW], F32))
              for i in range(4)]
        t_vones = None
        for t in range(NT):
            t_vones = sch.op("vector", (lambda t_: (lambda e: e.memset(
                vsb[t_].ap().rearrange("p (h x) -> p h x", x=65)[:, :, 64:65],
                1.0)))(t), token=(t == NT - 1))
        bank_free = [None] * 4

        wslab = [ph.enter_context(nc.sbuf_tensor(f"wsl{i}", [P, D], F32))
                 for i in range(2)]
        wsl2 = ph.enter_context(nc.sbuf_tensor("wsl2", [P, D], F32))
        wqb = [ph.enter_context(nc.sbuf_tensor(f"wqb{i}", [P, D], BF16))
               for i in range(2)]
        Ssc = ph.enter_context(nc.sbuf_tensor("Ssc", [P, 32], F32))
        Sd = ph.enter_context(nc.sbuf_tensor("Sd", [P, 32], F32))
        Rsc = ph.enter_context(nc.sbuf_tensor("Rsc", [P, 32], F32))
        Rs2 = ph.enter_context(nc.sbuf_tensor("Rs2", [P, 32], F32))
        RsB = ph.enter_context(nc.sbuf_tensor("RsB", [P, 32], F32))
        tstg = [ph.enter_context(nc.sbuf_tensor(f"tstg{i}", [P, 512], BF16))
                for i in range(2)]
        psT = [ph.enter_context(nc.psum_tensor(f"psT{i}", [P, QW], BF16))
               for i in range(2)]
        deq_spills = []

        def b3(t):
            return t.ap()[:, 0:D].rearrange("p (a b) -> p a b", b=32)

        def deq_gen():
            t_q_hist = [None, None]
            t_psT_free = [None, None]
            t_stg_free = [None, None]
            t_tr_bypar = [None, None]
            units = []
            for src_d, dst_hbm, key, nsl, ncof in (
                    (wu_d, wuT_d, "wuT", NI, 1),
                    (wg_d, wgT_d, "wgT", NI, 1),
                    (wd_d, wdT_d, "wdT", NC, 3)):
                for widx in range(nsl):
                    for coff in range(ncof):
                        units.append((src_d, dst_hbm, key, widx, coff,
                                      ncof))
            prev = None
            for ui, unit in enumerate(units + [None, None]):
                if unit is not None:
                    src_d, dst_hbm, key, widx, coff, ncof = unit
                    sl = wslab[ui % 2]
                    w = ([t_q_hist[ui % 2]]
                         if t_q_hist[ui % 2] is not None else [])
                    t_l = sch.dma("sync", sl[:],
                                  src_d[widx * P:(widx + 1) * P,
                                        coff * D:(coff + 1) * D], waits=w)
                    sch.op("vector", (lambda s_: (lambda e: e.tensor_reduce(
                        Ssc[:], b3(s_), AX.X, ALU.max,
                        apply_absolute_value=True)))(sl), waits=[t_l])
                    sch.op("vector", lambda e: e.drain())
                    t_ss = sch.op("vector", lambda e: e.tensor_scalar(
                        Sd[:], Ssc[:], 1.0 / 7.0, 1e-12,
                        ALU.mult, ALU.max), token=True)
                    sch.op("scalar", lambda e: e.activation(
                        Rs2[:], Sd[:], AF.Ln, bias=0.0, scale=1.0),
                        waits=[t_ss])
                    t_r0 = sch.op("scalar", lambda e: e.activation(
                        Rsc[:], Rs2[:], AF.Exp, bias=0.0, scale=-1.0),
                        token=True)
                    sch.op("vector", lambda e: e.tensor_tensor(
                        Rs2[:], Sd[:], Rsc[:], ALU.mult), waits=[t_r0])
                    sch.op("vector", lambda e: e.drain())
                    sch.op("vector", lambda e: e.tensor_scalar(
                        RsB[:], Rs2[:], -1.0, 2.0, ALU.mult, ALU.add))
                    sch.op("vector", lambda e: e.drain())
                    sch.op("vector", lambda e: e.tensor_tensor(
                        Rsc[:], Rsc[:], RsB[:], ALU.mult))
                    sch.op("vector", lambda e: e.drain())
                    sch.op("vector", (lambda s_: (lambda e: e.tensor_tensor(
                        b3(wsl2), b3(s_),
                        Rsc[:, :, None].broadcast_to([P, 32, 32]),
                        ALU.mult)))(sl))
                    sch.op("vector", (lambda s_: (lambda e: e.tensor_scalar(
                        s_[:], wsl2[:], MAGIC, MAGIC,
                        ALU.add, ALU.subtract)))(sl))
                    t_q = sch.op("vector", (lambda s_, qb_: (lambda e:
                                 e.tensor_tensor(
                        b3(qb_), b3(s_),
                        Sd[:, :, None].broadcast_to([P, 32, 32]),
                        ALU.mult)))(sl, wqb[ui % 2]),
                        waits=[t_tr_bypar[ui % 2]]
                        if t_tr_bypar[ui % 2] is not None else (),
                        token=True)
                    t_q_hist[ui % 2] = t_q
                    nxt = (ui, t_q, dst_hbm, key,
                           (widx * ncof + coff) * 2)
                else:
                    nxt = None
                if prev is not None:
                    pui, t_q, dst_hbm, key, bib = prev
                    for jg in range(2):
                        pT = psT[jg]
                        stg = tstg[jg]
                        for jj in range(4):
                            j = 4 * jg + jj
                            w = [t_q] if jj == 0 else []
                            if jj == 0 and t_psT_free[jg] is not None:
                                w.append(t_psT_free[jg])
                            sch.op("tensor", (lambda j_, jj_, pT_, qb_:
                                              (lambda e: e.transpose(
                                pT_[:, jj_ * P:(jj_ + 1) * P],
                                qb_[:, j_ * P:(j_ + 1) * P], idn[:])))(
                                    j, jj, pT, wqb[pui % 2]),
                                waits=w + ([t_id] if jg == 0 and pui == 0
                                           else []))
                        t_tr = sch.op("tensor", lambda e: e.drain(),
                                      token=True)
                        t_tr_bypar[pui % 2] = t_tr
                        wev = [t_tr] + ([t_stg_free[jg]]
                                        if t_stg_free[jg] is not None
                                        else [])
                        t_ev = sch.op("scalar", (lambda pT_, stg_:
                                      (lambda e: e.copy(
                            stg_[:], pT_[:])))(pT, stg),
                            waits=wev, token=True)
                        t_psT_free[jg] = t_ev
                        bi = bib + jg
                        t_stg_free[jg] = sch.dma(
                            "sync", dst_hbm[bi * P:(bi + 1) * P, :],
                            stg[:], waits=[t_ev])
                        spill.setdefault(key, []).append(t_stg_free[jg])
                        deq_spills.append(t_stg_free[jg])
                prev = nxt
                yield

        deq = deq_gen()
        deq_n = [0, 0]

        def pace():
            deq_n[0] += 1
            tgt = min(72, deq_n[0] * 72 // 96 + 1)
            while deq_n[1] < tgt:
                if next(deq, "END") == "END":
                    break
                deq_n[1] += 1

        for mi, (wd, dst) in enumerate(((wqT_d, qsb), (wkT_d, ksb))):
            wl = ([t for t in bank_free if t is not None] if mi else [])
            wtoks = [sch.dma("gpsimd", wfull[c][:], wd[c * P:(c + 1) * P, :],
                             waits=wl if c == 0 else ())
                     for c in range(NC)]
            ti = 0
            for oc in range(NC):
                for n in range(NQ):
                    bk = ti % 4
                    ti += 1
                    pt = pj[bk]
                    for kc in range(NC):
                        w = []
                        if kc == 0:
                            w = list(wtoks)
                            if bank_free[bk] is not None:
                                w.append(bank_free[bk])
                        sch.op("tensor", (lambda kc_, oc_, n_, pt_, st_, sp_:
                                          (lambda e: e.matmul(
                            pt_[:], wfull[kc_][:, oc_ * P:(oc_ + 1) * P],
                            A[kc_][:, n_ * QW:(n_ + 1) * QW],
                            start=st_, stop=sp_)))(
                                kc, oc, n, pt, kc == 0, kc == NC - 1),
                            waits=w)
                    t_mm = sch.op("tensor", lambda e: e.drain(),
                                  token=True)
                    if mi == 0:
                        bank_free[bk] = sch.op(
                            "vector", (lambda d_, oc_, n_, pt_: (lambda e:
                                       e.tensor_scalar(
                                d_[oc_][:, n_ * QW:(n_ + 1) * QW], pt_[:],
                                ATT_SCALE, None, ALU.mult)))(dst, oc, n, pt),
                            waits=[t_mm], token=True)
                    else:
                        bank_free[bk] = sch.op(
                            "scalar", (lambda d_, oc_, n_, pt_: (lambda e:
                                       e.copy(
                                d_[oc_][:, n_ * QW:(n_ + 1) * QW], pt_[:])))(
                                    dst, oc, n, pt),
                            waits=[t_mm], token=True)
                    pace()
        wtoks = [sch.dma("gpsimd", wfull[c][:], wvT_d[c * P:(c + 1) * P, :],
                         waits=[bank_free[i] for i in range(4)] if c == 0
                         else ())
                 for c in range(NC)]
        ti = 0
        for t in range(NT):
            for half in range(2):
                bk = ti % 4
                ti += 1
                pt = pj[bk]
                for kc in range(NC):
                    w = []
                    if kc == 0:
                        w = list(wtoks)
                        if bank_free[bk] is not None:
                            w.append(bank_free[bk])
                    sch.op("tensor", (lambda kc_, t_, h_, pt_, st_, sp_:
                                      (lambda e: e.matmul(
                        pt_[:], A[kc_][:, t_ * P:(t_ + 1) * P],
                        wfull[kc_][:, h_ * QW:(h_ + 1) * QW],
                        start=st_, stop=sp_)))(
                            kc, t, half, pt, kc == 0, kc == NC - 1),
                        waits=w)
                t_mm = sch.op("tensor", lambda e: e.drain(), token=True)
                bank_free[bk] = sch.op(
                    "scalar", (lambda t_, h_, pt_: (lambda e: e.copy(
                        vsb[t_].ap().rearrange("p (hh x) -> p hh x", x=65)[
                            :, h_ * 8:(h_ + 1) * 8, 0:64],
                        pt_.ap().rearrange("p (hh x) -> p hh x", x=64))))(
                            t, half, pt),
                    waits=[t_mm, t_vones], token=True)
                pace()
        while next(deq, "END") != "END":
            pass
        mxs = {}
        for tk in deq_spills:
            mxs[tk.sem] = max(mxs.get(tk.sem, 0), tk.val)
        sch.op("sync", None, waits=[Tok(s_, v_) for s_, v_ in mxs.items()])
        sch.barrier()

    with ExitStack() as ph:
        ttsb = [ph.enter_context(nc.sbuf_tensor(f"tts{i}", [P, 3968], BF16))
                for i in range(4)]
        esb = [ph.enter_context(nc.sbuf_tensor(f"esb{i}", [P, 2 * QW], BF16))
               for i in range(3)]
        rrep = [ph.enter_context(nc.sbuf_tensor(f"rrep{i}", [HD, QW], F32))
                for i in range(2)]
        oscr = [ph.enter_context(nc.sbuf_tensor(f"oscr{i}", [HD, QW], F32))
                for i in range(2)]
        obias = [ph.enter_context(nc.sbuf_tensor(f"obias{i}", [HD, QW], F32))
                 for i in range(2)]
        ostg = ph.enter_context(nc.sbuf_tensor("ostg", [HD, QW], BF16))
        psS = [ph.enter_context(nc.psum_tensor(f"psS{i}", [P, 2 * QW],
                                               F32)) for i in range(2)]
        psUE = [ph.enter_context(nc.psum_tensor(f"psUE{h}", [P, QW], F32))
                for h in range(2)]
        psUB = [ph.enter_context(nc.psum_tensor(f"psUB{h}", [P, QW], F32))
                for h in range(2)]

        qzb = [[ph.enter_context(nc.sbuf_tensor(f"qz{i}{j}", [P, QW], BF16))
                for j in range(2)] for i in range(2)]
        t_qzz = sch.op("vector", lambda e: (
            e.memset(qzb[0][0][HD:P, :], 0.0)), token=False)
        sch.op("vector", lambda e: e.memset(qzb[0][1][0:HD, :], 0.0))
        sch.op("vector", lambda e: e.memset(qzb[1][0][HD:P, :], 0.0))
        t_qzz = sch.op("vector", lambda e: e.memset(qzb[1][1][0:HD, :], 0.0),
                       token=True)
        qz_free = [None, None]
        strip_tok = [None, None, None, None]
        strip_tok[0] = sch.dma("gpsimd", ttsb[0][:], tt_d[0])
        strip_tok[1] = sch.dma("gpsimd", ttsb[1][:], tt_d[1])
        pair_done = [None, None]
        t_exp = [None, None]
        t_av = [None] * 4
        ue_free = [None, None]
        ub_free = [None, None]
        oscr_free = [None, None]
        obias_free = [None, None]
        rrep_free = [None, None]
        t_ostg_dma = None
        rrow_free = [None] * 4
        fscr_free = [None] * 4

        for pr in range(8):
            par = pr % 2
            cur = [ttsb[par * 2], ttsb[par * 2 + 1]]
            cur_toks = [strip_tok[par * 2], strip_tok[par * 2 + 1]]
            if pr < 7:
                npar = (pr + 1) % 2
                w = [pair_done[npar]] if pair_done[npar] is not None else []
                strip_tok[npar * 2] = sch.dma(
                    "gpsimd", ttsb[npar * 2][:], tt_d[2 * pr + 2], waits=w)
                strip_tok[npar * 2 + 1] = sch.dma(
                    "gpsimd", ttsb[npar * 2 + 1][:], tt_d[2 * pr + 3])
            def emit_qz(pq, qt):
                qp = qt % 2
                w = [qz_free[qp]] if qz_free[qp] is not None else [t_qzz]
                return [
                    sch.op("scalar", (lambda pr_, qt_, qp_: (lambda e:
                           e.copy(
                        qzb[qp_][0][0:HD, :],
                        qsb[pr_][0:HD, qt_ * QW:(qt_ + 1) * QW])))(
                            pq, qt, qp),
                        waits=w, token=True),
                    sch.op("scalar", (lambda pr_, qt_, qp_: (lambda e:
                           e.copy(
                        qzb[qp_][1][HD:P, :],
                        qsb[pr_][HD:P, qt_ * QW:(qt_ + 1) * QW])))(
                            pq, qt, qp),
                        token=True)]

            if pr == 0:
                t_qz_next = emit_qz(0, 0)
            for qt in range(NQ):
                def emit_av(kc, t_e, want_token=False):
                    eb = esb[kc % 3]
                    for hh in range(2):
                        w = [t_e] if hh == 0 else []
                        if kc == 0:
                            if ue_free[hh] is not None:
                                w.append(ue_free[hh])
                            if ub_free[hh] is not None:
                                w.append(ub_free[hh])
                            w.extend(cur_toks)
                        col = (2 * pr + hh) * 65
                        ms = qt * QW - kc * P + 1920
                        sch.op("tensor", (lambda kc_, hh_, st_, sp_, b_,
                                          col_: (lambda e: e.matmul(
                            psUE[hh_][0:65, :],
                            vsb[kc_][:, col_:col_ + 65],
                            b_[:, hh_ * QW:(hh_ + 1) * QW],
                            start=st_, stop=sp_)))(
                                kc, hh, kc == 0, kc == 15, eb, col),
                            waits=w)
                        sch.op("tensor", (lambda kc_, hh_, st_, sp_, ms_, c_,
                                          col_: (lambda e: e.matmul(
                            psUB[hh_][0:65, :],
                            vsb[kc_][:, col_:col_ + 65],
                            c_[:, ms_:ms_ + QW],
                            start=st_, stop=sp_)))(
                                kc, hh, kc == 0, kc == 15, ms, cur[hh], col),
                            waits=[])
                    if want_token:
                        return sch.op("tensor", lambda e: e.drain(),
                                      token=True)
                    return None

                qp = qt % 2
                t_qz = t_qz_next
                pend = []
                for kc in range(16):
                    p2 = kc % 2
                    ps = psS[p2]
                    for hh in range(2):
                        w = []
                        if hh == 0 and t_exp[p2] is not None:
                            w.append(t_exp[p2])
                        if kc == 0 and hh == 0:
                            w.extend(t_qz)
                        t_qk = sch.op("tensor", (lambda pr_, kc_, hh_,
                                          qz_, ps_: (lambda e: e.matmul(
                            ps_[:, hh_ * QW:(hh_ + 1) * QW],
                            ksb[pr_][:, kc_ * P:(kc_ + 1) * P],
                            qz_[:, 0:QW],
                            start=True, stop=True)))(pr, kc, hh,
                                                     qzb[qp][hh], ps),
                            waits=w, token=(hh == 1))
                    qz_free[qp] = t_qk
                    if pend:
                        kcp, tep = pend.pop(0)
                        emit_av(kcp, tep)
                    w = [t_qk]
                    t_exp[p2] = sch.op(
                        "scalar", (lambda b_, ps_: (lambda e: e.activation(
                            b_[:], ps_[:], AF.Exp, bias=0.0,
                            scale=1.0)))(esb[kc % 3], ps),
                        waits=w, token=True)
                    pend.append((kc, t_exp[p2]))
                if qt < NQ - 1:
                    t_qz_next = emit_qz(pr, qt + 1)
                elif pr < 7:
                    t_qz_next = emit_qz(pr + 1, 0)
                t_pv = emit_av(*pend.pop(0), want_token=True)

                for hh in range(2):
                    slot = (qt * 2 + hh) % 4
                    wln = [t_pv]
                    if rrow_free[slot] is not None:
                        wln.append(rrow_free[slot])
                    sch.op("scalar", (lambda hh_, sl: (lambda e: e.activation(
                        rowbank[64:65, sl * QW:(sl + 1) * QW], psUE[hh_][64:65, :],
                        AF.Ln, bias=0.0, scale=1.0)))(hh, slot),
                        waits=wln)
                    w = [oscr_free[hh]] if oscr_free[hh] is not None else []
                    ue_free[hh] = sch.op(
                        "scalar", (lambda hh_: (lambda e: e.copy(
                            oscr[hh_][:], psUE[hh_][0:64, :])))(hh),
                        waits=w, token=True)
                    w = [obias_free[hh]] if obias_free[hh] is not None else []
                    ub_free[hh] = sch.op(
                        "scalar", (lambda hh_: (lambda e: e.copy(
                            obias[hh_][:], psUB[hh_][0:64, :])))(hh),
                        waits=w, token=True)
                    t_r = sch.op("scalar", (lambda sl: (lambda e: e.activation(
                        rowbank[64:65, sl * QW:(sl + 1) * QW],
                        rowbank[64:65, sl * QW:(sl + 1) * QW],
                        AF.Exp, bias=0.0, scale=-1.0)))(slot), token=True)
                    t_w = sch.dma("sync", rrow_d.ap()[slot:slot+1], rowbank[64:65, slot * QW:(slot + 1) * QW],
                        waits=[t_r])
                    rrow_free[slot] = t_w
                    wb = [t_w]
                    if rrep_free[hh] is not None:
                        wb.append(rrep_free[hh])
                    t_rr = sch.dma("sync", rrep[hh][:],
                                   rrow_d.ap()[slot:slot+1]
                                   .partition_broadcast(HD), waits=wb)
                    fslot = fscr[0:64, slot * QW:(slot + 1) * QW]
                    w = [t_rr, ue_free[hh]]
                    if fscr_free[slot] is not None:
                        w.append(fscr_free[slot])
                    t_mul = sch.op(
                        "vector", (lambda hh_, fs_: (lambda e: e.tensor_tensor(
                            fs_, oscr[hh_][:], rrep[hh_][:],
                            ALU.mult)))(hh, fslot),
                        waits=w, token=True)
                    rrep_free[hh] = t_mul
                    oscr_free[hh] = t_mul
                    if hh == 0:
                        t_o = sch.op(
                            "vector", (lambda pr_, qt_, fs_: (lambda e:
                                       e.tensor_tensor(
                                A[pr_][0:64, qt_ * QW:(qt_ + 1) * QW],
                                fs_, obias[0][:], ALU.add)))(
                                    pr, qt, fslot),
                            waits=[ub_free[0]], token=True)
                    else:
                        w2 = [ub_free[1], t_ostg_dma] if t_ostg_dma is not None \
                            else [ub_free[1]]
                        t_o = sch.op(
                            "vector", (lambda fs_: (lambda e: e.tensor_tensor(
                                ostg[:], fs_, obias[1][:],
                                ALU.add)))(fslot),
                            waits=w2, token=True)
                        t_ostg_dma = sch.dma(
                            "sync", A[pr][64:128, qt * QW:(qt + 1) * QW],
                            ostg[:], waits=[t_o])
                    obias_free[hh] = t_o
                    fscr_free[slot] = t_o
            pair_done[par] = sch.op("tensor", lambda e: e.drain(),
                                    token=True)
        t_attn_out = t_ostg_dma
        sch.barrier()
    big.close()
    late = ExitStack()
    X2 = [late.enter_context(nc.sbuf_tensor(f"X2{c}", [P, S], BF16))
          for c in range(NC)]
    x2_spill = []

    with ExitStack() as ph:
        abrep, t_ab, _ = ln_stats(
            ph, lambda c, r: (A[c][:], [t_attn_out]))
        t_prev = None
        for c in range(NC):
            sch.op("vector", (lambda c_, ab_: (lambda e: e.tensor_tensor(
                fscr[:], A[c_][:], ab_[:, 0:S], ALU.mult)))(c, abrep),
                waits=[t_ab] if c == 0 else ())
            sch.op("vector", (lambda ab_: (lambda e: e.tensor_tensor(
                fscr[:], fscr[:], ab_[:, S:2 * S], ALU.add)))(abrep))
            t_prev = sch.op("vector", (lambda c_: (lambda e: e.tensor_scalar(
                X2[c_][:], fscr[:], gam[:, c_:c_ + 1], bet[:, c_:c_ + 1],
                ALU.mult, ALU.add)))(c),
                waits=gb_toks if c == 0 else (), token=True)
        sch.barrier()
    with ExitStack() as ph:
        abrep, t_ab, _ = ln_stats(ph, lambda c, r: (X2[c][:], []))
        xb0 = ph.enter_context(nc.sbuf_tensor("xb0", [P, S], BF16))
        xb1 = ph.enter_context(nc.sbuf_tensor("xb1", [P, S], BF16))
        prev = [None, None]
        for c in range(NC):
            b = [xb0, xb1][c % 2]
            w = [prev[c % 2]] if prev[c % 2] is not None else []
            t_l = sch.dma("gpsimd", b[:], xT_d[c * P:(c + 1) * P, :], waits=w)
            sch.op("vector", (lambda c_, ab_: (lambda e: e.tensor_tensor(
                fscr[:], X2[c_][:], ab_[:, 0:S], ALU.mult)))(c, abrep),
                waits=[t_ab] if c == 0 else ())
            sch.op("vector", (lambda ab_: (lambda e: e.tensor_tensor(
                fscr[:], fscr[:], ab_[:, S:2 * S], ALU.add)))(abrep))
            prev[c % 2] = sch.op("vector", (lambda c_, b_: (lambda e:
                                 e.tensor_tensor(
                X2[c_][:], fscr[:], b_[:], ALU.add)))(c, b),
                waits=[t_l], token=True)
        sch.barrier()

    with ExitStack() as ph:
        abrep, t_ab, _ = ln_stats(ph, lambda c, r: (X2[c][:], []))
        t_mz = sch.op("vector", lambda e: e.memset(lmax[:, 0:2], 0.0),
                      token=True)
        t_prev = t_mz
        for c in range(NC):
            sch.op("vector", (lambda c_, ab_: (lambda e: e.tensor_tensor(
                fscr[:], X2[c_][:], ab_[:, 0:S], ALU.mult)))(c, abrep),
                waits=[t_ab, t_mz] if c == 0 else ())
            sch.op("vector", (lambda c_, ab_: (lambda e: e.tensor_tensor(
                A[c_][:], fscr[:], ab_[:, S:2 * S], ALU.add)))(c, abrep))
            sch.op("vector", (lambda c_: (lambda e: e.tensor_reduce(
                lmax[:, 2:3], A[c_][:], AX.X, ALU.max,
                apply_absolute_value=True)))(c))
            sch.op("vector", lambda e: e.drain())
            t_prev = sch.op("vector", lambda e: e.tensor_tensor(
                lmax[:, 0:1], lmax[:, 0:1], lmax[:, 2:3], ALU.max),
                token=True)
            x2_spill.append(sch.dma(
                "sync", x2_d.ap()[c * P:(c + 1) * P, :], X2[c][:],
                waits=[t_prev]))
        t_g = fold_partitions_maxes(t_prev, 0, [0])
        t_s1 = make_scale(t_g, 0, 10)
        if dbg2_d is not None:
            sch.dma("sync", dbg2_d[:, 0:16], gsc[:, 0:16], waits=[t_s1])
        for c in range(NC):
            tq_ = qdq_pair(A[c][:], A[c][:], 0, [t_s1] if c == 0 else [],
                           token=True)
            if xq_d is not None:
                sch.dma("sync", xq_d[c * P:(c + 1) * P, :], A[c][:],
                        waits=[tq_])
        sch.op("sync", None, waits=x2_spill)
        sch.barrier()
    late.close()

    big2 = ExitStack()
    wuT = big2.enter_context(nc.sbuf_tensor("wuT", [P, 48 * 512], BF16))
    wgT = big2.enter_context(nc.sbuf_tensor("wgT", [P, 48 * 512], BF16))

    def ug_stat(wT, kc, oc):
        off = oc * 1024 + (kc // 4) * 512 + (kc % 4) * P
        return wT[:, off:off + P]
    with ExitStack() as ph:
        evb = [ph.enter_context(nc.sbuf_tensor(f"ev{i}", [P, S], BF16))
               for i in range(2)]
        pj2 = [ph.enter_context(nc.psum_tensor(f"pj2{i}", [P, QW], F32))
               for i in range(4)]
        b0s = [ph.enter_context(nc.sbuf_tensor(f"s1b{i}", [P, S], BF16))
               for i in range(2)]
        fscr2 = ph.enter_context(nc.sbuf_tensor("fscr2", [P, S], F32))
        fscrC = ph.enter_context(nc.sbuf_tensor("fscrC", [P, S], F32))
        tg2b = ph.enter_context(nc.sbuf_tensor("tg2b", [P, S], BF16))
        t_mz = sch.op("vector", lambda e: e.memset(lmax[:], 0.0),
                      token=True)
        wg_toks = [sch.dma("sync", wgT[:, b * 512:(b + 1) * 512],
                           wgT_d.ap()[b * P:(b + 1) * P, :],
                           waits=spill["wgT"] if b == 0 else ())
                   for b in range(48)]
        wu_toks = [sch.dma("sync", wuT[:, b * 512:(b + 1) * 512],
                           wuT_d.ap()[b * P:(b + 1) * P, :],
                           waits=spill["wuT"] if b == 0 else ())
                   for b in range(48)]
        bank_free = [None] * 4
        ev_free = [None, None]

        def mm_group(wT, oc, first_toks, sp_d, key, acol, scol):
            eb = evb[oc % 2]
            for n in range(NQ):
                pt = pj2[n]
                for kc in range(NC):
                    w = []
                    if kc == 0 and bank_free[n] is not None:
                        w.append(bank_free[n])
                    if first_toks is not None and n == 0 and kc == 0:
                        w.extend(first_toks)
                    sch.op("tensor", (lambda wT_, kc_, oc_, n_, pt_, st_,
                                      sp_: (lambda e: e.matmul(
                        pt_[:], ug_stat(wT_, kc_, oc_),
                        A[kc_][:, n_ * QW:(n_ + 1) * QW],
                        start=st_, stop=sp_)))(
                            wT, kc, oc, n, pt, kc == 0, kc == NC - 1),
                        waits=w)
            t_mm = sch.op("tensor", lambda e: e.drain(), token=True)
            t_ev = None
            for n in range(NQ):
                w = [t_mm] if n == 0 else []
                if n == 0 and ev_free[oc % 2] is not None:
                    w.append(ev_free[oc % 2])
                t_ev = sch.op("scalar", (lambda b_, n_, pt_: (lambda e:
                                         e.copy(
                    b_[:, n_ * QW:(n_ + 1) * QW], pt_[:])))(
                        eb, n, pj2[n]),
                    waits=w, token=True)
                bank_free[n] = t_ev
            sch.op("vector", (lambda b_, sc: (lambda e: e.tensor_reduce(
                lmax[:, sc:sc + 1], b_[:], AX.X, ALU.max,
                apply_absolute_value=True)))(eb, scol),
                waits=[t_ev, t_mz])
            sch.op("vector", lambda e: e.drain())
            t_mx = sch.op("vector", (lambda ac, sc: (lambda e:
                          e.tensor_tensor(
                lmax[:, ac:ac + 1], lmax[:, ac:ac + 1], lmax[:, sc:sc + 1],
                ALU.max)))(acol, scol), token=True)
            ev_free[oc % 2] = sch.dma(
                "sync", sp_d.ap()[oc * P:(oc + 1) * P, :], eb[:],
                waits=[t_ev, t_mx])
            spill[(key, oc)] = ev_free[oc % 2]
            return t_mx, t_mm

        for oc in range(NI):
            t_mx_g, _ = mm_group(wgT, oc, wg_toks if oc == 0 else None,
                                 gt_d, "gt", 0, 2)
        t_g3 = fold_partitions_maxes(t_mx_g, 1, [0])
        t_s3 = make_scale(t_g3, 2, 10)

        prev = [None, None]
        t_tg_free = None
        for oc in range(NI):
            t_mx_u, _ = mm_group(wuT, oc, wu_toks if oc == 0 else None,
                                 up_d, "up", 1, 2)
            b0 = b0s[oc % 2]
            w = [prev[oc % 2]] if prev[oc % 2] is not None else []
            w.append(spill[("gt", oc)])
            t_l = sch.dma("sync", b0[:], gt_d.ap()[oc * P:(oc + 1) * P, :],
                          waits=w)
            sch.op("vector", (lambda b_: (lambda e: e.tensor_scalar(
                fscrC[:], b_[:], gsc[:, 5:6], MAGIC, ALU.mult, ALU.add)))(b0),
                waits=[t_l, t_s3] if oc == 0 else [t_l])
            prev[oc % 2] = sch.op("vector", lambda e: e.tensor_scalar(
                fscr[:], fscrC[:], MAGIC, gsc[:, 4:5], ALU.subtract, ALU.mult),
                token=True)
            t_gq = prev[oc % 2]
            t_sig = sch.op("scalar", lambda e: e.activation(
                fscr2[:], fscr[:], AF.Sigmoid, bias=0.0, scale=1.0),
                waits=[t_gq], token=True)
            sch.op("vector", lambda e: e.tensor_scalar(
                fscrC[:], fscr2[:], 65536.0, MAGIC, ALU.mult, ALU.add),
                waits=[t_sig])
            sch.op("vector", lambda e: e.tensor_scalar(
                fscr2[:], fscrC[:], MAGIC, 65535.0, ALU.subtract, ALU.min))
            w = [t_tg_free] if t_tg_free is not None else []
            t_t2 = sch.op("vector", lambda e: e.tensor_tensor(
                tg2b[:], fscr[:], fscr2[:], ALU.mult), waits=w, token=True)
            sch.op("vector", lambda e: e.tensor_reduce(
                lmax[:, 3:4], tg2b[:], AX.X, ALU.max,
                apply_absolute_value=True))
            sch.op("vector", lambda e: e.drain())
            t_mx_s1 = sch.op("vector", lambda e: e.tensor_tensor(
                lmax[:, 0:1], lmax[:, 0:1], lmax[:, 3:4], ALU.max),
                token=True)
            t_tg_free = sch.dma("sync", or_d.ap()[oc * P:(oc + 1) * P, :],
                                tg2b[:], waits=[t_t2, t_mx_s1])
            spill[("or", oc)] = t_tg_free
        t_g24 = fold_partitions_maxes(t_mx_s1, 2, [1, 0])
        t_s2 = make_scale(t_g24, 1, 10)
        t_s4 = make_scale(t_g24, 3, 11, pre=1.0 / 65536.0)
        sch.barrier()
    big2.close()

    with ExitStack() as ph:
        wdT = ph.enter_context(nc.sbuf_tensor("wdT", [P, 48 * 512], BF16))

        def wd_stat(icx, oc):
            off = ((oc * 3 + icx // 8) * 2 + (icx % 8) // 4) * 512 \
                + (icx % 4) * P
            return wdT[:, off:off + P]
        b0s = [ph.enter_context(nc.sbuf_tensor(f"s2b{i}", [P, S], BF16))
               for i in range(2)]
        upb = [ph.enter_context(nc.sbuf_tensor(f"upb{i}", [P, S], BF16))
               for i in range(2)]
        fscr2 = ph.enter_context(nc.sbuf_tensor("fscr2b", [P, S], F32))
        fscrC = ph.enter_context(nc.sbuf_tensor("fscrCb", [P, S], F32))
        ph9_out = ph.enter_context(nc.sbuf_tensor("ph9o", [P, S], BF16))
        wd_toks = [sch.dma("sync", wdT[:, b * 512:(b + 1) * 512],
                           wdT_d.ap()[b * P:(b + 1) * P, :],
                           waits=spill["wdT"] if b == 0 else ())
                   for b in range(48)]
        prev = [None, None]
        t_up_free = [None, None]
        t_out_free = None
        for oc in range(NI):
            b0 = b0s[oc % 2]
            w = [prev[oc % 2]] if prev[oc % 2] is not None else []
            w.append(spill[("or", oc)])
            t_l = sch.dma("sync", b0[:], or_d.ap()[oc * P:(oc + 1) * P, :],
                          waits=w)
            w = [t_up_free[oc % 2]] if t_up_free[oc % 2] is not None else []
            w.append(spill[("up", oc)])
            t_lu = sch.dma("sync", upb[oc % 2][:],
                           up_d.ap()[oc * P:(oc + 1) * P, :], waits=w)
            sch.op("vector", (lambda b_: (lambda e: e.tensor_scalar(
                fscrC[:], b_[:], gsc[:, 7:8], MAGIC, ALU.mult, ALU.add)))(b0),
                waits=([t_l, t_s4, t_s2] if oc == 0 else [t_l]))
            prev[oc % 2] = sch.op("vector", lambda e: e.tensor_scalar(
                fscr[:], fscrC[:], MAGIC, gsc[:, 6:7], ALU.subtract, ALU.mult),
                token=True)
            sch.op("vector", (lambda u_: (lambda e: e.tensor_scalar(
                fscrC[:], u_[:], gsc[:, 3:4], MAGIC, ALU.mult,
                ALU.add)))(upb[oc % 2]), waits=[t_lu])
            t_up_free[oc % 2] = sch.op("vector", lambda e: e.tensor_scalar(
                fscr2[:], fscrC[:], MAGIC, gsc[:, 2:3],
                ALU.subtract, ALU.mult), token=True)
            w = [t_out_free] if t_out_free is not None else []
            t_o = sch.op("vector", lambda e: e.tensor_tensor(
                ph9_out[:], fscr[:], fscr2[:], ALU.mult), waits=w, token=True)
            sch.op("vector", lambda e: e.tensor_reduce(
                lmax[:, 2:3], ph9_out[:], AX.X, ALU.max,
                apply_absolute_value=True))
            sch.op("vector", lambda e: e.drain())
            t_mx = sch.op("vector", lambda e: e.tensor_tensor(
                lmax[:, 5:6], lmax[:, 5:6], lmax[:, 2:3], ALU.max),
                token=True)
            t_out_free = sch.dma("sync", gt_d.ap()[oc * P:(oc + 1) * P, :],
                                 ph9_out[:], waits=[t_o, t_mx])
            spill[("og", oc)] = t_out_free
        t_g = fold_partitions_maxes(t_mx, 3, [5])
        t_s5 = make_scale(t_g, 4, 10)
        ob = [ph.enter_context(nc.sbuf_tensor(f"ob{i}", [P, QW], BF16))
              for i in range(NI)]
        orb = [ph.enter_context(nc.sbuf_tensor(f"orb{i}", [P, QW], BF16))
               for i in range(2)]
        pj3 = [ph.enter_context(nc.psum_tensor(f"pj3{i}", [P, QW], F32))
               for i in range(NC)]
        bank_free = [None] * NC
        ob_free = [None] * NI

        prev = [None, None]
        for n in range(NQ):
            for icx in range(NI):
                b0 = orb[icx % 2]
                w = [prev[icx % 2]] if prev[icx % 2] is not None else []
                w.append(spill[("og", icx)])
                t_l = sch.dma("sync", b0[:, 0:QW],
                              gt_d.ap()[icx * P:(icx + 1) * P,
                                        n * QW:(n + 1) * QW], waits=w)
                w = [t_l] + ([t_s5] if (n == 0 and icx == 0) else [])
                if ob_free[icx] is not None:
                    w.append(ob_free[icx])
                sch.op("vector", (lambda b_: (lambda e: e.tensor_scalar(
                    fscr[:, 0:QW], b_[:, 0:QW], gsc[:, 9:10], MAGIC,
                    ALU.mult, ALU.add)))(b0), waits=w)
                t_ov = sch.op(
                    "vector", (lambda icx_: (lambda e: e.tensor_scalar(
                        ob[icx_][:], fscr[:, 0:QW], MAGIC, gsc[:, 8:9],
                        ALU.subtract, ALU.mult)))(icx), token=True)
                prev[icx % 2] = t_ov
                for oc in range(NC):
                    w = []
                    if oc == 0:
                        w = [t_ov] + list(
                            [wd_toks[-1]] if (n == 0 and icx == 0) else [])
                    if icx == 0:
                        if bank_free[oc] is not None:
                            w.append(bank_free[oc])
                    sch.op("tensor", (lambda icx_, oc_, st_, sp_:
                                      (lambda e: e.matmul(
                        pj3[oc_][:], wd_stat(icx_, oc_),
                        ob[icx_][:], start=st_, stop=sp_)))(
                            icx, oc, icx == 0, icx == NI - 1),
                        waits=w)
                if icx % 4 == 3:
                    tok4 = sch.op("tensor", lambda e: e.drain(),
                                  token=True)
                    for i4 in range(icx - 3, icx + 1):
                        ob_free[i4] = tok4
            t_mm = ob_free[NI - 1]
            for oc in range(NC):
                t_ev = sch.op("scalar", (lambda oc_, n_: (lambda e:
                              e.copy(
                    A[oc_][:, n_ * QW:(n_ + 1) * QW], pj3[oc_][:])))(oc, n),
                    waits=[t_mm] if oc == 0 else (), token=True)
                bank_free[oc] = t_ev
                spill.setdefault(("y", oc), []).append(t_ev)
        sch.barrier()

    with ExitStack() as ph:
        obuf = [ph.enter_context(nc.sbuf_tensor(f"obf{i}", [P, S], F32))
                for i in range(2)]
        xrb = [ph.enter_context(nc.sbuf_tensor(f"xr{i}", [P, S], BF16))
               for i in range(2)]

        def get_y(c, reuse):
            return A[c][:], list(spill[("y", c)])

        abrep, t_ab, cons = ln_stats(ph, get_y)
        prev = [None, None]
        oprev = [None, None]
        for c in range(NC):
            okb = obuf[c % 2]
            w = [prev[c % 2]] if prev[c % 2] is not None else []
            t_x2 = sch.dma("sync", xrb[c % 2][:],
                           x2_d.ap()[c * P:(c + 1) * P, :], waits=w)
            sch.op("vector", (lambda c_, ab_: (lambda e: e.tensor_tensor(
                fscr[:], A[c_][:], ab_[:, 0:S], ALU.mult)))(c, abrep),
                waits=[t_ab])
            sch.op("vector", (lambda ab_: (lambda e: e.tensor_tensor(
                fscr[:], fscr[:], ab_[:, S:2 * S], ALU.add)))(abrep))
            w = [oprev[c % 2], t_x2] if oprev[c % 2] is not None else [t_x2]
            prev[c % 2] = sch.op(
                "vector", (lambda c_, o_: (lambda e: e.tensor_tensor(
                    o_[:], fscr[:], xrb[c_ % 2][:], ALU.add)))(c, okb),
                waits=w, token=True)
            oprev[c % 2] = sch.dma(
                "sync", out_d[c * P:(c + 1) * P, :], obuf[c % 2][:],
                waits=[prev[c % 2]])
        if dbg_d is not None:
            sch.dma("sync", dbg_d[:, 0:12], gsc[:, 0:12])
            sch.dma("sync", dbg_d[:, 12:16], lmax[:])
        total = sch.qn["sync"]
        fin = []
        for r in range(DMA_ROT):
            n_r = (total - r + DMA_ROT - 1) // DMA_ROT if total > r else 0
            if n_r:
                fin.append(Tok(f"q_sync{r}", 16 * n_r))
        sch.op("sync", None, waits=fin)

    emit(nc, sch)
    gstack.close()
    return nc


def emit(nc, sch):
    with ExitStack() as es:
        names = ["t_" + e for e in ENGINES] + ["B", "C"]
        for q in ("sync", "gpsimd", "scalar"):
            for r in range(DMA_ROT):
                names.append(f"q_{q}{r}")
        sems = {n: es.enter_context(nc.semaphore(n)) for n in names}
        with nc.Block() as block:
            def runner(eng_name):
                def f(eng):
                    for make, waits, tok in sch.items[eng_name]:
                        for t in waits:
                            eng.wait_ge(sems[t.sem], t.val)
                        if make is None:
                            continue
                        inst = make(eng)
                        if tok is not None:
                            if tok == "C":
                                inst.then_inc(sems["C"], 1)
                            elif tok.sem.startswith("q_"):
                                inst.then_inc(sems[tok.sem], 16)
                            elif eng_name in ("vector", "scalar") and \
                                    tok.sem != "B":
                                eng.drain().then_inc(sems[tok.sem], 1)
                            else:
                                inst.then_inc(sems[tok.sem], 1)

                return f
            block.sync(runner("sync"))
            block.gpsimd(runner("gpsimd"))
            block.vector(runner("vector"))
            block.scalar(runner("scalar"))
            block.tensor(runner("tensor"))


def kernel(L0_attn_x, L0_attn_Wq, L0_attn_Wk, L0_attn_Wv,
           L0_attn_rel_bias_table, L0_attn_ln_gamma, L0_attn_ln_beta,
           L0_ffn_w_gate, L0_ffn_w_up, L0_ffn_w_down):
    x = np.asarray(L0_attn_x)
    table = np.asarray(L0_attn_rel_bias_table)
    idx = 127 + np.arange(3968)[None, :] - np.arange(128)[:, None]
    ttarr = np.ascontiguousarray(np.transpose(table[idx, :], (2, 0, 1)))
    base = {
        "wqT": np.ascontiguousarray(np.asarray(L0_attn_Wq).T),
        "wkT": np.ascontiguousarray(np.asarray(L0_attn_Wk).T),
        "wvT": np.ascontiguousarray(np.asarray(L0_attn_Wv).T),
        "tt": ttarr,
        "gam": np.ascontiguousarray(
            np.asarray(L0_attn_ln_gamma).reshape(NC, P)),
        "bet": np.ascontiguousarray(
            np.asarray(L0_attn_ln_beta).reshape(NC, P)),
        "wg": np.ascontiguousarray(np.asarray(L0_ffn_w_gate)),
        "wu": np.ascontiguousarray(np.asarray(L0_ffn_w_up)),
        "wd": np.ascontiguousarray(np.asarray(L0_ffn_w_down)),
        "ident": np.eye(P, dtype=ml_dtypes.bfloat16),
    }
    in_maps = []
    for b in range(8):
        m = dict(base)
        m["xT"] = np.ascontiguousarray(x[b].T)
        in_maps.append(m)
    nc = build_program()
    res = run_bass_kernel_spmd(nc, in_maps, core_ids=list(range(8)))
    outs = [np.asarray(r["out"]).T for r in res.results]
    return np.ascontiguousarray(np.stack(outs, axis=0))


if __name__ == "__main__":
    print("building...")
    nc = build_program()
    print("built ok")
